# version 06
# baseline (speedup 1.0000x reference)
import math
from contextlib import ExitStack

import numpy as np
import ml_dtypes
import concourse.bass as bass
import concourse.mybir as mybir
from concourse.bass_utils import run_bass_kernel_spmd

F32 = mybir.dt.float32
BF16 = mybir.dt.bfloat16
AF = mybir.ActivationFunctionType
ALU = mybir.AluOpType
AX = mybir.AxisListType
BF = ml_dtypes.bfloat16

NCORES = 8
B, S, D = 4, 2048, 2048
T = B * S
TPC = T // NCORES
NH, EH = 16, 128
HPC = NH // 2
SCALE = 1.0 / math.sqrt(EH)
NE, IE, ISH = 8, 1024, 4096
DI, MNH, DS, DCONV = 4096, 32, 64, 4
CONV_DIM = DI + 2 * DS
DIP = 2 * DI + 2 * DS + MNH
MCH = 128
HF = 5632

_eye_bf = np.eye(128, dtype=BF)
_eye_f32 = np.eye(128, dtype=np.float32)
_tril_bf = np.tril(np.ones((128, 128), dtype=np.float32)).astype(BF)
_triu_bf = np.triu(np.ones((128, 128), dtype=np.float32)).astype(BF)
_ones_col_bf = np.ones((128, 1), dtype=BF)
_ones_row_bf = np.ones((1, 128), dtype=BF)
_ones_row_f32 = np.ones((1, 128), dtype=np.float32)


class Prog:
    ENGS = ("tensor", "vector", "scalar", "gpsimd", "sync")
    DMA_POOL = 40

    def __init__(self):
        self.nc = bass.Bass("TRN2", target_bir_lowering=False, debug=False)
        self.steps = []
        self.counts = {}
        self.ndma = {"s": 0, "g": 0}
        self.ncoll = 0
        self.ctx = ExitStack()
        self._names = {}
        self._track = {}

    def _nm(self, base):
        n = self._names.get(base, 0)
        self._names[base] = n + 1
        return f"{base}{n}" if n else base

    def sbuf(self, name, shape, dtype):
        return self.ctx.enter_context(self.nc.sbuf_tensor(self._nm("sb_" + name), shape, dtype))

    def psum(self, name, shape, dtype=F32):
        return self.ctx.enter_context(self.nc.psum_tensor(self._nm("ps_" + name), shape, dtype))

    def inp(self, name, shape, dtype=F32):
        return self.nc.dram_tensor(name, list(shape), dtype, kind="ExternalInput").ap()

    def outp(self, name, shape, dtype=F32):
        return self.nc.dram_tensor(name, list(shape), dtype, kind="ExternalOutput").ap()

    def const(self, name, arr):
        return self.nc.inline_tensor(np.ascontiguousarray(arr), self._nm(name)).ap()

    @staticmethod
    def _iv(ap):
        t = ap.tensor
        name = t.name
        dims = list(ap.ap)
        off = int(ap.offset)
        space = str(getattr(ap, "space", "")).upper()
        is_dram = "DRAM" in space or "HBM" in space
        if is_dram or len(dims) == 0:
            span = 1
            for st, ct in dims:
                span += abs(int(st)) * (int(ct) - 1)
            return (name, off, off + span, 0, 129)
        bp = ap.base_partition
        bp = int(bp() if callable(bp) else bp)
        pc = int(dims[0][1]) if dims else 128
        span = 1
        for st, ct in dims[1:]:
            span += abs(int(st)) * (int(ct) - 1)
        lo, hi = off, off + span
        if "PSUM" in space:
            esz = mybir.dt.size(ap.dtype)
            bank = 2048 // esz
            lo = (lo // bank) * bank
            hi = ((hi + bank - 1) // bank) * bank
            return (name, lo, hi, 0, 129)
        return (name, lo, hi, bp, bp + pc)

    def _dep_read(self, iv, deps):
        name, lo, hi, plo, phi = iv
        for ent in self._track.get(name, ()):
            if ent[0] == "w" and ent[1] < hi and lo < ent[2] and ent[3] < phi and plo < ent[4]:
                deps.add(ent[5])
        self._track.setdefault(name, []).append(("r", lo, hi, plo, phi, len(self.steps)))

    def _dep_write(self, iv, deps):
        name, lo, hi, plo, phi = iv
        lst = self._track.setdefault(name, [])
        keep = []
        for ent in lst:
            if ent[1] < hi and lo < ent[2] and ent[3] < phi and plo < ent[4]:
                deps.add(ent[5])
                if lo <= ent[1] and ent[2] <= hi and plo <= ent[3] and ent[4] <= phi:
                    continue
            keep.append(ent)
        keep.append(("w", lo, hi, plo, phi, len(self.steps)))
        self._track[name] = keep

    def _step(self, eng, fn, reads, writes, semkey, inc, extra_waits=None):
        deps = set()
        for ap in reads:
            if ap is not None and hasattr(ap, "tensor"):
                self._dep_read(self._iv(ap), deps)
        for ap in writes:
            if ap is not None and hasattr(ap, "tensor"):
                self._dep_write(self._iv(ap), deps)
        deps.discard(len(self.steps))
        waits = dict(extra_waits or {})
        for d in deps:
            deng, dkey, dval = self.steps[d][0], self.steps[d][2], self.steps[d][4]
            if deng == eng == "tensor" and dkey == ("e", "tensor"):
                continue
            waits[dkey] = max(waits.get(dkey, 0), dval)
        cur = self.counts.get(semkey, 0) + inc
        self.counts[semkey] = cur
        self.steps.append((eng, fn, semkey, inc, cur, waits))

    def step(self, eng, fn, reads=(), writes=()):
        self._step(eng, fn, reads, writes, ("e", eng), 1)

    def dma(self, out, in_, eng="sync", **kw):
        pool = "g" if eng == "gpsimd" else "s"
        idx = self.ndma[pool] % self.DMA_POOL
        self.ndma[pool] += 1
        key = ("d" + pool, idx)
        prevgen = self.counts.get(key, 0)
        extra = {key: prevgen} if prevgen else None
        self._step(eng, (lambda e, o=out, i=in_, k=kw: e.dma_start(out=o, in_=i, **k)),
                   [in_], [out], key, 16, extra)

    def mm(self, out, lhsT, rhs, start=True, stop=True):
        self.step("tensor", (lambda e, o=out, l=lhsT, r=rhs, st=start, sp=stop:
                             e.matmul(o, l, r, start=st, stop=sp)),
                  reads=[lhsT, rhs], writes=[out])

    def trans(self, out, in_, ident):
        self.step("tensor", (lambda e, o=out, i=in_, d=ident: e.transpose(o, i, d)),
                  reads=[in_, ident], writes=[out])

    def act(self, out, in_, func, bias=0.0, scale=1.0, accum_out=None):
        reads = [in_]
        if hasattr(bias, "tensor"):
            reads.append(bias)
        writes = [out]
        if accum_out is not None:
            writes.append(accum_out)
        self.step("scalar", (lambda e, o=out, i=in_, f=func, b=bias, s=scale, a=accum_out:
                             e.activation(out=o, in_=i, func=f, bias=b, scale=s, accum_out=a)),
                  reads=reads, writes=writes)

    _VOP = {
        "tensor_copy": (1, 1), "tensor_tensor": (1, 2), "tensor_add": (1, 2),
        "tensor_sub": (1, 2), "tensor_mul": (1, 2), "tensor_max": (1, 2),
        "tensor_scalar": (1, 3), "scalar_tensor_tensor": (1, 3),
        "tensor_tensor_scan": (1, 3), "reduce_sum": (1, 1), "reduce_max": (1, 1),
        "tensor_reduce": (1, 1), "reciprocal": (1, 1), "memset": (1, 0),
        "select": (1, 3), "mul": (1, 2), "add": (1, 2), "copy": (1, 1),
        "iota": (1, 0), "transpose": (1, 1),
    }

    def _vecop(self, eng, meth, *args, **kw):
        nw, nr = self._VOP[meth]
        writes = [a for a in args[0:nw] if hasattr(a, "tensor")]
        reads = [a for a in args[nw:nw + nr] if hasattr(a, "tensor")]
        if kw.get("accum_out") is not None:
            writes.append(kw["accum_out"])
        self.step(eng, (lambda e, m=meth, a=args, k=kw: getattr(e, m)(*a, **k)),
                  reads=reads, writes=writes)

    def v(self, meth, *args, **kw):
        self._vecop("vector", meth, *args, **kw)

    def g(self, meth, *args, **kw):
        self._vecop("gpsimd", meth, *args, **kw)

    def s(self, meth, *args, **kw):
        self._vecop("scalar", meth, *args, **kw)

    def coll(self, fn, reads=(), writes=()):
        key = ("c", self.ncoll)
        self.ncoll += 1
        self._step("gpsimd", fn, reads, writes, key, 1)

    def finish(self):
        semkeys = sorted(self.counts.keys(), key=str)
        sems = {k: self.ctx.enter_context(
            self.nc.semaphore(f"S_{k[0]}_{k[1]}")) for k in semkeys}
        block = self.ctx.enter_context(self.nc.Block())
        per = {}
        for (eng, fn, semkey, inc, val, waits) in self.steps:
            per.setdefault(eng, []).append((fn, semkey, inc, waits))
        for eng, lst in per.items():
            sec = getattr(block, eng)

            def make_body(lst=lst):
                def body(e):
                    prev = {}
                    for fn, semkey, inc, waits in lst:
                        for wkey in sorted(waits.keys(), key=str):
                            val = waits[wkey]
                            if val > prev.get(wkey, 0):
                                e.wait_ge(sems[wkey], val)
                                prev[wkey] = val
                        if semkey[0] == "c":
                            fn(e).then_inc(sems[semkey])
                        else:
                            fn(e).then_inc(sems[semkey], inc)
                return body
            sec(make_body())

        def final_wait(e):
            for k in semkeys:
                e.wait_ge(sems[k], self.counts[k])
        block.sync(final_wait)
        self.ctx.close()
        return self.nc


def _run(nc, in_maps):
    res = run_bass_kernel_spmd(nc, in_maps, core_ids=list(range(NCORES)))
    return res.results


def build_ln(rows=TPC, d=D):
    p = Prog()
    h = p.inp("h", (rows, d))
    xln = p.outp("xln", (rows, d), BF16)
    x_sb = p.sbuf("x", [128, d], F32)
    sq_sb = p.sbuf("sq", [128, d], F32)
    o_sb = p.sbuf("o", [128, d], BF16)
    rs = p.sbuf("rs", [128, 1], F32)
    mean = p.sbuf("mean", [128, 1], F32)
    ssq = p.sbuf("ssq", [128, 1], F32)
    ex2 = p.sbuf("ex2", [128, 1], F32)
    m2 = p.sbuf("m2", [128, 1], F32)
    var = p.sbuf("var", [128, 1], F32)
    std = p.sbuf("std", [128, 1], F32)
    rstd = p.sbuf("rstd", [128, 1], F32)
    for i in range(rows // 128):
        sl = slice(i * 128, (i + 1) * 128)
        p.dma(x_sb[:], h[sl, :])
        p.v("reduce_sum", rs[:], x_sb[:], AX.X)
        p.s("mul", mean[:], rs[:], 1.0 / d)
        p.act(sq_sb[:], x_sb[:], AF.Square, accum_out=ssq[:])
        p.s("mul", ex2[:], ssq[:], 1.0 / d)
        p.v("tensor_mul", m2[:], mean[:], mean[:])
        p.v("tensor_sub", var[:], ex2[:], m2[:])
        p.v("tensor_scalar", var[:], var[:], 1e-5, None, ALU.add)
        p.act(std[:], var[:], AF.Sqrt)
        p.v("reciprocal", rstd[:], std[:])
        p.v("tensor_scalar", o_sb[:], x_sb[:], mean[:], rstd[:], ALU.subtract, ALU.mult)
        p.dma(xln[sl, :], o_sb[:])
    return p.finish()


def run_ln(h_full):
    nc = build_ln()
    in_maps = [{"h": np.ascontiguousarray(h_full[c * TPC:(c + 1) * TPC])}
               for c in range(NCORES)]
    res = _run(nc, in_maps)
    return np.concatenate([r["xln"] for r in res], axis=0)


def build_qkv():
    p = Prog()
    xln = p.inp("xln", (S, D), BF16)
    wq = p.inp("wq", (HPC, 128, D // 128, EH))
    wk = p.inp("wk", (HPC, 128, D // 128, EH))
    wv = p.inp("wv", (HPC, 128, D // 128, EH))
    q_o = p.outp("q", (HPC, EH, S), BF16)
    k_o = p.outp("k", (HPC, EH, S), BF16)
    v_o = p.outp("v", (HPC, S, EH), BF16)

    nDC = D // 128
    xt = p.sbuf("xt", [128, nDC, S], BF16)
    w_sb = {n: p.sbuf(f"w_{n}", [128, nDC, EH], BF16) for n in "qkv"}
    qk_sb = p.sbuf("qk", [128, S], BF16)
    v_sb = p.sbuf("v", [128, S // 128, EH], BF16)
    ps = p.psum("ps", [128, 512], F32)
    ps_v = p.psum("psv", [128, EH], F32)

    for dc in range(nDC):
        p.dma(xt[:, dc, :], xln[:, dc * 128:(dc + 1) * 128], transpose=True)
    for j in range(HPC):
        for n, w in (("q", wq), ("k", wk), ("v", wv)):
            p.dma(w_sb[n][:], w[j], eng="gpsimd")
        for n, o in (("q", q_o), ("k", k_o)):
            for sg in range(S // 512):
                for dc in range(nDC):
                    p.mm(ps[:], w_sb[n][:, dc, :], xt[:, dc, sg * 512:(sg + 1) * 512],
                         start=(dc == 0), stop=(dc == nDC - 1))
                p.v("tensor_copy", qk_sb[:, sg * 512:(sg + 1) * 512], ps[:])
            p.dma(o[j], qk_sb[:])
        for st in range(S // 128):
            for dc in range(nDC):
                p.mm(ps_v[:], xt[:, dc, st * 128:(st + 1) * 128], w_sb["v"][:, dc, :],
                     start=(dc == 0), stop=(dc == nDC - 1))
            p.v("tensor_copy", v_sb[:, st, :], ps_v[:])
        p.dma(v_o[j].rearrange("(t p) e -> p t e", p=128), v_sb[:])
    return p.finish()


def run_qkv(xln, wq_t, wk_t, wv_t):
    nc = build_qkv()
    in_maps = []
    for c in range(NCORES):
        b, hh = c // 2, c % 2
        hs = slice(hh * HPC, (hh + 1) * HPC)
        def lay(w):
            return np.ascontiguousarray(
                w[hs].reshape(HPC, D // 128, 128, EH).transpose(0, 2, 1, 3))
        in_maps.append({
            "xln": np.ascontiguousarray(xln[b]),
            "wq": lay(wq_t), "wk": lay(wk_t), "wv": lay(wv_t),
        })
    res = _run(nc, in_maps)
    q = np.stack([np.concatenate([res[2 * b]["q"], res[2 * b + 1]["q"]]) for b in range(B)])
    k = np.stack([np.concatenate([res[2 * b]["k"], res[2 * b + 1]["k"]]) for b in range(B)])
    v = np.stack([np.concatenate([res[2 * b]["v"], res[2 * b + 1]["v"]]) for b in range(B)])
    return q, k, v


def build_attnz():
    p = Prog()
    q_i = p.inp("q", (HPC, EH, S), BF16)
    k_i = p.inp("k", (HPC, EH, S), BF16)
    v_i = p.inp("v", (HPC, 128, S // 128, EH), BF16)
    z_o = p.outp("z", (HPC, EH, S), BF16)

    ident_bf = p.const("ident_bf", _eye_bf)
    id11_f32 = p.const("id11_f32", np.ones((1, 1), dtype=np.float32))
    tril = p.const("tril", _tril_bf)
    ones_col = p.const("ones_col", _ones_col_bf)

    nST = S // 128
    q_sb = p.sbuf("q", [128, S], BF16)
    k_sb = p.sbuf("k", [128, S], BF16)
    v_sb = p.sbuf("v", [128, nST, EH], BF16)
    sq_sb = p.sbuf("sqs", [128, S], BF16)
    id_bf_sb = p.sbuf("idbf", [128, 128], BF16)
    id11_sb = p.sbuf("id11", [1, 1], F32)
    tril_sb = p.sbuf("tril", [128, 128], BF16)
    ones_sb = p.sbuf("ones", [128, 1], BF16)
    r2_sb = p.sbuf("r2", [1, S], F32)
    k2_sb = p.sbuf("k2", [1, S], F32)
    km = p.sbuf("km", [1, 1], F32)
    negb = p.sbuf("negb", [128, nST], F32)
    P_sb = p.sbuf("P", [128, S], BF16)
    Pd_sb = p.sbuf("Pd", [128, 128], BF16)
    pt_sb = p.sbuf("pt", [128, 4, 128], BF16)
    sums = p.sbuf("sums", [128, 4], F32)
    sumtot = p.sbuf("sumtot", [128, 1], F32)
    rsum = p.sbuf("rsum", [128, 1], F32)
    diag_sb = p.sbuf("diag", [128, 128], BF16)
    z_sb = p.sbuf("z", [128, S], BF16)

    ps_s = p.psum("ps_s", [128, S], F32)
    ps_p = p.psum("ps_p", [128, 512], F32)
    ps_z = p.psum("ps_z", [128, 128], F32)
    ps_row = p.psum("ps_row", [1, 512], F32)
    ps_t = p.psum("ps_t", [128, 1], F32)

    p.dma(id_bf_sb[:], ident_bf[:])
    p.dma(id11_sb[:], id11_f32[:])
    p.dma(tril_sb[:], tril[:])
    p.dma(ones_sb[:], ones_col[:])

    for j in range(HPC):
        p.dma(q_sb[:], q_i[j])
        p.dma(k_sb[:], k_i[j])
        p.dma(v_sb[:], v_i[j])
        for (src, dst) in ((q_sb, r2_sb), (k_sb, k2_sb)):
            p.act(sq_sb[:], src[:], AF.Square)
            for sg in range(S // 512):
                p.mm(ps_row[:], ones_sb[:], sq_sb[:, sg * 512:(sg + 1) * 512])
                p.v("tensor_copy", dst[:, sg * 512:(sg + 1) * 512], ps_row[:])
        p.v("reduce_max", km[:], k2_sb[:], AX.X)
        p.v("tensor_scalar", r2_sb[:], r2_sb[:], km[:], None, ALU.mult)
        p.act(r2_sb[:], r2_sb[:], AF.Sqrt, scale=1.0 / EH)
        p.s("mul", r2_sb[:], r2_sb[:], -1.0)
        for st in range(nST):
            p.trans(ps_t[:], r2_sb[:, st * 128:(st + 1) * 128], id11_sb[:])
            p.v("tensor_copy", negb[:, st:st + 1], ps_t[:])
        for i in range(nST):
            nm = (i + 1) * 128
            ng = (nm + 511) // 512
            lastw = nm - (ng - 1) * 512
            for g in range(ng):
                w = min(512, nm - g * 512)
                p.mm(ps_s[:, g * 512:g * 512 + w], q_sb[:, i * 128:(i + 1) * 128],
                     k_sb[:, g * 512:g * 512 + w])
            for g in range(ng - 1):
                p.act(P_sb[:, g * 512:(g + 1) * 512], ps_s[:, g * 512:(g + 1) * 512],
                      AF.Exp, bias=negb[:, i:i + 1], scale=SCALE,
                      accum_out=sums[:, g:g + 1])
            off = (ng - 1) * 512
            p.act(P_sb[:, off:off + lastw], ps_s[:, off:off + lastw],
                  AF.Exp, bias=negb[:, i:i + 1], scale=SCALE)
            p.g("tensor_tensor", Pd_sb[:], P_sb[:, nm - 128:nm], tril_sb[:], ALU.mult)
            p.v("tensor_copy", P_sb[:, nm - 128:nm], Pd_sb[:])
            p.v("reduce_sum", sums[:, ng - 1:ng], P_sb[:, off:off + lastw], AX.X)
            p.v("reduce_sum", sumtot[:], sums[:, 0:ng], AX.X)
            p.v("reciprocal", rsum[:], sumtot[:])
            p.v("tensor_scalar", diag_sb[:], id_bf_sb[:], rsum[:], None, ALU.mult)
            for j4 in range((i + 1 + 3) // 4):
                cnt = min(4, i + 1 - j4 * 4)
                for u in range(cnt):
                    m = j4 * 4 + u
                    p.mm(ps_p[:, u * 128:(u + 1) * 128],
                         P_sb[:, m * 128:(m + 1) * 128], diag_sb[:])
                p.v("tensor_copy", pt_sb[:, 0:cnt, :], ps_p[:, 0:cnt * 128])
                for u in range(cnt):
                    m = j4 * 4 + u
                    p.mm(ps_z[:], v_sb[:, m, :], pt_sb[:, u, :],
                         start=(m == 0), stop=(m == i))
            p.v("tensor_copy", z_sb[:, i * 128:(i + 1) * 128], ps_z[:])
        p.dma(z_o[j], z_sb[:])
    return p.finish()


def run_attnz(q, k, v):
    nc = build_attnz()
    in_maps = []
    for c in range(NCORES):
        b, hh = c // 2, c % 2
        hs = slice(hh * HPC, (hh + 1) * HPC)
        in_maps.append({
            "q": np.ascontiguousarray(q[b, hs]),
            "k": np.ascontiguousarray(k[b, hs]),
            "v": np.ascontiguousarray(
                v[b, hs].reshape(HPC, S // 128, 128, EH).transpose(0, 2, 1, 3)),
        })
    res = _run(nc, in_maps)
    return np.stack([np.concatenate([res[2 * b]["z"], res[2 * b + 1]["z"]]) for b in range(B)])


def build_attnout():
    p = Prog()
    z_i = p.inp("z", (EH, NH, TPC), BF16)
    wo = p.inp("wo", (EH, NH, D))
    x_i = p.inp("x", (TPC, D))
    h1_o = p.outp("h1", (TPC, D))

    z_sb = p.sbuf("z", [128, NH, TPC], BF16)
    wo_sb = p.sbuf("wo", [128, NH, D], BF16)
    x_sb = p.sbuf("x", [128, D], F32)
    o_sb = p.sbuf("o", [128, D], F32)
    ps = p.psum("ps", [128, 512], F32)

    p.dma(z_sb[:], z_i[:])
    p.dma(wo_sb[:], wo[:], eng="gpsimd")
    for st in range(TPC // 128):
        p.dma(x_sb[:], x_i[st * 128:(st + 1) * 128, :])
        for dg in range(D // 512):
            for h in range(NH):
                p.mm(ps[:], z_sb[:, h, st * 128:(st + 1) * 128],
                     wo_sb[:, h, dg * 512:(dg + 1) * 512],
                     start=(h == 0), stop=(h == NH - 1))
            p.v("tensor_add", o_sb[:, dg * 512:(dg + 1) * 512], ps[:],
                x_sb[:, dg * 512:(dg + 1) * 512])
        p.dma(h1_o[st * 128:(st + 1) * 128, :], o_sb[:])
    return p.finish()


def run_attnout(z, wo_t, x_full):
    nc = build_attnout()
    wo_l = np.ascontiguousarray(wo_t.transpose(1, 0, 2))
    in_maps = []
    for c in range(NCORES):
        r0 = c * TPC
        b, s0 = r0 // S, r0 % S
        in_maps.append({
            "z": np.ascontiguousarray(z[b, :, :, s0:s0 + TPC].transpose(1, 0, 2)),
            "wo": wo_l,
            "x": np.ascontiguousarray(x_full[r0:r0 + TPC]),
        })
    res = _run(nc, in_maps)
    return np.concatenate([r["h1"] for r in res], axis=0)


def prep_attn_weights(W_Q, W_K, W_V, W_O):
    wq_t = np.ascontiguousarray(W_Q.transpose(0, 2, 1))
    wk_t = np.ascontiguousarray(W_K.transpose(0, 2, 1))
    wv_t = np.ascontiguousarray(W_V.transpose(0, 2, 1))
    wo_t = np.ascontiguousarray(W_O.transpose(0, 2, 1))
    return wq_t, wk_t, wv_t, wo_t




def build_moe():
    p = Prog()
    xln = p.inp("xln", (TPC, D), BF16)
    h1_i = p.inp("h1", (TPC, D))
    gw = p.inp("gw", (128, D // 128, NE))
    sge = p.inp("sge", (128, D // 128))
    eg = p.inp("eg", (NE, 128, D // 128, IE))
    eu = p.inp("eu", (NE, 128, D // 128, IE))
    ed = p.inp("ed", (NE, 128, IE // 128, D))
    sg = p.inp("sg", (ISH // 128, 128, D // 128, 128))
    su = p.inp("su", (ISH // 128, 128, D // 128, 128))
    sd = p.inp("sd", (4, 128, 8, D))
    h2_o = p.outp("h2", (TPC, D))

    ident_bf = p.const("ident_bf", _eye_bf)
    ones_row = p.const("ones_row", _ones_row_bf)

    nDC = D // 128
    nST = TPC // 128
    xt = p.sbuf("xt", [128, nDC, TPC], BF16)
    id_bf_sb = p.sbuf("idbf", [128, 128], BF16)
    ones_sb = p.sbuf("ones", [1, 128], BF16)
    gw_sb = p.sbuf("gw", [128, nDC, NE], BF16)
    sge_sb = p.sbuf("sge", [128, nDC], BF16)
    probs = p.sbuf("probs", [128, NE], F32)
    m1 = p.sbuf("m1", [128, 1], F32)
    sumr = p.sbuf("sumr", [128, 1], F32)
    t8 = p.sbuf("t8", [128, NE], F32)
    mask1 = p.sbuf("mask1", [128, NE], F32)
    pr2 = p.sbuf("pr2", [128, NE], F32)
    m2 = p.sbuf("m2", [128, 1], F32)
    cmask = p.sbuf("cmask", [128, NE], F32)
    combbf = p.sbuf("combbf", [128, NE], BF16)
    sigbf = p.sbuf("sigbf", [128, 1], BF16)
    combT = p.sbuf("combT", [1, NE, TPC], BF16)
    sigrow = p.sbuf("sigrow", [1, TPC], BF16)
    comb_bc = p.sbuf("comb_bc", [128, NE, TPC], BF16)
    sig_bc = p.sbuf("sig_bc", [128, TPC], BF16)
    wbuf = p.sbuf("wbuf", [128, nDC, IE], BF16)
    sgw = p.sbuf("sgw", [128, nDC, 128], BF16)
    suw = p.sbuf("suw", [128, nDC, 128], BF16)
    hg = p.sbuf("hg", [128, IE // 128, 512], BF16)
    h_e = p.sbuf("h_e", [128, IE // 128, 512], BF16)
    hgs = p.sbuf("hgs", [128, 512], BF16)
    h_sh = p.sbuf("h_sh", [128, ISH // 128, 512], BF16)
    macc = p.sbuf("macc", [128, 4, D], F32)
    sdw = wbuf[:].rearrange("p a b -> p (a b)").rearrange("p (c d) -> p c d", c=8)
    edbuf = h_sh[:].rearrange("p a b -> p (a b)").rearrange("p (c d) -> p c d", c=8)

    ps_g = p.psum("g", [128, 512], F32)
    ps_u = p.psum("u", [128, 512], F32)
    ps_d = p.psum("d", [128, 512], F32)
    ps_r = p.psum("r", [128, NE + 1], F32)
    ps_misc = p.psum("misc", [128, 512], F32)

    p.dma(id_bf_sb[:], ident_bf[:])
    p.dma(ones_sb[:], ones_row[:])
    for dc in range(nDC):
        p.dma(xt[:, dc, :], xln[:, dc * 128:(dc + 1) * 128], transpose=True)
    p.dma(gw_sb[:], gw[:], eng="gpsimd")
    p.dma(sge_sb[:], sge[:], eng="gpsimd")

    for st in range(nST):
        ssl = slice(st * 128, (st + 1) * 128)
        for dc in range(nDC):
            p.mm(ps_r[:, 0:NE], xt[:, dc, ssl], gw_sb[:, dc, :],
                 start=(dc == 0), stop=(dc == nDC - 1))
        p.v("reduce_max", m1[:], ps_r[:, 0:NE], AX.X)
        p.s("mul", m1[:], m1[:], -1.0)
        p.act(probs[:], ps_r[:, 0:NE], AF.Exp, bias=m1[:], accum_out=sumr[:])
        p.v("reciprocal", sumr[:], sumr[:])
        p.v("tensor_scalar", probs[:], probs[:], sumr[:], None, ALU.mult)
        p.v("reduce_max", m1[:], probs[:], AX.X)
        p.v("tensor_scalar", mask1[:], probs[:], m1[:], None, ALU.is_ge)
        p.v("tensor_mul", t8[:], probs[:], mask1[:])
        p.v("tensor_sub", pr2[:], probs[:], t8[:])
        p.v("reduce_max", m2[:], pr2[:], AX.X)
        p.v("tensor_scalar", cmask[:], probs[:], m2[:], None, ALU.is_ge)
        p.v("tensor_mul", t8[:], probs[:], cmask[:])
        p.v("tensor_copy", combbf[:], t8[:])
        for e in range(NE):
            p.mm(ps_misc[0:1, 0:128], combbf[:, e:e + 1], id_bf_sb[:])
            p.v("tensor_copy", combT[0:1, e, ssl], ps_misc[0:1, 0:128])
        for dc in range(nDC):
            p.mm(ps_r[:, NE:NE + 1], xt[:, dc, ssl], sge_sb[:, dc:dc + 1],
                 start=(dc == 0), stop=(dc == nDC - 1))
        p.act(sigbf[:], ps_r[:, NE:NE + 1], AF.Sigmoid)
        p.mm(ps_misc[0:1, 0:128], sigbf[:], id_bf_sb[:])
        p.v("tensor_copy", sigrow[:, ssl], ps_misc[0:1, 0:128])

    for e in range(NE):
        for g2 in range(TPC // 512):
            gsl = slice(g2 * 512, (g2 + 1) * 512)
            p.mm(ps_misc[:, 0:512], ones_sb[:], combT[0:1, e, gsl])
            p.v("tensor_copy", comb_bc[:, e, gsl], ps_misc[:, 0:512])
    for g2 in range(TPC // 512):
        gsl = slice(g2 * 512, (g2 + 1) * 512)
        p.mm(ps_misc[:, 0:512], ones_sb[:], sigrow[:, gsl])
        p.v("tensor_copy", sig_bc[:, gsl], ps_misc[:, 0:512])

    for g2 in range(TPC // 512):
        gsl = slice(g2 * 512, (g2 + 1) * 512)
        for st in range(4):
            row0 = g2 * 512 + st * 128
            p.dma(macc[:, st, :], h1_i[row0:row0 + 128, :])
        for jt in range(ISH // 128):
            jsl = slice(jt * 128, (jt + 1) * 128)
            p.dma(sgw[:], sg[jt], eng="gpsimd")
            p.dma(suw[:], su[jt], eng="gpsimd")
            for dc in range(nDC):
                p.mm(ps_g[:], sgw[:, dc, :], xt[:, dc, gsl],
                     start=(dc == 0), stop=(dc == nDC - 1))
            p.act(hgs[:], ps_g[:], AF.Silu)
            for dc in range(nDC):
                p.mm(ps_u[:], suw[:, dc, :], xt[:, dc, gsl],
                     start=(dc == 0), stop=(dc == nDC - 1))
            p.v("tensor_tensor", h_sh[:, jt, :], ps_u[:], hgs[:], ALU.mult)
            p.g("tensor_tensor", h_sh[:, jt, :], h_sh[:, jt, :], sig_bc[:, gsl], ALU.mult)
        for jtg in range(4):
            p.dma(sdw[:], sd[jtg], eng="gpsimd")
            for st in range(4):
                for dg in range(D // 512):
                    dsl = slice(dg * 512, (dg + 1) * 512)
                    for jc in range(8):
                        p.mm(ps_d[:], h_sh[:, jtg * 8 + jc, st * 128:(st + 1) * 128],
                             sdw[:, jc, dsl], start=(jc == 0), stop=(jc == 7))
                    p.v("tensor_tensor", macc[:, st, dsl], ps_d[:], macc[:, st, dsl], ALU.add)
        for e in range(NE):
            p.dma(wbuf[:], eg[e], eng="gpsimd")
            for it in range(IE // 128):
                for dc in range(nDC):
                    p.mm(ps_g[:], wbuf[:, dc, it * 128:(it + 1) * 128], xt[:, dc, gsl],
                         start=(dc == 0), stop=(dc == nDC - 1))
                p.act(hg[:, it, :], ps_g[:], AF.Silu)
            p.dma(wbuf[:], eu[e], eng="gpsimd")
            for it in range(IE // 128):
                for dc in range(nDC):
                    p.mm(ps_u[:], wbuf[:, dc, it * 128:(it + 1) * 128], xt[:, dc, gsl],
                         start=(dc == 0), stop=(dc == nDC - 1))
                p.v("tensor_tensor", h_e[:, it, :], ps_u[:], hg[:, it, :], ALU.mult)
                p.g("tensor_tensor", h_e[:, it, :], h_e[:, it, :], comb_bc[:, e, gsl], ALU.mult)
            p.dma(edbuf[:], ed[e], eng="gpsimd")
            for st in range(4):
                for dg in range(D // 512):
                    dsl = slice(dg * 512, (dg + 1) * 512)
                    for ic in range(IE // 128):
                        p.mm(ps_d[:], h_e[:, ic, st * 128:(st + 1) * 128],
                             edbuf[:, ic, dsl], start=(ic == 0), stop=(ic == IE // 128 - 1))
                    p.v("tensor_tensor", macc[:, st, dsl], ps_d[:], macc[:, st, dsl], ALU.add)
        for st in range(4):
            row0 = g2 * 512 + st * 128
            p.dma(h2_o[row0:row0 + 128, :], macc[:, st, :])
    return p.finish()


def run_moe(xln, h1, gw_t, sge_t, eg_t, eu_t, ed_t, sg_t, su_t, sd_t):
    nc = build_moe()
    in_maps = []
    for c in range(NCORES):
        r0 = c * TPC
        in_maps.append({
            "xln": np.ascontiguousarray(xln[r0:r0 + TPC]),
            "h1": np.ascontiguousarray(h1[r0:r0 + TPC]),
            "gw": gw_t, "sge": sge_t, "eg": eg_t, "eu": eu_t, "ed": ed_t,
            "sg": sg_t, "su": su_t, "sd": sd_t,
        })
    res = _run(nc, in_maps)
    return np.concatenate([r["h2"] for r in res], axis=0)


def prep_moe_weights(gate_w, expert_gate_w, expert_up_w, expert_down_w,
                     shared_gate_w, shared_up_w, shared_down_w, shared_expert_gate_w):
    nDC = D // 128
    gw_t = np.ascontiguousarray(gate_w.T.reshape(nDC, 128, NE).transpose(1, 0, 2))
    sge_t = np.ascontiguousarray(shared_expert_gate_w.reshape(nDC, 128).T)
    def lay_du(w):
        return np.ascontiguousarray(
            w.transpose(0, 2, 1).reshape(NE, nDC, 128, w.shape[1]).transpose(0, 2, 1, 3))
    eg_t = lay_du(expert_gate_w)
    eu_t = lay_du(expert_up_w)
    ed_t = np.ascontiguousarray(
        expert_down_w.transpose(0, 2, 1).reshape(NE, IE // 128, 128, D).transpose(0, 2, 1, 3))
    def lay_sh(w):
        return np.ascontiguousarray(
            w.T.reshape(nDC, 128, ISH // 128, 128).transpose(2, 1, 0, 3))
    sg_t = lay_sh(shared_gate_w)
    su_t = lay_sh(shared_up_w)
    sd_t = np.ascontiguousarray(
        shared_down_w.T.reshape(4, 8, 128, D).transpose(0, 2, 1, 3))
    return gw_t, sge_t, eg_t, eu_t, ed_t, sg_t, su_t, sd_t




def build_mambain():
    p = Prog()
    SEXT = TPC + 128
    nFT = 66
    xln = p.inp("xln", (SEXT, D), BF16)
    win = p.inp("win", (nFT, 128, D // 128, 128))
    cw = p.inp("cw", (128, CONV_DIM // 128, DCONV))
    cb = p.inp("cb", (128, CONV_DIM // 128))
    dtb = p.inp("dtb", (32, 1))
    sz_o = p.outp("sz", (DI, TPC), BF16)
    xbcc_o = p.outp("xbcc", (CONV_DIM, TPC), BF16)
    dt_o = p.outp("dt", (32, TPC))

    nDC = D // 128
    xt = p.sbuf("xt", [128, nDC, SEXT], BF16)
    wbuf = p.sbuf("wbuf", [128, nDC, 128], BF16)
    cw_sb = p.sbuf("cw", [128, CONV_DIM // 128, DCONV], F32)
    cb_sb = p.sbuf("cb", [128, CONV_DIM // 128], F32)
    dtb_sb = p.sbuf("dtb", [32, 1], F32)
    ztile = p.sbuf("ztile", [128, TPC], BF16)
    xbc_t = p.sbuf("xbc_t", [128, SEXT], BF16)
    cv = p.sbuf("cv", [128, TPC], F32)
    cvb = p.sbuf("cvb", [128, TPC], BF16)
    dtraw = p.sbuf("dtraw", [32, TPC], F32)
    dtt = p.sbuf("dtt", [32, TPC], F32)
    dta = p.sbuf("dta", [32, TPC], F32)
    ps = p.psum("i", [128, 512], F32)

    p.dma(cw_sb[:], cw[:], eng="gpsimd")
    p.dma(cb_sb[:], cb[:], eng="gpsimd")
    p.dma(dtb_sb[:], dtb[:])
    for dc in range(nDC):
        p.dma(xt[:, dc, :], xln[:, dc * 128:(dc + 1) * 128], transpose=True)

    sgroups = [(0, 512), (512, 512), (1024, 128)]
    for ft in range(nFT):
        p.dma(wbuf[:], win[ft], eng="gpsimd")
        if ft < 32:
            for (o, w) in sgroups[:2]:
                for dc in range(nDC):
                    p.mm(ps[:, 0:w], wbuf[:, dc, :], xt[:, dc, 128 + o:128 + o + w],
                         start=(dc == 0), stop=(dc == nDC - 1))
                p.act(ztile[:, o:o + w], ps[:, 0:w], AF.Silu)
            p.dma(sz_o[ft * 128:(ft + 1) * 128, :], ztile[:])
        elif ft < 65:
            ct = ft - 32
            for (o, w) in sgroups:
                for dc in range(nDC):
                    p.mm(ps[:, 0:w], wbuf[:, dc, :], xt[:, dc, o:o + w],
                         start=(dc == 0), stop=(dc == nDC - 1))
                p.v("tensor_copy", xbc_t[:, o:o + w], ps[:, 0:w])
            p.v("tensor_scalar", cv[:], xbc_t[:, 125:125 + TPC],
                cw_sb[:, ct, 0:1], cb_sb[:, ct:ct + 1], ALU.mult, ALU.add)
            for k2 in range(1, DCONV):
                p.v("scalar_tensor_tensor", cv[:], xbc_t[:, 125 + k2:125 + k2 + TPC],
                    cw_sb[:, ct, k2:k2 + 1], cv[:], ALU.mult, ALU.add)
            p.act(cvb[:], cv[:], AF.Silu)
            p.dma(xbcc_o[ct * 128:(ct + 1) * 128, :], cvb[:])
        else:
            for (o, w) in sgroups[:2]:
                for dc in range(nDC):
                    p.mm(ps[:, 0:w], wbuf[:, dc, :], xt[:, dc, 128 + o:128 + o + w],
                         start=(dc == 0), stop=(dc == nDC - 1))
                p.v("tensor_copy", dtraw[:, o:o + w], ps[0:32, 0:w])
            p.v("tensor_scalar", dtt[:], dtraw[:], dtb_sb[:], None, ALU.add)
            p.act(dta[:], dtt[:], AF.Abs)
            p.act(dta[:], dta[:], AF.Exp, scale=-1.0)
            p.v("tensor_scalar", dta[:], dta[:], 1.0, None, ALU.add)
            p.act(dta[:], dta[:], AF.Ln)
            p.v("tensor_scalar", dtt[:], dtt[:], 0.0, None, ALU.max)
            p.v("tensor_add", dtt[:], dtt[:], dta[:])
            p.dma(dt_o[:], dtt[:])
    return p.finish()


def run_mambain(xln, conv_w, conv_b, dt_bias, W_in):
    nc = build_mambain()
    nDC = D // 128
    win_pad = np.zeros((66 * 128, D), dtype=np.float32)
    win_pad[:DIP] = W_in
    win_l = np.ascontiguousarray(
        win_pad.reshape(66, 128, nDC, 128).transpose(0, 3, 2, 1))
    cw_l = np.ascontiguousarray(conv_w.reshape(CONV_DIM // 128, 128, DCONV).transpose(1, 0, 2))
    cb_l = np.ascontiguousarray(conv_b.reshape(CONV_DIM // 128, 128).T)
    dtb_l = np.ascontiguousarray(dt_bias.reshape(32, 1))
    xln_b = xln.reshape(B, S, D)
    in_maps = []
    for c in range(NCORES):
        r0 = c * TPC
        b, s0 = r0 // S, r0 % S
        if s0 == 0:
            ext = np.zeros((128, D), dtype=BF)
        else:
            ext = xln_b[b, s0 - 128:s0]
        in_maps.append({
            "xln": np.ascontiguousarray(np.concatenate([ext, xln_b[b, s0:s0 + TPC]], axis=0)),
            "win": win_l, "cw": cw_l, "cb": cb_l, "dtb": dtb_l,
        })
    res = _run(nc, in_maps)
    sz = np.stack([np.concatenate([res[2 * b]["sz"], res[2 * b + 1]["sz"]], axis=1)
                   for b in range(B)])
    xbcc = np.stack([np.concatenate([res[2 * b]["xbcc"], res[2 * b + 1]["xbcc"]], axis=1)
                     for b in range(B)])
    dt = np.stack([np.concatenate([res[2 * b]["dt"], res[2 * b + 1]["dt"]], axis=1)
                   for b in range(B)])
    return sz, xbcc, dt


def build_ssd():
    p = Prog()
    nHD = 16
    nC = S // MCH
    xf_i = p.inp("xf", (nHD * 128, S), BF16)
    b_i = p.inp("bi", (DS, S), BF16)
    c_i = p.inp("ci", (DS, S), BF16)
    dt_i = p.inp("dt", (nHD, S))
    alog_i = p.inp("alog", (nHD, 1))
    dvec_i = p.inp("dvec", (1, nHD))
    y_o = p.outp("y", (nHD * 128, S), BF16)

    id16 = p.const("id16", np.eye(16, dtype=np.float32))
    id64_bf = p.const("id64bf", np.eye(64, dtype=BF))
    id128_bf = p.const("id128bf", _eye_bf)
    id128_f = p.const("id128f", _eye_f32)
    triu_c = p.const("triu", _triu_bf)
    ones1x64 = p.const("ones1x64", np.ones((1, 64), dtype=np.float32))
    ones1x128f = p.const("ones1x128f", _ones_row_f32)

    xf = p.sbuf("xf", [128, nHD, S], BF16)
    bsb = p.sbuf("bsb", [DS, S], BF16)
    csb = p.sbuf("csb", [DS, S], BF16)
    y_all = p.sbuf("yall", [128, nHD, S], BF16)
    dt_sb = p.sbuf("dt", [nHD, S], F32)
    alog = p.sbuf("alog", [nHD, 1], F32)
    acol = p.sbuf("acol", [nHD, 1], F32)
    dvec = p.sbuf("dvec", [1, nHD], F32)
    dcols = p.sbuf("dcols", [128, nHD], F32)
    ones_s = p.sbuf("ones_s", [nHD, S], F32)
    dA = p.sbuf("dA", [nHD, S], F32)
    cum = p.sbuf("cum", [nHD, S], F32)
    negcum = p.sbuf("negcum", [nHD, S], F32)
    ecr = p.sbuf("ecr", [nHD, S], F32)
    dec = p.sbuf("dec", [nHD, S], F32)
    cdec = p.sbuf("cdec", [nHD, nC], F32)
    id16_sb = p.sbuf("id16", [16, 16], F32)
    id64b_sb = p.sbuf("id64b", [64, 64], BF16)
    id128b_sb = p.sbuf("id128b", [128, 128], BF16)
    id128f_sb = p.sbuf("id128f", [128, 128], F32)
    triu_sb = p.sbuf("triu", [128, 128], BF16)
    on64_sb = p.sbuf("on64", [1, 64], F32)
    on128_sb = p.sbuf("on128", [1, 128], F32)
    ones128t = p.sbuf("ones128t", [128, 128], F32)
    cumT = p.sbuf("cumT", [128, nHD], F32)
    negcumT = p.sbuf("negcumT", [128, nHD], F32)
    dtT = p.sbuf("dtT", [128, nHD], F32)
    decT = p.sbuf("decT", [128, nHD], F32)
    ecrT = p.sbuf("ecrT", [128, nHD], F32)
    cdec_row = p.sbuf("cdecrow", [1, nHD], F32)
    cdec_bc = p.sbuf("cdecbc", [64, nHD], F32)
    btok = p.sbuf("btok", [128, DS], BF16)
    sm = p.sbuf("sm", [128, 128], BF16)
    colbc = p.sbuf("colbc", [128, 128], F32)
    colbc64 = p.sbuf("colbc64", [128, 64], F32)
    lrelu = p.sbuf("lrelu", [128, 128], F32)
    lt = p.sbuf("lt", [128, 128], BF16)
    phat = p.sbuf("phat", [128, 128], BF16)
    xdt = p.sbuf("xdt", [128, 128], BF16)
    bd = p.sbuf("bd", [128, DS], BF16)
    chat = p.sbuf("chat", [DS, 128], BF16)
    carry = p.sbuf("carry", [DS, nHD, 128], F32)
    prevbf = p.sbuf("prevbf", [DS, 128], BF16)
    ysb = p.sbuf("ysb", [128, 128], F32)

    ps_S = p.psum("S", [128, 128], F32)
    ps_tr = p.psum("tr", [128, 128], F32)
    ps_xt = p.psum("xt", [128, 128], BF16)
    ps_st = p.psum("st", [DS, 128], F32)
    ps_y = p.psum("y", [128, 128], F32)
    ps_sm = p.psum("smll", [128, 16], F32)

    for cst, dst in ((id16, id16_sb), (id64_bf, id64b_sb), (id128_bf, id128b_sb),
                     (id128_f, id128f_sb), (triu_c, triu_sb), (ones1x64, on64_sb),
                     (ones1x128f, on128_sb)):
        p.dma(dst[:], cst[:])
    p.g("memset", ones128t[:], 1.0)
    p.g("memset", ones_s[:], 1.0)
    p.g("memset", carry[:], 0.0)
    p.dma(xf[:], xf_i.rearrange("(h p) s -> p h s", p=128))
    p.dma(bsb[:], b_i[:])
    p.dma(csb[:], c_i[:])
    p.dma(dt_sb[:], dt_i[:])
    p.dma(alog[:], alog_i[:])
    p.dma(dvec[:], dvec_i[:])
    p.act(acol[:], alog[:], AF.Exp)
    p.s("mul", acol[:], acol[:], -1.0)
    p.v("tensor_scalar", dA[:], dt_sb[:], acol[:], None, ALU.mult)
    p.v("tensor_tensor_scan", cum[:], ones_s[:], dA[:], 0.0, ALU.mult, ALU.add)
    p.s("mul", negcum[:], cum[:], -1.0)
    p.mm(ps_sm[:, 0:nHD], on128_sb[:], dvec[:])
    p.v("tensor_copy", dcols[:], ps_sm[:, 0:nHD])
    for c in range(nC):
        csl = slice(c * MCH, (c + 1) * MCH)
        last = (c + 1) * MCH - 1
        if c == 0:
            p.act(ecr[:, csl], cum[:, csl], AF.Exp)
            p.act(cdec[:, 0:1], cum[:, last:last + 1], AF.Exp)
        else:
            prev = c * MCH - 1
            p.act(ecr[:, csl], cum[:, csl], AF.Exp, bias=negcum[:, prev:prev + 1])
            p.act(cdec[:, c:c + 1], cum[:, last:last + 1], AF.Exp,
                  bias=negcum[:, prev:prev + 1])
        p.act(dec[:, csl], negcum[:, csl], AF.Exp, bias=cum[:, last:last + 1])

    for c in range(nC):
        csl = slice(c * MCH, (c + 1) * MCH)
        for (src, dst) in ((cum, cumT), (dt_sb, dtT), (dec, decT), (ecr, ecrT)):
            p.trans(ps_sm[:], src[:, csl], id16_sb[:])
            p.v("tensor_copy", dst[:], ps_sm[:])
        p.s("mul", negcumT[:], cumT[:], -1.0)
        p.trans(ps_xt[:, 0:DS], bsb[:, csl], id64b_sb[:])
        p.v("tensor_copy", btok[:], ps_xt[:, 0:DS])
        p.mm(ps_S[:], bsb[:, csl], csb[:, csl])
        p.v("tensor_tensor", sm[:], ps_S[:], triu_sb[:], ALU.mult)
        p.trans(ps_sm[0:1, 0:nHD], cdec[:, c:c + 1], id16_sb[:])
        p.v("tensor_copy", cdec_row[:], ps_sm[0:1, 0:nHD])
        p.mm(ps_sm[0:64, 0:nHD], on64_sb[:], cdec_row[:])
        p.v("tensor_copy", cdec_bc[:], ps_sm[0:64, 0:nHD])
        for h in range(nHD):
            p.g("tensor_scalar", colbc[:], ones128t[:], negcumT[:, h:h + 1], None, ALU.mult)
            p.trans(ps_tr[:], colbc[:], id128f_sb[:])
            p.v("tensor_scalar", lrelu[:], ps_tr[:], cumT[:, h:h + 1], 0.0,
                ALU.add, ALU.max)
            p.act(lt[:], lrelu[:], AF.Exp, scale=-1.0)
            p.g("tensor_tensor", phat[:], sm[:], lt[:], ALU.mult)
            p.trans(ps_xt[:], xf[:, h, csl], id128b_sb[:])
            p.v("tensor_scalar", xdt[:], ps_xt[:], dtT[:, h:h + 1], None, ALU.mult)
            p.g("tensor_scalar", bd[:], btok[:], decT[:, h:h + 1], None, ALU.mult)
            p.mm(ps_st[:], bd[:], xdt[:])
            p.g("tensor_scalar", colbc64[:], ones128t[:, 0:64], ecrT[:, h:h + 1],
                None, ALU.mult)
            p.trans(ps_tr[0:64, :], colbc64[:], id128f_sb[:])
            p.v("tensor_tensor", chat[:], ps_tr[0:64, :], csb[:, csl], ALU.mult)
            p.v("tensor_copy", prevbf[:], carry[:, h, :])
            p.v("scalar_tensor_tensor", carry[:, h, :], carry[:, h, :],
                cdec_bc[:, h:h + 1], ps_st[:], ALU.mult, ALU.add)
            p.mm(ps_y[:], xdt[:], phat[:], start=True, stop=False)
            p.mm(ps_y[:], prevbf[:], chat[:], start=False, stop=True)
            p.v("scalar_tensor_tensor", y_all[:, h, csl], xf[:, h, csl],
                dcols[:, h:h + 1], ps_y[:], ALU.mult, ALU.add)
    for h in range(nHD):
        p.dma(y_o[h * 128:(h + 1) * 128, :], y_all[:, h, :])
    return p.finish()


def run_ssd(xbcc, dt, A_log, Dv):
    nc = build_ssd()
    in_maps = []
    for c in range(NCORES):
        b, hh = c // 2, c % 2
        in_maps.append({
            "xf": np.ascontiguousarray(xbcc[b, hh * 2048:(hh + 1) * 2048]),
            "bi": np.ascontiguousarray(xbcc[b, DI:DI + DS]),
            "ci": np.ascontiguousarray(xbcc[b, DI + DS:DI + 2 * DS]),
            "dt": np.ascontiguousarray(dt[b, hh * 16:(hh + 1) * 16]),
            "alog": np.ascontiguousarray(A_log[hh * 16:(hh + 1) * 16].reshape(16, 1)),
            "dvec": np.ascontiguousarray(Dv[hh * 16:(hh + 1) * 16].reshape(1, 16)),
        })
    res = _run(nc, in_maps)
    return np.stack([np.concatenate([res[2 * b]["y"], res[2 * b + 1]["y"]], axis=0)
                     for b in range(B)])


def build_mambaout():
    p = Prog()
    nCC = DI // 128
    y_i = p.inp("y", (nCC, 128, TPC), BF16)
    sz_i = p.inp("sz", (nCC, 128, TPC), BF16)
    wout = p.inp("wout", (nCC, 128, D))
    normw = p.inp("normw", (128, nCC))
    h2_i = p.inp("h2", (TPC, D))
    h3_o = p.outp("h3", (TPC, D))

    ones_col = p.const("ones_col", _ones_col_bf)
    ones_row = p.const("ones_row", _ones_row_bf)

    yg = p.sbuf("yg", [128, nCC, TPC], BF16)
    szt = p.sbuf("szt", [128, TPC], BF16)
    sq = p.sbuf("sq", [128, TPC], BF16)
    onec = p.sbuf("onec", [128, 1], BF16)
    oner = p.sbuf("oner", [1, 128], BF16)
    nw = p.sbuf("nw", [128, nCC], F32)
    ssq = p.sbuf("ssq", [1, TPC], F32)
    rstd = p.sbuf("rstd", [1, TPC], F32)
    rstdb = p.sbuf("rstdb", [1, TPC], BF16)
    rstd_bc = p.sbuf("rstdbc", [128, TPC], F32)
    wo8 = p.sbuf("wo8", [128, 8, D], BF16)
    macc = p.sbuf("macc", [128, 4, D], F32)

    ps_a = p.psum("a", [1, 512], F32)
    ps_b = p.psum("b", [1, 512], F32)
    ps_bc = p.psum("bc", [128, 512], F32)
    ps_d = p.psum("d", [128, 512], F32)

    p.dma(onec[:], ones_col[:])
    p.dma(oner[:], ones_row[:])
    p.dma(nw[:], normw[:])
    for cc in range(nCC):
        p.dma(yg[:, cc, :], y_i[cc])
        p.dma(szt[:], sz_i[cc])
        p.g("tensor_tensor", yg[:, cc, :], yg[:, cc, :], szt[:], ALU.mult)
        p.act(sq[:], yg[:, cc, :], AF.Square)
        for half, psx in ((0, ps_a), (1, ps_b)):
            p.mm(psx[:], onec[:], sq[:, half * 512:(half + 1) * 512],
                 start=(cc == 0), stop=(cc == nCC - 1))
    p.v("tensor_copy", ssq[:, 0:512], ps_a[:])
    p.v("tensor_copy", ssq[:, 512:1024], ps_b[:])
    p.s("mul", ssq[:], ssq[:], 1.0 / DI)
    p.v("tensor_scalar", ssq[:], ssq[:], 1e-5, None, ALU.add)
    p.act(ssq[:], ssq[:], AF.Sqrt)
    p.v("reciprocal", rstd[:], ssq[:])
    p.v("tensor_copy", rstdb[:], rstd[:])
    for half in range(2):
        p.mm(ps_bc[:], oner[:], rstdb[:, half * 512:(half + 1) * 512])
        p.v("tensor_copy", rstd_bc[:, half * 512:(half + 1) * 512], ps_bc[:])
    for cc in range(nCC):
        p.v("scalar_tensor_tensor", yg[:, cc, :], yg[:, cc, :], nw[:, cc:cc + 1],
            rstd_bc[:], ALU.mult, ALU.mult)
    for g2 in range(TPC // 512):
        for st in range(4):
            row0 = g2 * 512 + st * 128
            p.dma(macc[:, st, :], h2_i[row0:row0 + 128, :])
        for ccg in range(4):
            p.dma(wo8[:], wout[ccg * 8:(ccg + 1) * 8].rearrange("c p d -> p c d"),
                  eng="gpsimd")
            for st in range(4):
                for dg in range(D // 512):
                    dsl = slice(dg * 512, (dg + 1) * 512)
                    for jc in range(8):
                        cc = ccg * 8 + jc
                        p.mm(ps_d[:], yg[:, cc, g2 * 512 + st * 128: g2 * 512 + (st + 1) * 128],
                             wo8[:, jc, dsl], start=(jc == 0), stop=(jc == 7))
                    p.v("tensor_tensor", macc[:, st, dsl], ps_d[:], macc[:, st, dsl], ALU.add)
        for st in range(4):
            row0 = g2 * 512 + st * 128
            p.dma(h3_o[row0:row0 + 128, :], macc[:, st, :])
    return p.finish()


def run_mambaout(y, sz, W_out, norm_w, h2):
    nc = build_mambaout()
    nCC = DI // 128
    wout_l = np.ascontiguousarray(W_out.T.reshape(nCC, 128, D))
    nw_l = np.ascontiguousarray(norm_w.reshape(nCC, 128).T)
    in_maps = []
    for c in range(NCORES):
        r0 = c * TPC
        b, s0 = r0 // S, r0 % S
        in_maps.append({
            "y": np.ascontiguousarray(y[b, :, s0:s0 + TPC].reshape(nCC, 128, TPC)),
            "sz": np.ascontiguousarray(sz[b, :, s0:s0 + TPC].reshape(nCC, 128, TPC)),
            "wout": wout_l, "normw": nw_l,
            "h2": np.ascontiguousarray(h2[r0:r0 + TPC]),
        })
    res = _run(nc, in_maps)
    return np.concatenate([r["h3"] for r in res], axis=0)




def build_qgamma():
    p = Prog()
    w1s = p.inp("w1s", (HF, 256))
    w2s = p.inp("w2s", (HF, 256))
    w3s = p.inp("w3s", (D, 704))
    part_o = p.outp("part", (1, 3))
    onescol = p.const("onescol", np.ones((128, 1), dtype=np.float32))

    wt = p.sbuf("wt", [128, 704], F32)
    acc = p.sbuf("acc", [128, 1], F32)
    red = p.sbuf("red", [128, 1], F32)
    ones_sb = p.sbuf("ones", [128, 1], F32)
    res = p.sbuf("res", [1, 3], F32)
    ps = p.psum("ps", [1, 1], F32)

    p.dma(ones_sb[:], onescol[:])
    for j, (w, rows, cols) in enumerate(((w1s, HF, 256), (w2s, HF, 256), (w3s, D, 704))):
        p.g("memset", acc[:], 0.0)
        for r in range(rows // 128):
            p.dma(wt[:, 0:cols], w[r * 128:(r + 1) * 128, :])
            p.v("reduce_sum", red[:], wt[:, 0:cols], AX.X, apply_absolute_value=True)
            p.v("tensor_add", acc[:], acc[:], red[:])
        p.mm(ps[:], acc[:], ones_sb[:])
        p.v("tensor_copy", res[:, j:j + 1], ps[:])
    p.dma(part_o[:], res[:])
    return p.finish()


def build_quant():
    p = Prog()
    w1s = p.inp("w1s", (704, D))
    w2s = p.inp("w2s", (704, D))
    w3s = p.inp("w3s", (256, HF))
    parts = p.inp("parts", (8, 3))
    q1_o = p.outp("q1t", (D, 704), BF16)
    q2_o = p.outp("q2t", (D, 704), BF16)
    q3_o = p.outp("q3t", (HF, 256), BF16)

    id128b = p.const("id128b", _eye_bf)
    id64b = p.const("id64b", np.eye(64, dtype=BF))
    id3 = p.const("id3", np.eye(3, dtype=np.float32))
    ones8 = p.const("ones8", np.ones((8, 1), dtype=np.float32))
    ones1x128 = p.const("ones1x128", _ones_row_f32)

    COUNT = float(HF * D)
    parts_sb = p.sbuf("parts", [8, 3], F32)
    ones8_sb = p.sbuf("ones8", [8, 1], F32)
    id3_sb = p.sbuf("id3", [3, 3], F32)
    on128_sb = p.sbuf("on128", [1, 128], F32)
    id128b_sb = p.sbuf("id128b", [128, 128], BF16)
    id64b_sb = p.sbuf("id64b", [64, 64], BF16)
    tot = p.sbuf("tot", [3, 1], F32)
    rgrow = p.sbuf("rgrow", [1, 3], F32)
    rgcol = p.sbuf("rgcol", [128, 3], F32)
    nrgcol = p.sbuf("nrgcol", [128, 3], F32)
    wt = p.sbuf("wt", [128, HF], F32)
    gtp = p.sbuf("gtp", [128, HF], F32)
    gtm = p.sbuf("gtm", [128, HF], F32)
    qt = p.sbuf("qt", [128, HF], BF16)
    qtt = p.sbuf("qtt", [128, 128], BF16)
    ps = p.psum("ps", [3, 3], F32)
    ps_b = p.psum("b", [128, 128], F32)
    ps_t = p.psum("t", [128, 128], BF16)

    for cst, dst in ((id128b, id128b_sb), (id64b, id64b_sb), (id3, id3_sb),
                     (ones8, ones8_sb), (ones1x128, on128_sb)):
        p.dma(dst[:], cst[:])
    p.dma(parts_sb[:], parts[:])
    p.mm(ps[0:3, 0:1], parts_sb[:], ones8_sb[:])
    p.v("tensor_copy", tot[:], ps[0:3, 0:1])
    p.s("mul", tot[:], tot[:], 1.0 / COUNT)
    p.v("tensor_scalar", tot[:], tot[:], 1e-8, None, ALU.add)
    p.v("reciprocal", tot[:], tot[:])
    p.trans(ps[0:1, 0:3], tot[:], id3_sb[:])
    p.v("tensor_copy", rgrow[:], ps[0:1, 0:3])
    for j in range(3):
        p.mm(ps_b[:, 0:1], on128_sb[:], rgrow[:, j:j + 1])
        p.v("tensor_copy", rgcol[:, j:j + 1], ps_b[:, 0:1])
    p.s("mul", nrgcol[:], rgcol[:], -1.0)

    def quant(w, q_o, j, rows, cols):
        nrc = rows // 128 + (1 if rows % 128 else 0)
        for r in range(nrc):
            pr = min(128, rows - r * 128)
            ident = id128b_sb if pr == 128 else id64b_sb
            p.dma(wt[0:pr, 0:cols], w[r * 128:r * 128 + pr, :])
            p.v("tensor_scalar", gtp[0:pr, 0:cols], wt[0:pr, 0:cols],
                rgcol[0:pr, j:j + 1], 0.5, ALU.mult, ALU.is_gt)
            p.v("tensor_scalar", gtm[0:pr, 0:cols], wt[0:pr, 0:cols],
                nrgcol[0:pr, j:j + 1], 0.5, ALU.mult, ALU.is_gt)
            p.v("tensor_tensor", qt[0:pr, 0:cols], gtp[0:pr, 0:cols],
                gtm[0:pr, 0:cols], ALU.subtract)
            for dc in range(cols // 128):
                p.trans(ps_t[:, 0:pr], qt[0:pr, dc * 128:(dc + 1) * 128], ident[0:pr, 0:pr])
                p.v("tensor_copy", qtt[:, 0:pr], ps_t[:, 0:pr])
                p.dma(q_o[dc * 128:(dc + 1) * 128, r * 128:r * 128 + pr], qtt[:, 0:pr])

    quant(w1s, q1_o, 0, 704, D)
    quant(w2s, q2_o, 1, 704, D)
    quant(w3s, q3_o, 2, 256, HF)
    return p.finish()


def run_quantize(w1, w2, w3):
    nc = build_qgamma()
    in_maps = []
    for c in range(NCORES):
        in_maps.append({
            "w1s": np.ascontiguousarray(w1[:, c * 256:(c + 1) * 256]),
            "w2s": np.ascontiguousarray(w2[:, c * 256:(c + 1) * 256]),
            "w3s": np.ascontiguousarray(w3[:, c * 704:(c + 1) * 704]),
        })
    res = _run(nc, in_maps)
    parts = np.concatenate([r["part"] for r in res], axis=0)

    nc2 = build_quant()
    in_maps = []
    for c in range(NCORES):
        in_maps.append({
            "parts": parts,
            "w1s": np.ascontiguousarray(w1[c * 704:(c + 1) * 704]),
            "w2s": np.ascontiguousarray(w2[c * 704:(c + 1) * 704]),
            "w3s": np.ascontiguousarray(w3[c * 256:(c + 1) * 256]),
        })
    res = _run(nc2, in_maps)
    q1t = np.concatenate([r["q1t"] for r in res], axis=1)
    q2t = np.concatenate([r["q2t"] for r in res], axis=1)
    q3t = np.concatenate([r["q3t"] for r in res], axis=1)
    return q1t, q2t, q3t


def build_bitffn():
    p = Prog()
    nFT = HF // 128
    nDC = D // 128
    xln = p.inp("xln", (TPC, D), BF16)
    h3_i = p.inp("h3", (TPC, D))
    q1 = p.inp("q1", (nFT, 128, nDC, 128), BF16)
    q2 = p.inp("q2", (nFT, 128, nDC, 128), BF16)
    q3 = p.inp("q3", (nFT, 128, D), BF16)
    out_o = p.outp("out", (TPC, D))

    xt = p.sbuf("xt", [128, nDC, TPC], BF16)
    w1b = p.sbuf("w1b", [128, nDC, 128], BF16)
    w2b = p.sbuf("w2b", [128, nDC, 128], BF16)
    sg1 = p.sbuf("sg1", [128, 512], BF16)
    hid = p.sbuf("hid", [128, nFT, TPC], BF16)
    q3b = p.sbuf("q3b", [128, 8, D], BF16)
    macc = p.sbuf("macc", [128, 4, D], F32)
    ps1 = p.psum("x1", [128, 512], F32)
    ps2 = p.psum("x2", [128, 512], F32)
    ps_d = p.psum("d", [128, 512], F32)

    for dc in range(nDC):
        p.dma(xt[:, dc, :], xln[:, dc * 128:(dc + 1) * 128], transpose=True)
    for ft in range(nFT):
        p.dma(w1b[:], q1[ft])
        p.dma(w2b[:], q2[ft])
        for sg2 in range(TPC // 512):
            gsl = slice(sg2 * 512, (sg2 + 1) * 512)
            for dc in range(nDC):
                p.mm(ps1[:], w1b[:, dc, :], xt[:, dc, gsl],
                     start=(dc == 0), stop=(dc == nDC - 1))
            p.act(sg1[:], ps1[:], AF.Sigmoid)
            for dc in range(nDC):
                p.mm(ps2[:], w2b[:, dc, :], xt[:, dc, gsl],
                     start=(dc == 0), stop=(dc == nDC - 1))
            p.v("tensor_tensor", hid[:, ft, gsl], ps2[:], sg1[:], ALU.mult)
    blocks = [(b8 * 8, min(8, nFT - b8 * 8)) for b8 in range((nFT + 7) // 8)]
    for g2 in range(TPC // 512):
        for st in range(4):
            row0 = g2 * 512 + st * 128
            p.dma(macc[:, st, :], h3_i[row0:row0 + 128, :])
        for (fc0, bw) in blocks:
            p.dma(q3b[:, 0:bw, :], q3[fc0:fc0 + bw].rearrange("c p d -> p c d"),
                  eng="gpsimd")
            for st in range(4):
                ssl = slice(g2 * 512 + st * 128, g2 * 512 + (st + 1) * 128)
                for dg in range(D // 512):
                    dsl = slice(dg * 512, (dg + 1) * 512)
                    for jc in range(bw):
                        p.mm(ps_d[:], hid[:, fc0 + jc, ssl], q3b[:, jc, dsl],
                             start=(jc == 0), stop=(jc == bw - 1))
                    p.v("tensor_tensor", macc[:, st, dsl], ps_d[:], macc[:, st, dsl], ALU.add)
        for st in range(4):
            row0 = g2 * 512 + st * 128
            p.dma(out_o[row0:row0 + 128, :], macc[:, st, :])
    return p.finish()


def run_bitffn(xln, h3, q1t, q2t, q3t):
    nc = build_bitffn()
    nFT = HF // 128
    nDC = D // 128
    def lay12(qt):
        return np.ascontiguousarray(
            qt.reshape(nDC, 128, nFT, 128).transpose(2, 1, 0, 3))
    q1l, q2l = lay12(q1t), lay12(q2t)
    q3l = np.ascontiguousarray(q3t.reshape(nFT, 128, D))
    in_maps = []
    for c in range(NCORES):
        r0 = c * TPC
        in_maps.append({
            "xln": np.ascontiguousarray(xln[r0:r0 + TPC]),
            "h3": np.ascontiguousarray(h3[r0:r0 + TPC]),
            "q1": q1l, "q2": q2l, "q3": q3l,
        })
    res = _run(nc, in_maps)
    return np.concatenate([r["out"] for r in res], axis=0)




def _carve(pool, start, shape):
    n = 1
    for s in shape:
        n *= s
    ap = pool[:, start:start + n]
    if len(shape) == 1:
        return ap
    names = " ".join(f"d{i}" for i in range(len(shape)))
    kw = {f"d{i}": s for i, s in enumerate(shape)}
    return ap.rearrange(f"p ({names}) -> p {names}", **kw)


class Env:
    pass


def build_fused(dbg=(), stop_after=None):
    p = Prog()
    e = Env()
    nDC = D // 128

    e.xb = p.inp("xb", (S, D))
    e.xrows = p.inp("xrows", (TPC, D))
    e.rolemask = p.inp("rolemask", (128, 1))
    e.wq = p.inp("wq", (HPC, 128, nDC, EH))
    e.wk = p.inp("wk", (HPC, 128, nDC, EH))
    e.wv = p.inp("wv", (HPC, 128, nDC, EH))
    e.wo = p.inp("wo", (EH, NH, D))
    e.gw = p.inp("gw", (128, nDC, NE))
    e.sge = p.inp("sge", (128, nDC))
    e.eg = p.inp("eg", (NE, 128, nDC, IE))
    e.eu = p.inp("eu", (NE, 128, nDC, IE))
    e.ed = p.inp("ed", (NE, 128, IE // 128, D))
    e.sg = p.inp("sg", (ISH // 128, 128, nDC, 128))
    e.su = p.inp("su", (ISH // 128, 128, nDC, 128))
    e.sd = p.inp("sd", (4, 128, 8, D))
    e.win = p.inp("win", (66, 128, nDC, 128))
    e.cw = p.inp("cw", (128, CONV_DIM // 128, DCONV))
    e.cb = p.inp("cb", (128, CONV_DIM // 128))
    e.dtb = p.inp("dtb", (32, 1))
    e.alog = p.inp("alog", (32, 1))
    e.dvec = p.inp("dvec", (1, 32))
    e.wout = p.inp("wout", (DI // 128, 128, D))
    e.normw = p.inp("normw", (128, DI // 128))
    e.w1t = p.inp("w1t", (D, HF))
    e.w2t = p.inp("w2t", (D, HF))
    e.w3t = p.inp("w3t", (HF, D))
    e.out = p.outp("out", (TPC, D))

    dram = lambda n, shp, dt=F32: p.nc.dram_tensor(n, list(shp), dt).ap()
    e.s_xln = dram("s_xln", (S, D), BF16)
    e.s_h = dram("s_h", (TPC, D))
    e.s_zloc = dram("s_zloc", (HPC * EH, S), BF16)
    e.s_zgat = dram("s_zgat", (NH * EH, S), BF16)
    e.s_xbraw = dram("s_xbraw", (CONV_DIM, TPC), BF16)
    e.s_xctr = dram("s_xctr", (CONV_DIM, 4), BF16)
    e.s_xgat = dram("s_xgat", (2 * CONV_DIM, 4), BF16)
    e.s_sz = dram("s_sz", (DI, TPC), BF16)
    e.s_xbcc = dram("s_xbcc", (CONV_DIM, TPC), BF16)
    e.s_dt = dram("s_dt", (32, TPC))
    e.s_cctr = dram("s_cctr", (DS * 32, 128))
    e.s_cgat = dram("s_cgat", (2 * DS * 32, 128))
    e.s_q1 = dram("s_q1", (D, HF), BF16)
    e.s_q2 = dram("s_q2", (D, HF), BF16)
    e.s_q3 = dram("s_q3", (HF, D), BF16)

    e.A = p.sbuf("poolA", [128, 32768], BF16)[:]
    e.B = p.sbuf("poolB", [128, 32768], BF16)[:]
    e.C = p.sbuf("poolC", [128, 8192], F32)[:]
    e.Dp = p.sbuf("poolD", [128, 8192], F32)[:]
    e.ps_big = p.psum("ps_big", [128, 2048], F32)[:]
    e.ps_a = p.psum("ps_a", [128, 512], F32)[:]
    e.ps_b = p.psum("ps_b", [128, 512], F32)[:]
    e.ps_c = p.psum("ps_c", [128, 512], F32)[:]
    e.ps_bf = p.psum("ps_bf", [128, 256], BF16)[:]

    e.id128b = p.sbuf("id128b", [128, 128], BF16)
    e.id128f = p.sbuf("id128f", [128, 128], F32)
    e.id64b = p.sbuf("id64b", [64, 64], BF16)
    e.id32f = p.sbuf("id32f", [32, 32], F32)
    e.trilb = p.sbuf("trilb", [128, 128], BF16)
    e.triub = p.sbuf("triub", [128, 128], BF16)
    e.onecol_b = p.sbuf("onecol_b", [128, 1], BF16)
    e.onerow_b = p.sbuf("onerow_b", [1, 128], BF16)
    e.onerow_f = p.sbuf("onerow_f", [1, 128], F32)
    e.on64_f = p.sbuf("on64_f", [1, 64], F32)
    e.ones128t = p.sbuf("ones128t", [128, 128], F32)
    for name, arr, dst in (
        ("c_id128b", _eye_bf, e.id128b), ("c_id128f", _eye_f32, e.id128f),
        ("c_id64b", np.eye(64, dtype=BF), e.id64b),
        ("c_id32f", np.eye(32, dtype=np.float32), e.id32f),
        ("c_trilb", _tril_bf, e.trilb), ("c_triub", _triu_bf, e.triub),
        ("c_onecol", _ones_col_bf, e.onecol_b), ("c_onerow", _ones_row_bf, e.onerow_b),
        ("c_onerowf", _ones_row_f32, e.onerow_f),
        ("c_on64f", np.ones((1, 64), dtype=np.float32), e.on64_f),
        ("c_id11f", np.ones((1, 1), dtype=np.float32), None),
    ):
        if dst is None:
            e.id11f = p.sbuf("id11f", [1, 1], F32)
            dst = e.id11f
        cst = p.const(name, arr)
        p.dma(dst[:], cst[:])
    p.g("memset", e.ones128t[:], 1.0)
    e.role = p.sbuf("role", [128, 1], F32)
    e.invrole = p.sbuf("invrole", [128, 1], F32)
    p.dma(e.role[:], e.rolemask[:])
    p.v("tensor_scalar", e.invrole[:], e.role[:], -1.0, 1.0, ALU.mult, ALU.add)

    phases = [
        ("quant", lambda: emit_quant_f(p, e)),
        ("ln0", lambda: emit_ln_f(p, e, e.xb, e.s_xln, S)),
        ("attnz", lambda: emit_attnz_f(p, e)),
        ("collz", lambda: p.coll(lambda g, i=e.s_zloc, o=e.s_zgat: g.collective_compute(
            "AllGather", ALU.bypass, [[0, 1], [2, 3], [4, 5], [6, 7]], [i[:, :]], [o[:, :]]),
            reads=[e.s_zloc[:, :]], writes=[e.s_zgat[:, :]])),
        ("attnout", lambda: emit_attnout_f(p, e)),
        ("ln1", lambda: emit_ln_f(p, e, e.s_h, e.s_xln, TPC)),
        ("moe", lambda: emit_moe_f(p, e)),
        ("ln2", lambda: emit_ln_f(p, e, e.s_h, e.s_xln, TPC)),
        ("mambain", lambda: emit_mambain_f(p, e)),
        ("collx", lambda: p.coll(lambda g, i=e.s_xctr, o=e.s_xgat: g.collective_compute(
            "AllGather", ALU.bypass, [[0, 1], [2, 3], [4, 5], [6, 7]], [i[:, :]], [o[:, :]]),
            reads=[e.s_xctr[:, :]], writes=[e.s_xgat[:, :]])),
        ("conv", lambda: emit_conv_f(p, e)),
        ("ssdA", lambda: emit_ssdA_f(p, e)),
        ("collc", lambda: p.coll(lambda g, i=e.s_cctr, o=e.s_cgat: g.collective_compute(
            "AllGather", ALU.bypass, [[0, 1], [2, 3], [4, 5], [6, 7]], [i[:, :]], [o[:, :]]),
            reads=[e.s_cctr[:, :]], writes=[e.s_cgat[:, :]])),
        ("ssdB", lambda: emit_ssdB_f(p, e)),
        ("mambaout", lambda: emit_mambaout_f(p, e)),
        ("ln3", lambda: emit_ln_f(p, e, e.s_h, e.s_xln, TPC)),
        ("bitffn", lambda: emit_bitffn_f(p, e)),
    ]
    for name, fn in phases:
        fn()
        if stop_after == name:
            break

    for name in dbg:
        src_ap, shp, dt = {
            "xln": (e.s_xln, (S, D), BF16),
            "h": (e.s_h, (TPC, D), F32),
            "zgat": (e.s_zgat, (NH * EH, S), BF16),
            "sz": (e.s_sz, (DI, TPC), BF16),
            "xbcc": (e.s_xbcc, (CONV_DIM, TPC), BF16),
            "xbraw": (e.s_xbraw, (CONV_DIM, TPC), BF16),
            "xgat": (e.s_xgat, (2 * CONV_DIM, 4), BF16),
            "dt": (e.s_dt, (32, TPC), F32),
            "q1": (e.s_q1, (D, HF), BF16),
            "q3": (e.s_q3, (HF, D), BF16),
            "cgat": (e.s_cgat, (2 * DS * 32, 128), F32),
        }[name]
        o = p.outp("dbg_" + name, shp, dt)
        rows, cols = shp
        nt = (rows + 127) // 128
        for i in range(nt):
            pr = min(128, rows - i * 128)
            stage = _carve(e.B, 0, (cols,)) if dt != F32 else _carve(e.C, 0, (cols,))
            p.dma(stage[0:pr, :], src_ap[i * 128:i * 128 + pr, :])
            p.dma(o[i * 128:i * 128 + pr, :], stage[0:pr, :])
    return p.finish()


def emit_quant_f(p, e):
    COUNT = float(HF * D)
    CH = 1408
    acc = p.sbuf("q_acc", [128, 1], F32)
    red = p.sbuf("q_red", [128, 1], F32)
    onescol_f = p.sbuf("q_ones", [128, 1], F32)
    tot3 = p.sbuf("q_tot", [1, 3], F32)
    rgcol = p.sbuf("q_rg", [128, 3], F32)
    nrgcol = p.sbuf("q_nrg", [128, 3], F32)
    p.g("memset", onescol_f[:], 1.0)
    ps1 = e.ps_a
    wt2 = [_carve(e.Dp, 0, (CH,)), _carve(e.Dp, CH, (CH,))]
    gt2 = [_carve(e.Dp, 2 * CH, (CH,)), _carve(e.Dp, 3 * CH, (CH,))]
    mats = ((e.w1t, e.s_q1, D, HF, 0), (e.w2t, e.s_q2, D, HF, 1),
            (e.w3t, e.s_q3, HF, D, 2))
    for (w, _, rows, cols, j) in mats:
        p.g("memset", acc[:], 0.0)
        k = 0
        for r in range(rows // 128):
            for c0 in range(0, cols, CH):
                wt = wt2[k % 2]
                k += 1
                cw = min(CH, cols - c0)
                p.dma(wt[:, 0:cw], w[r * 128:(r + 1) * 128, c0:c0 + cw], eng="gpsimd")
                p.v("reduce_sum", red[:], wt[:, 0:cw], AX.X, apply_absolute_value=True)
                p.v("tensor_add", acc[:], acc[:], red[:])
        p.mm(ps1[0:1, 0:1], acc[:], onescol_f[:])
        p.v("tensor_copy", tot3[:, j:j + 1], ps1[0:1, 0:1])
    p.s("mul", tot3[:], tot3[:], 1.0 / COUNT)
    p.v("tensor_scalar", tot3[:], tot3[:], 1e-8, None, ALU.add)
    p.v("reciprocal", tot3[:], tot3[:])
    for j in range(3):
        p.mm(ps1[0:128, 0:1], e.onerow_f[:], tot3[:, j:j + 1])
        p.v("tensor_copy", rgcol[:, j:j + 1], ps1[0:128, 0:1])
    p.s("mul", nrgcol[:], rgcol[:], -1.0)
    k = 0
    for (w, q_o, rows, cols, j) in mats:
        for r in range(rows // 128):
            for c0 in range(0, cols, CH):
                wt, gtm = wt2[k % 2], gt2[k % 2]
                k += 1
                cw = min(CH, cols - c0)
                p.dma(wt[:, 0:cw], w[r * 128:(r + 1) * 128, c0:c0 + cw], eng="gpsimd")
                p.g("tensor_scalar", gtm[:, 0:cw], wt[:, 0:cw],
                    nrgcol[:, j:j + 1], 0.5, ALU.mult, ALU.is_gt)
                p.g("tensor_scalar", wt[:, 0:cw], wt[:, 0:cw],
                    rgcol[:, j:j + 1], 0.5, ALU.mult, ALU.is_gt)
                p.v("tensor_tensor", gtm[:, 0:cw], wt[:, 0:cw], gtm[:, 0:cw], ALU.subtract)
                p.dma(q_o[r * 128:(r + 1) * 128, c0:c0 + cw], gtm[:, 0:cw], eng="gpsimd")


def emit_ln_f(p, e, src, dst, rows):
    x_sb = _carve(e.C, 0, (D,))
    sq_sb = _carve(e.C, 2048, (D,))
    o_sb = _carve(e.A, 0, (D,))
    rs = p.sbuf("ln_rs", [128, 1], F32)
    mean = p.sbuf("ln_mean", [128, 1], F32)
    ssq = p.sbuf("ln_ssq", [128, 1], F32)
    m2 = p.sbuf("ln_m2", [128, 1], F32)
    var = p.sbuf("ln_var", [128, 1], F32)
    rstd = p.sbuf("ln_rstd", [128, 1], F32)
    for i in range(rows // 128):
        sl = slice(i * 128, (i + 1) * 128)
        p.dma(x_sb[:, :], src[sl, :])
        p.v("reduce_sum", rs[:], x_sb[:, :], AX.X)
        p.s("mul", mean[:], rs[:], 1.0 / D)
        p.act(sq_sb[:, :], x_sb[:, :], AF.Square, accum_out=ssq[:])
        p.s("mul", ssq[:], ssq[:], 1.0 / D)
        p.v("tensor_mul", m2[:], mean[:], mean[:])
        p.v("tensor_sub", var[:], ssq[:], m2[:])
        p.v("tensor_scalar", var[:], var[:], 1e-5, None, ALU.add)
        p.act(var[:], var[:], AF.Sqrt)
        p.v("reciprocal", rstd[:], var[:])
        p.v("tensor_scalar", o_sb[:, :], x_sb[:, :], mean[:], rstd[:],
            ALU.subtract, ALU.mult)
        p.dma(dst[sl, :], o_sb[:, :])


def emit_attnz_f(p, e):
    nDC = D // 128
    nST = S // 128
    xt = _carve(e.A, 0, (nDC, S))
    q_sb2 = [_carve(e.B, 0, (S,)), _carve(e.B, 2048, (S,))]
    k_sb2 = [_carve(e.B, 4096, (S,)), _carve(e.B, 6144, (S,))]
    v_sb2 = [_carve(e.B, 8192, (nST, EH)), _carve(e.B, 10240, (nST, EH))]
    P_sb = _carve(e.B, 12288, (S,))
    sq_sb = _carve(e.B, 12288, (S,))
    z_sb2 = [_carve(e.B, 16384, (S,)), _carve(e.B, 18432, (S,))]
    pt_sb = _carve(e.B, 20480, (4, 128))
    diag_sb = _carve(e.B, 20992, (128,))
    Pd_sb = _carve(e.B, 21120, (128,))
    w_q2 = [_carve(e.B, 21248, (nDC, EH)), _carve(e.B, 23296, (nDC, EH))]
    w_k2 = [_carve(e.B, 25344, (nDC, EH)), _carve(e.B, 27392, (nDC, EH))]
    w_v2 = [_carve(e.B, 29440, (nDC, EH)), _carve(e.B, 29440, (nDC, EH))]
    r2 = _carve(e.C, 0, (S,))[0:1, :]
    k2 = _carve(e.C, 2048, (S,))[0:1, :]
    km = p.sbuf("az_km", [1, 1], F32)
    negb = p.sbuf("az_negb", [128, nST], F32)
    sums = p.sbuf("az_sums", [128, 4], F32)
    sumtot = p.sbuf("az_st", [128, 1], F32)
    rsum = p.sbuf("az_rs", [128, 1], F32)
    ps_s = e.ps_big
    ps_p = e.ps_a
    ps_z = e.ps_b
    ps_r = e.ps_c[0:1, :]
    ps_t = e.ps_c
    ps_v = e.ps_c

    for dc in range(nDC):
        p.dma(xt[:, dc, :], e.s_xln[:, dc * 128:(dc + 1) * 128], transpose=True)
    for j in range(HPC):
        par = j % 2
        q_sb, k_sb, v_sb, z_sb = q_sb2[par], k_sb2[par], v_sb2[par], z_sb2[par]
        w_q, w_k, w_v = w_q2[par], w_k2[par], w_v2[par]
        for (w_in, w_dst) in ((e.wq, w_q), (e.wk, w_k), (e.wv, w_v)):
            p.dma(w_dst[:, :, :], w_in[j], eng="gpsimd")
        for (w_dst, dst) in ((w_q, q_sb), (w_k, k_sb)):
            for sg in range(S // 512):
                for dc in range(nDC):
                    p.mm(ps_p[:], w_dst[:, dc, :], xt[:, dc, sg * 512:(sg + 1) * 512],
                         start=(dc == 0), stop=(dc == nDC - 1))
                p.v("tensor_copy", dst[:, sg * 512:(sg + 1) * 512], ps_p[:])
        for st in range(nST):
            for dc in range(nDC):
                p.mm(ps_v[:, 0:EH], xt[:, dc, st * 128:(st + 1) * 128], w_v[:, dc, :],
                     start=(dc == 0), stop=(dc == nDC - 1))
            p.v("tensor_copy", v_sb[:, st, :], ps_v[:, 0:EH])
        for (srcb, dst) in ((q_sb, r2), (k_sb, k2)):
            p.act(sq_sb[:, :], srcb[:, :], AF.Square)
            for sg in range(S // 512):
                p.mm(ps_r[:], e.onecol_b[:], sq_sb[:, sg * 512:(sg + 1) * 512])
                p.v("tensor_copy", dst[:, sg * 512:(sg + 1) * 512], ps_r[:])
        p.v("reduce_max", km[:], k2[:], AX.X)
        p.v("tensor_scalar", r2[:], r2[:], km[:], None, ALU.mult)
        p.act(r2[:], r2[:], AF.Sqrt, scale=1.0 / EH)
        p.s("mul", r2[:], r2[:], -1.0)
        for st in range(nST):
            p.trans(ps_t[0:128, 0:1], r2[:, st * 128:(st + 1) * 128], e.id11f[:])
            p.v("tensor_copy", negb[:, st:st + 1], ps_t[0:128, 0:1])
        for i in range(nST):
            nm = (i + 1) * 128
            ng = (nm + 511) // 512
            lastw = nm - (ng - 1) * 512
            for g in range(ng):
                w = min(512, nm - g * 512)
                p.mm(ps_s[:, g * 512:g * 512 + w], q_sb[:, i * 128:(i + 1) * 128],
                     k_sb[:, g * 512:g * 512 + w])
            for g in range(ng - 1):
                p.act(P_sb[:, g * 512:(g + 1) * 512], ps_s[:, g * 512:(g + 1) * 512],
                      AF.Exp, bias=negb[:, i:i + 1], scale=SCALE,
                      accum_out=sums[:, g:g + 1])
            off = (ng - 1) * 512
            p.act(P_sb[:, off:off + lastw], ps_s[:, off:off + lastw],
                  AF.Exp, bias=negb[:, i:i + 1], scale=SCALE)
            p.g("tensor_tensor", Pd_sb[:, :], P_sb[:, nm - 128:nm], e.trilb[:], ALU.mult)
            p.v("tensor_copy", P_sb[:, nm - 128:nm], Pd_sb[:, :])
            p.v("reduce_sum", sums[:, ng - 1:ng], P_sb[:, off:off + lastw], AX.X)
            p.v("reduce_sum", sumtot[:], sums[:, 0:ng], AX.X)
            p.v("reciprocal", rsum[:], sumtot[:])
            p.v("tensor_scalar", diag_sb[:, :], e.id128b[:], rsum[:], None, ALU.mult)
            for j4 in range((i + 4) // 4):
                cnt = min(4, i + 1 - j4 * 4)
                for u in range(cnt):
                    m = j4 * 4 + u
                    p.mm(ps_p[:, u * 128:(u + 1) * 128],
                         P_sb[:, m * 128:(m + 1) * 128], diag_sb[:, :])
                p.v("tensor_copy", pt_sb[:, 0:cnt, :], ps_p[:, 0:cnt * 128])
                for u in range(cnt):
                    m = j4 * 4 + u
                    p.mm(ps_z[:, 0:128], v_sb[:, m, :], pt_sb[:, u, :],
                         start=(m == 0), stop=(m == i))
            p.v("tensor_copy", z_sb[:, i * 128:(i + 1) * 128], ps_z[:, 0:128])
        p.dma(e.s_zloc[j * 128:(j + 1) * 128, :], z_sb[:, :])


def emit_attnout_f(p, e):
    nDC = D // 128
    z16 = _carve(e.B, 0, (NH, TPC))
    wo_sb = _carve(e.A, 0, (NH, D))
    zh0 = _carve(e.B, 16384, (TPC,))
    zh1 = _carve(e.B, 17408, (TPC,))
    x_sb = _carve(e.C, 0, (D,))
    o_sb = _carve(e.C, 2048, (D,))
    ps = e.ps_a
    p.dma(wo_sb[:, :, :], e.wo[:], eng="gpsimd")
    zg = e.s_zgat.rearrange("(h p) s -> h p s", p=128)
    for h in range(NH):
        p.dma(zh0[:, :], zg[h, :, 0:TPC])
        p.dma(zh1[:, :], zg[h, :, TPC:S])
        p.v("tensor_scalar", zh0[:, :], zh0[:, :], e.invrole[:], None, ALU.mult)
        p.v("scalar_tensor_tensor", z16[:, h, :], zh1[:, :], e.role[:], zh0[:, :],
            ALU.mult, ALU.add)
    for st in range(TPC // 128):
        p.dma(x_sb[:, :], e.xrows[st * 128:(st + 1) * 128, :])
        for dg in range(D // 512):
            for h in range(NH):
                p.mm(ps[:], z16[:, h, st * 128:(st + 1) * 128],
                     wo_sb[:, h, dg * 512:(dg + 1) * 512],
                     start=(h == 0), stop=(h == NH - 1))
            p.v("tensor_add", o_sb[:, dg * 512:(dg + 1) * 512], ps[:],
                x_sb[:, dg * 512:(dg + 1) * 512])
        p.dma(e.s_h[st * 128:(st + 1) * 128, :], o_sb[:, :])




def emit_moe_f(p, e):
    nDC = D // 128
    nST = TPC // 128
    xt = _carve(e.A, 0, (nDC, TPC))
    h_sh = _carve(e.A, 16384, (ISH // 128, 512))
    gw_sb = _carve(e.B, 22528, (nDC, NE))
    combT = _carve(e.B, 0, (NE, TPC))[0:1, :, :]
    sigrow = _carve(e.B, 8192, (TPC,))[0:1, :]
    comb_bc = _carve(e.B, 24576, (NE, TPC))
    sgw2 = [_carve(e.B, 0, (nDC, 128)), _carve(e.B, 2048, (nDC, 128))]
    suw2 = [_carve(e.B, 4096, (nDC, 128)), _carve(e.B, 6144, (nDC, 128))]
    sdw2 = [_carve(e.B, 0, (4, D)), _carve(e.B, 8192, (4, D))]
    wege = _carve(e.B, 0, (nDC, IE))
    wede = _carve(e.B, 0, (IE // 128, D))
    weue = _carve(e.A, 16384, (nDC, IE))
    h_e2 = _carve(e.B, 16384, (IE // 128, TPC))
    macc2 = [_carve(e.C, 0, (4, D)), _carve(e.Dp, 0, (4, D))]
    probs = p.sbuf("mo_probs", [128, NE], F32)
    m1 = p.sbuf("mo_m1", [128, 1], F32)
    sumr = p.sbuf("mo_sumr", [128, 1], F32)
    t8 = p.sbuf("mo_t8", [128, NE], F32)
    mask1 = p.sbuf("mo_mask1", [128, NE], F32)
    pr2 = p.sbuf("mo_pr2", [128, NE], F32)
    m2 = p.sbuf("mo_m2", [128, 1], F32)
    cmask = p.sbuf("mo_cmask", [128, NE], F32)
    combbf = p.sbuf("mo_combbf", [128, NE], BF16)
    sigbf = p.sbuf("mo_sigbf", [128, 1], BF16)
    sge_sb = p.sbuf("mo_sge", [128, nDC], BF16)
    sig_bc = p.sbuf("mo_sigbc", [128, TPC], BF16)
    ps_r = e.ps_big[:, 0:NE + 1]
    ps_m = e.ps_big[:, 512:1024]
    ps_g2 = [e.ps_a, e.ps_big[:, 1024:1536]]
    ps_u2 = [e.ps_b, e.ps_big[:, 1536:2048]]
    ps_d2 = [e.ps_c, e.ps_big[:, 0:512]]

    for dc in range(nDC):
        p.dma(xt[:, dc, :], e.s_xln[0:TPC, dc * 128:(dc + 1) * 128], transpose=True)
    p.dma(gw_sb[:, :, :], e.gw[:], eng="gpsimd")
    p.dma(sge_sb[:], e.sge[:], eng="gpsimd")
    for st in range(nST):
        ssl = slice(st * 128, (st + 1) * 128)
        for dc in range(nDC):
            p.mm(ps_r[:, 0:NE], xt[:, dc, ssl], gw_sb[:, dc, :],
                 start=(dc == 0), stop=(dc == nDC - 1))
        p.v("reduce_max", m1[:], ps_r[:, 0:NE], AX.X)
        p.s("mul", m1[:], m1[:], -1.0)
        p.act(probs[:], ps_r[:, 0:NE], AF.Exp, bias=m1[:], accum_out=sumr[:])
        p.v("reciprocal", sumr[:], sumr[:])
        p.v("tensor_scalar", probs[:], probs[:], sumr[:], None, ALU.mult)
        p.v("reduce_max", m1[:], probs[:], AX.X)
        p.v("tensor_scalar", mask1[:], probs[:], m1[:], None, ALU.is_ge)
        p.v("tensor_mul", t8[:], probs[:], mask1[:])
        p.v("tensor_sub", pr2[:], probs[:], t8[:])
        p.v("reduce_max", m2[:], pr2[:], AX.X)
        p.v("tensor_scalar", cmask[:], probs[:], m2[:], None, ALU.is_ge)
        p.v("tensor_mul", t8[:], probs[:], cmask[:])
        p.v("tensor_copy", combbf[:], t8[:])
        for ex in range(NE):
            p.mm(ps_m[0:1, 0:128], combbf[:, ex:ex + 1], e.id128b[:])
            p.v("tensor_copy", combT[0:1, ex, ssl], ps_m[0:1, 0:128])
        for dc in range(nDC):
            p.mm(ps_r[:, NE:NE + 1], xt[:, dc, ssl], sge_sb[:, dc:dc + 1],
                 start=(dc == 0), stop=(dc == nDC - 1))
        p.act(sigbf[:], ps_r[:, NE:NE + 1], AF.Sigmoid)
        p.mm(ps_m[0:1, 0:128], sigbf[:], e.id128b[:])
        p.v("tensor_copy", sigrow[:, ssl], ps_m[0:1, 0:128])
    for ex in range(NE):
        for g2 in range(TPC // 512):
            gsl = slice(g2 * 512, (g2 + 1) * 512)
            p.mm(ps_m[:, 0:512], e.onerow_b[:], combT[0:1, ex, gsl])
            p.v("tensor_copy", comb_bc[:, ex, gsl], ps_m[:, 0:512])
    for g2 in range(TPC // 512):
        gsl = slice(g2 * 512, (g2 + 1) * 512)
        p.mm(ps_m[:, 0:512], e.onerow_b[:], sigrow[:, gsl])
        p.v("tensor_copy", sig_bc[:, gsl], ps_m[:, 0:512])
    for g2 in range(TPC // 512):
        for st in range(4):
            row0 = g2 * 512 + st * 128
            p.dma(macc2[g2][:, st, :], e.s_h[row0:row0 + 128, :])
    for g2 in range(TPC // 512):
        gsl = slice(g2 * 512, (g2 + 1) * 512)
        macc = macc2[g2]
        for jt in range(ISH // 128):
            par = jt % 2
            p.dma(sgw2[par][:, :, :], e.sg[jt], eng="gpsimd")
            p.dma(suw2[par][:, :, :], e.su[jt], eng="gpsimd")
            pg, pu = ps_g2[par], ps_u2[par]
            for dc in range(nDC):
                p.mm(pg[:], sgw2[par][:, dc, :], xt[:, dc, gsl],
                     start=(dc == 0), stop=(dc == nDC - 1))
            p.act(h_sh[:, jt, :], pg[:], AF.Silu)
            for dc in range(nDC):
                p.mm(pu[:], suw2[par][:, dc, :], xt[:, dc, gsl],
                     start=(dc == 0), stop=(dc == nDC - 1))
            p.v("tensor_tensor", h_sh[:, jt, :], pu[:], h_sh[:, jt, :], ALU.mult)
            p.g("tensor_tensor", h_sh[:, jt, :], h_sh[:, jt, :], sig_bc[:, gsl], ALU.mult)
        for jtg in range(8):
            par = jtg % 2
            p.dma(sdw2[par][:, :, :],
                  e.sd[jtg // 2, :, (jtg % 2) * 4:(jtg % 2) * 4 + 4, :], eng="gpsimd")
            for st in range(4):
                for dg in range(D // 512):
                    dsl = slice(dg * 512, (dg + 1) * 512)
                    pd = ps_d2[(st * 4 + dg) % 2]
                    for jc in range(4):
                        p.mm(pd[:], h_sh[:, jtg * 4 + jc, st * 128:(st + 1) * 128],
                             sdw2[par][:, jc, dsl], start=(jc == 0), stop=(jc == 3))
                    p.v("tensor_tensor", macc[:, st, dsl], pd[:], macc[:, st, dsl], ALU.add)
    for ex in range(NE):
        p.dma(wege[:, :, :], e.eg[ex], eng="gpsimd")
        p.dma(weue[:, :, :], e.eu[ex], eng="gpsimd")
        for g2 in range(TPC // 512):
            gsl = slice(g2 * 512, (g2 + 1) * 512)
            for it in range(IE // 128):
                pg = ps_g2[it % 2]
                for dc in range(nDC):
                    p.mm(pg[:], wege[:, dc, it * 128:(it + 1) * 128], xt[:, dc, gsl],
                         start=(dc == 0), stop=(dc == nDC - 1))
                p.act(h_e2[:, it, gsl], pg[:], AF.Silu)
                pu = ps_u2[it % 2]
                for dc in range(nDC):
                    p.mm(pu[:], weue[:, dc, it * 128:(it + 1) * 128],
                         xt[:, dc, gsl], start=(dc == 0), stop=(dc == nDC - 1))
                p.v("tensor_tensor", h_e2[:, it, gsl], pu[:], h_e2[:, it, gsl], ALU.mult)
                p.g("tensor_tensor", h_e2[:, it, gsl], h_e2[:, it, gsl],
                    comb_bc[:, ex, gsl], ALU.mult)
        p.dma(wede[:, :, :], e.ed[ex], eng="gpsimd")
        for g2 in range(TPC // 512):
            macc = macc2[g2]
            for st in range(4):
                sl2 = slice(g2 * 512 + st * 128, g2 * 512 + (st + 1) * 128)
                for dg in range(D // 512):
                    dsl = slice(dg * 512, (dg + 1) * 512)
                    pd = ps_d2[(st * 4 + dg) % 2]
                    for ic in range(IE // 128):
                        p.mm(pd[:], h_e2[:, ic, sl2], wede[:, ic, dsl],
                             start=(ic == 0), stop=(ic == IE // 128 - 1))
                    p.v("tensor_tensor", macc[:, st, dsl], pd[:], macc[:, st, dsl], ALU.add)
    for g2 in range(TPC // 512):
        for st in range(4):
            row0 = g2 * 512 + st * 128
            p.dma(e.s_h[row0:row0 + 128, :], macc2[g2][:, st, :])


def emit_mambain_f(p, e):
    nDC = D // 128
    xt = _carve(e.A, 0, (nDC, TPC))
    wbuf2 = [_carve(e.B, 0, (nDC, 128)), _carve(e.B, 6144, (nDC, 128))]
    ztile = _carve(e.B, 2048, (TPC,))
    xbc_sb = _carve(e.B, 3072, (TPC,))
    dtraw = _carve(e.C, 0, (TPC,))
    dtt = _carve(e.C, 1024, (TPC,))
    dta = _carve(e.C, 2048, (TPC,))
    dtb_sb = p.sbuf("mi_dtb", [32, 1], F32)
    ps2 = [e.ps_a, e.ps_b]
    p.dma(dtb_sb[:], e.dtb[:])
    for dc in range(nDC):
        p.dma(xt[:, dc, :], e.s_xln[0:TPC, dc * 128:(dc + 1) * 128], transpose=True)
    for ft in range(66):
        wbuf = wbuf2[ft % 2]
        p.dma(wbuf[:, :, :], e.win[ft], eng="gpsimd")
        if ft < 32:
            for g2 in range(TPC // 512):
                gsl = slice(g2 * 512, (g2 + 1) * 512)
                ps = ps2[g2]
                for dc in range(nDC):
                    p.mm(ps[:], wbuf[:, dc, :], xt[:, dc, gsl],
                         start=(dc == 0), stop=(dc == nDC - 1))
                p.act(ztile[:, gsl], ps[:], AF.Silu)
            p.dma(e.s_sz[ft * 128:(ft + 1) * 128, :], ztile[:, :])
        elif ft < 65:
            ct = ft - 32
            for g2 in range(TPC // 512):
                gsl = slice(g2 * 512, (g2 + 1) * 512)
                ps = ps2[g2]
                for dc in range(nDC):
                    p.mm(ps[:], wbuf[:, dc, :], xt[:, dc, gsl],
                         start=(dc == 0), stop=(dc == nDC - 1))
                p.v("tensor_copy", xbc_sb[:, gsl], ps[:])
            p.dma(e.s_xbraw[ct * 128:(ct + 1) * 128, :], xbc_sb[:, :])
            p.dma(e.s_xctr[ct * 128:(ct + 1) * 128, :], xbc_sb[:, TPC - 4:TPC])
        else:
            for g2 in range(TPC // 512):
                gsl = slice(g2 * 512, (g2 + 1) * 512)
                ps = ps2[g2]
                for dc in range(nDC):
                    p.mm(ps[:], wbuf[:, dc, :], xt[:, dc, gsl],
                         start=(dc == 0), stop=(dc == nDC - 1))
                p.v("tensor_copy", dtraw[0:32, gsl], ps[0:32, :])
            p.v("tensor_scalar", dtt[0:32, :], dtraw[0:32, :], dtb_sb[:], None, ALU.add)
            p.act(dta[0:32, :], dtt[0:32, :], AF.Abs)
            p.act(dta[0:32, :], dta[0:32, :], AF.Exp, scale=-1.0)
            p.v("tensor_scalar", dta[0:32, :], dta[0:32, :], 1.0, None, ALU.add)
            p.act(dta[0:32, :], dta[0:32, :], AF.Ln)
            p.v("tensor_scalar", dtt[0:32, :], dtt[0:32, :], 0.0, None, ALU.max)
            p.v("tensor_add", dtt[0:32, :], dtt[0:32, :], dta[0:32, :])
            p.dma(e.s_dt[:, :], dtt[0:32, :])


def emit_conv_f(p, e):
    nCT = CONV_DIM // 128
    ext = _carve(e.B, 0, (1028,))
    cvb = _carve(e.B, 2048, (TPC,))
    bnd = _carve(e.B, 3072, (4,))
    cv = _carve(e.C, 0, (TPC,))
    cw_sb = p.sbuf("cv_cw", [128, nCT, DCONV], F32)
    cb_sb = p.sbuf("cv_cb", [128, nCT], F32)
    p.dma(cw_sb[:], e.cw[:], eng="gpsimd")
    p.dma(cb_sb[:], e.cb[:], eng="gpsimd")
    for ct in range(nCT):
        sl = slice(ct * 128, (ct + 1) * 128)
        p.dma(bnd[:, :], e.s_xgat[sl, :])
        p.v("tensor_scalar", ext[:, 0:4], bnd[:, :], e.role[:], None, ALU.mult)
        p.dma(ext[:, 4:1028], e.s_xbraw[sl, :])
        p.v("tensor_scalar", cv[:, :], ext[:, 1:1 + TPC],
            cw_sb[:, ct, 0:1], cb_sb[:, ct:ct + 1], ALU.mult, ALU.add)
        for k2 in range(1, DCONV):
            p.v("scalar_tensor_tensor", cv[:, :], ext[:, 1 + k2:1 + k2 + TPC],
                cw_sb[:, ct, k2:k2 + 1], cv[:, :], ALU.mult, ALU.add)
        p.act(cvb[:, :], cv[:, :], AF.Silu)
        p.dma(e.s_xbcc[sl, :], cvb[:, :])




def emit_ssdA_f(p, e):
    nHD = 32
    nC2 = TPC // MCH
    xf = _carve(e.A, 0, (nHD, TPC))
    y_all = _carve(e.B, 0, (nHD, TPC))
    cum = _carve(e.C, 0, (TPC,))[0:nHD, :]
    negcum = _carve(e.C, 1024, (TPC,))[0:nHD, :]
    dec = _carve(e.C, 2048, (TPC,))[0:nHD, :]
    ecr = _carve(e.C, 3072, (TPC,))[0:nHD, :]
    dA = _carve(e.C, 4096, (TPC,))[0:nHD, :]
    ones_s = _carve(e.C, 5120, (TPC,))[0:nHD, :]
    dt_sb = _carve(e.C, 6144, (TPC,))[0:nHD, :]
    colbc = _carve(e.C, 7168, (128,))
    lrelu = _carve(e.C, 7296, (128,))
    colbc64 = _carve(e.C, 7424, (64,))
    carry = _carve(e.Dp, 0, (nHD, 128))[0:DS, :, :]
    cumT = _carve(e.Dp, 4096, (32,))
    negcumT = _carve(e.Dp, 4128, (32,))
    dtT = _carve(e.Dp, 4160, (32,))
    decT = _carve(e.Dp, 4192, (32,))
    ecrT_all = _carve(e.Dp, 4224, (nC2, 32))
    cdec = _carve(e.Dp, 4480, (nC2,))[0:nHD, :]
    zeros8 = _carve(e.Dp, 4488, (nC2,))[0:nHD, :]
    Gcp = _carve(e.Dp, 4496, (nC2,))[0:nHD, :]
    Gprev = _carve(e.Dp, 4504, (nC2,))[0:nHD, :]
    cdec_row = _carve(e.Dp, 4512, (32,))[0:1, :]
    cdec_bc = _carve(e.Dp, 4544, (32,))[0:DS, :]
    e.ssd_views = (xf, y_all, ecr, ecrT_all, Gprev, cum)
    lt = p.sbuf("sd_lt", [128, 128], BF16)
    phat = p.sbuf("sd_phat", [128, 128], BF16)
    xdt = p.sbuf("sd_xdt", [128, 128], BF16)
    sm = p.sbuf("sd_sm", [128, 128], BF16)
    btok = p.sbuf("sd_btok", [128, DS], BF16)
    bd = p.sbuf("sd_bd", [128, DS], BF16)
    chat = p.sbuf("sd_chat", [DS, 128], BF16)
    prevbf = p.sbuf("sd_prev", [DS, 128], BF16)
    B0 = p.sbuf("sd_B0", [DS, TPC], BF16)
    C0 = p.sbuf("sd_C0", [DS, TPC], BF16)
    e.ssd_named = (lt, phat, xdt, sm, btok, bd, chat, prevbf, B0, C0)
    alog_sb = p.sbuf("sd_alog", [32, 1], F32)
    acol = p.sbuf("sd_acol", [32, 1], F32)
    dvec_sb = p.sbuf("sd_dvec", [1, 32], F32)
    dcols = p.sbuf("sd_dcols", [128, 32], F32)
    ps_S = e.ps_big[:, 0:128]
    ps_tr = e.ps_big[:, 512:640]
    ps_st = e.ps_a[0:DS, 0:128]
    ps_y = e.ps_b[:, 0:128]
    ps_sm = e.ps_c[:, 0:32]
    ps_xt = e.ps_bf[:, 0:128]

    for h in range(nHD):
        p.dma(xf[:, h, :], e.s_xbcc[h * 128:(h + 1) * 128, :])
    p.dma(B0[:, :], e.s_xbcc[DI:DI + DS, :])
    p.dma(C0[:, :], e.s_xbcc[DI + DS:DI + 2 * DS, :])
    p.dma(dt_sb[:, :], e.s_dt[:, :])
    p.dma(alog_sb[:], e.alog[:])
    p.dma(dvec_sb[:], e.dvec[:])
    p.g("memset", ones_s[:, :], 1.0)
    p.g("memset", zeros8[:, :], 0.0)
    p.g("memset", carry[:, :, :], 0.0)
    p.act(acol[:], alog_sb[:], AF.Exp)
    p.s("mul", acol[:], acol[:], -1.0)
    p.v("tensor_scalar", dA[:, :], dt_sb[:, :], acol[:], None, ALU.mult)
    p.v("tensor_tensor_scan", cum[:, :], ones_s[:, :], dA[:, :], 0.0, ALU.mult, ALU.add)
    p.s("mul", negcum[:, :], cum[:, :], -1.0)
    p.mm(ps_sm[:, 0:32], e.onerow_f[:], dvec_sb[:])
    p.v("tensor_copy", dcols[:], ps_sm[:, 0:32])
    for c in range(nC2):
        csl = slice(c * MCH, (c + 1) * MCH)
        last = (c + 1) * MCH - 1
        if c == 0:
            p.act(ecr[:, csl], cum[:, csl], AF.Exp)
            p.act(cdec[:, 0:1], cum[:, last:last + 1], AF.Exp)
        else:
            prev = c * MCH - 1
            p.act(ecr[:, csl], cum[:, csl], AF.Exp, bias=negcum[:, prev:prev + 1])
            p.act(cdec[:, c:c + 1], cum[:, last:last + 1], AF.Exp,
                  bias=negcum[:, prev:prev + 1])
        p.act(dec[:, csl], negcum[:, csl], AF.Exp, bias=cum[:, last:last + 1])
    p.v("tensor_tensor_scan", Gcp[:, :], cdec[:, :], zeros8[:, :], 1.0, ALU.mult, ALU.add)
    p.g("memset", Gprev[:, 0:1], 1.0)
    p.v("tensor_copy", Gprev[:, 1:nC2], Gcp[:, 0:nC2 - 1])

    for c in range(nC2):
        csl = slice(c * MCH, (c + 1) * MCH)
        for (srcv, dstv) in ((cum, cumT), (dt_sb, dtT), (dec, decT)):
            p.trans(ps_sm[0:128, 0:32], srcv[:, csl], e.id32f[:])
            p.v("tensor_copy", dstv[:, :], ps_sm[0:128, 0:32])
        p.s("mul", negcumT[:, :], cumT[:, :], -1.0)
        p.trans(ps_sm[0:128, 0:32], ecr[:, csl], e.id32f[:])
        p.v("tensor_copy", ecrT_all[:, c, :], ps_sm[0:128, 0:32])
        p.trans(ps_xt[:, 0:DS], B0[:, csl], e.id64b[:])
        p.v("tensor_copy", btok[:], ps_xt[:, 0:DS])
        p.mm(ps_S[:, :], B0[:, csl], C0[:, csl])
        p.v("tensor_tensor", sm[:], ps_S[:, :], e.triub[:], ALU.mult)
        p.trans(ps_sm[0:1, 0:32], cdec[:, c:c + 1], e.id32f[:])
        p.v("tensor_copy", cdec_row[:, :], ps_sm[0:1, 0:32])
        p.mm(ps_sm[0:DS, 0:32], e.on64_f[:], cdec_row[:, :])
        p.v("tensor_copy", cdec_bc[:, :], ps_sm[0:DS, 0:32])
        for h in range(nHD):
            p.act(colbc[:, :], e.ones128t[:], AF.Identity, scale=negcumT[:, h:h + 1])
            p.trans(ps_tr[:, :], colbc[:, :], e.id128f[:])
            p.act(lrelu[:, :], ps_tr[:, :], AF.Relu, bias=cumT[:, h:h + 1])
            p.act(lt[:], lrelu[:, :], AF.Exp, scale=-1.0)
            p.g("tensor_tensor", phat[:], sm[:], lt[:], ALU.mult)
            p.trans(ps_xt[:, 0:128], xf[:, h, csl], e.id128b[:])
            p.v("tensor_scalar", xdt[:], ps_xt[:, 0:128], dtT[:, h:h + 1], None, ALU.mult)
            p.g("tensor_scalar", bd[:], btok[:], decT[:, h:h + 1], None, ALU.mult)
            p.mm(ps_st[:, :], bd[:], xdt[:])
            p.act(colbc64[:, :], e.ones128t[:, 0:64], AF.Identity,
                  scale=ecrT_all[:, c, h:h + 1])
            p.trans(ps_tr[0:DS, :], colbc64[:, :], e.id128f[:])
            p.v("tensor_tensor", chat[:], ps_tr[0:DS, 0:128], C0[:, csl], ALU.mult)
            p.v("tensor_copy", prevbf[:], carry[:, h, :])
            p.v("scalar_tensor_tensor", carry[:, h, :], carry[:, h, :],
                cdec_bc[:, h:h + 1], ps_st[:, :], ALU.mult, ALU.add)
            p.mm(ps_y[:, :], xdt[:], phat[:], start=True, stop=False)
            p.mm(ps_y[:, :], prevbf[:], chat[:], start=False, stop=True)
            p.v("scalar_tensor_tensor", y_all[:, h, csl], xf[:, h, csl],
                dcols[:, h:h + 1], ps_y[:, :], ALU.mult, ALU.add)
    cflat = _carve(e.Dp, 0, (nHD * 128,))[0:DS, :]
    p.dma(e.s_cctr.rearrange("(n h) pp -> n (h pp)", n=DS), cflat[:, :])


def emit_ssdB_f(p, e):
    nHD = 32
    nC2 = TPC // MCH
    (xf, y_all, ecr, ecrT_all, Gprev, cum) = e.ssd_views
    (lt, phat, xdt, sm, btok, bd, chat, prevbf, B0, C0) = e.ssd_named
    init = _carve(e.Dp, 0, (nHD, 128))[0:DS, :, :]
    initflat = _carve(e.Dp, 0, (nHD * 128,))[0:DS, :]
    g_row = _carve(e.Dp, 4512, (32,))[0:1, :]
    g_bc = _carve(e.Dp, 4544, (32,))[0:DS, :]
    colbc64 = _carve(e.C, 7424, (64,))
    initg = p.sbuf("sb_initg", [DS, 128], BF16)
    ps_tr = e.ps_big[:, 512:640]
    ps_y = e.ps_b[:, 0:128]
    ps_sm = e.ps_c[:, 0:32]
    p.dma(initflat[:, :], e.s_cgat[0:DS * 32, :].rearrange("(n h) pp -> n (h pp)", n=DS))
    p.v("tensor_scalar", initflat[:, :], initflat[:, :], e.role[0:DS, :], None, ALU.mult)
    for c in range(nC2):
        csl = slice(c * MCH, (c + 1) * MCH)
        p.trans(ps_sm[0:1, 0:32], Gprev[:, c:c + 1], e.id32f[:])
        p.v("tensor_copy", g_row[:, :], ps_sm[0:1, 0:32])
        p.mm(ps_sm[0:DS, 0:32], e.on64_f[:], g_row[:, :])
        p.v("tensor_copy", g_bc[:, :], ps_sm[0:DS, 0:32])
        for h in range(nHD):
            p.v("tensor_scalar", initg[:], init[:, h, :], g_bc[:, h:h + 1], None, ALU.mult)
            p.act(colbc64[:, :], e.ones128t[:, 0:64], AF.Identity,
                  scale=ecrT_all[:, c, h:h + 1])
            p.trans(ps_tr[0:DS, :], colbc64[:, :], e.id128f[:])
            p.v("tensor_tensor", chat[:], ps_tr[0:DS, 0:128], C0[:, csl], ALU.mult)
            p.mm(ps_y[:, :], initg[:], chat[:])
            p.v("tensor_tensor", y_all[:, h, csl], ps_y[:, :], y_all[:, h, csl], ALU.add)


def emit_mambaout_f(p, e):
    nCC = DI // 128
    y_all = _carve(e.B, 0, (nCC, TPC))
    wo8 = _carve(e.A, 0, (8, D))
    sz_t = _carve(e.A, 16384, (TPC,))
    sq = _carve(e.A, 17408, (TPC,))
    macc = _carve(e.C, 0, (4, D))
    rstd_bc = _carve(e.Dp, 0, (TPC,))
    nw = p.sbuf("mo2_nw", [128, nCC], F32)
    ssq = _carve(e.Dp, 1024, (TPC,))[0:1, :]
    rstdb = p.sbuf("mo2_rstdb", [1, TPC], BF16)
    ps_sa = e.ps_a[0:1, :]
    ps_sb2 = e.ps_b[0:1, :]
    ps_bc = e.ps_c
    ps_d = e.ps_big[:, 0:512]
    p.dma(nw[:], e.normw[:])
    for cc in range(nCC):
        p.dma(sz_t[:, :], e.s_sz[cc * 128:(cc + 1) * 128, :])
        p.g("tensor_tensor", y_all[:, cc, :], y_all[:, cc, :], sz_t[:, :], ALU.mult)
        p.act(sq[:, :], y_all[:, cc, :], AF.Square)
        for half, psx in ((0, ps_sa), (1, ps_sb2)):
            p.mm(psx[:, :], e.onecol_b[:], sq[:, half * 512:(half + 1) * 512],
                 start=(cc == 0), stop=(cc == nCC - 1))
    p.v("tensor_copy", ssq[:, 0:512], ps_sa[:, :])
    p.v("tensor_copy", ssq[:, 512:1024], ps_sb2[:, :])
    p.s("mul", ssq[:], ssq[:], 1.0 / DI)
    p.v("tensor_scalar", ssq[:], ssq[:], 1e-5, None, ALU.add)
    p.act(ssq[:], ssq[:], AF.Sqrt)
    p.v("reciprocal", ssq[:], ssq[:])
    p.v("tensor_copy", rstdb[:], ssq[:])
    for half in range(2):
        p.mm(ps_bc[:, :], e.onerow_b[:], rstdb[:, half * 512:(half + 1) * 512])
        p.v("tensor_copy", rstd_bc[:, half * 512:(half + 1) * 512], ps_bc[:, :])
    for cc in range(nCC):
        p.v("scalar_tensor_tensor", y_all[:, cc, :], y_all[:, cc, :], nw[:, cc:cc + 1],
            rstd_bc[:, :], ALU.mult, ALU.mult)
    for g2 in range(TPC // 512):
        for st in range(4):
            row0 = g2 * 512 + st * 128
            p.dma(macc[:, st, :], e.s_h[row0:row0 + 128, :])
        for ccg in range(4):
            p.dma(wo8[:, :, :], e.wout[ccg * 8:(ccg + 1) * 8].rearrange("c p d -> p c d"),
                  eng="gpsimd")
            for st in range(4):
                ssl = slice(g2 * 512 + st * 128, g2 * 512 + (st + 1) * 128)
                for dg in range(D // 512):
                    dsl = slice(dg * 512, (dg + 1) * 512)
                    for jc in range(8):
                        p.mm(ps_d[:, :], y_all[:, ccg * 8 + jc, ssl], wo8[:, jc, dsl],
                             start=(jc == 0), stop=(jc == 7))
                    p.v("tensor_tensor", macc[:, st, dsl], ps_d[:, :], macc[:, st, dsl], ALU.add)
        for st in range(4):
            row0 = g2 * 512 + st * 128
            p.dma(e.s_h[row0:row0 + 128, :], macc[:, st, :])


def emit_bitffn_f(p, e):
    nFT = HF // 128
    nDC = D // 128
    xt = _carve(e.A, 0, (nDC, TPC))
    qreg = [_carve(e.A, 16384, (nDC, 512)), _carve(e.A, 24576, (nDC, 512)),
            _carve(e.B, 23040, (nDC, 512))]
    q3reg = [_carve(e.A, 16384, (4, D)), _carve(e.A, 24576, (4, D))]
    hid = _carve(e.B, 0, (nFT, 512))
    sg1 = _carve(e.B, 22528, (512,))
    macc = _carve(e.C, 0, (4, D))
    ps12 = [e.ps_a, e.ps_big[:, 1024:1536], e.ps_b, e.ps_big[:, 1536:2048]]
    ps_d2 = [e.ps_c, e.ps_big[:, 0:512]]
    for dc in range(nDC):
        p.dma(xt[:, dc, :], e.s_xln[0:TPC, dc * 128:(dc + 1) * 128], transpose=True)
    q3blocks = [(b4 * 4, min(4, nFT - b4 * 4)) for b4 in range((nFT + 3) // 4)]
    for g2 in range(TPC // 512):
        gsl = slice(g2 * 512, (g2 + 1) * 512)
        for ftg in range(HF // 512):
            r1 = qreg[(2 * ftg) % 3]
            r2 = qreg[(2 * ftg + 1) % 3]
            for dc in range(nDC):
                p.dma(r1[:, dc, :], e.s_q1[dc * 128:(dc + 1) * 128,
                                           ftg * 512:(ftg + 1) * 512])
                p.dma(r2[:, dc, :], e.s_q2[dc * 128:(dc + 1) * 128,
                                           ftg * 512:(ftg + 1) * 512])
            for ftl in range(4):
                ft = ftg * 4 + ftl
                p1 = ps12[ftl % 4]
                p2 = ps12[(ftl + 2) % 4]
                for dc in range(nDC):
                    p.mm(p1[:], r1[:, dc, ftl * 128:(ftl + 1) * 128],
                         xt[:, dc, gsl], start=(dc == 0), stop=(dc == nDC - 1))
                p.act(sg1[:, :], p1[:], AF.Sigmoid)
                for dc in range(nDC):
                    p.mm(p2[:], r2[:, dc, ftl * 128:(ftl + 1) * 128],
                         xt[:, dc, gsl], start=(dc == 0), stop=(dc == nDC - 1))
                p.v("tensor_tensor", hid[:, ft, :], p2[:], sg1[:, :], ALU.mult)
        for st in range(4):
            row0 = g2 * 512 + st * 128
            p.dma(macc[:, st, :], e.s_h[row0:row0 + 128, :])
        for bi, (fc0, bw) in enumerate(q3blocks):
            q3b = q3reg[bi % 2]
            for j in range(bw):
                p.dma(q3b[:, j, :], e.s_q3[(fc0 + j) * 128:(fc0 + j + 1) * 128, :])
            for st in range(4):
                for dg in range(D // 512):
                    dsl = slice(dg * 512, (dg + 1) * 512)
                    pd = ps_d2[(st * 4 + dg) % 2]
                    for jc in range(bw):
                        p.mm(pd[:], hid[:, fc0 + jc, st * 128:(st + 1) * 128],
                             q3b[:, jc, dsl], start=(jc == 0), stop=(jc == bw - 1))
                    p.v("tensor_tensor", macc[:, st, dsl], pd[:], macc[:, st, dsl], ALU.add)
        for st in range(4):
            row0 = g2 * 512 + st * 128
            p.dma(e.out[row0:row0 + 128, :], macc[:, st, :])


def run_fused(ii, dbg=(), stop_after=None):
    nDC = D // 128
    x = ii["L0_attn_x"]
    wq_t, wk_t, wv_t, wo_t = prep_attn_weights(
        ii["L0_attn_W_Q"], ii["L0_attn_W_K"], ii["L0_attn_W_V"], ii["L0_attn_W_O"])
    wo_l = np.ascontiguousarray(wo_t.transpose(1, 0, 2))
    def lay_qkv(w, hs):
        return np.ascontiguousarray(
            w[hs].reshape(HPC, nDC, 128, EH).transpose(0, 2, 1, 3))
    moe_w = prep_moe_weights(
        ii["L0_ffn_gate_w"], ii["L0_ffn_expert_gate_w"], ii["L0_ffn_expert_up_w"],
        ii["L0_ffn_expert_down_w"], ii["L0_ffn_shared_gate_w"], ii["L0_ffn_shared_up_w"],
        ii["L0_ffn_shared_down_w"], ii["L0_ffn_shared_expert_gate_w"])
    gw_t, sge_t, eg_t, eu_t, ed_t, sg_t, su_t, sd_t = moe_w
    W_in = ii["L1_attn_W_in"]
    win_pad = np.zeros((66 * 128, D), dtype=np.float32)
    win_pad[:DIP] = W_in
    win_l = np.ascontiguousarray(win_pad.reshape(66, 128, nDC, 128).transpose(0, 3, 2, 1))
    cw_l = np.ascontiguousarray(ii["L1_attn_conv_w"].reshape(CONV_DIM // 128, 128, DCONV)
                                .transpose(1, 0, 2))
    cb_l = np.ascontiguousarray(ii["L1_attn_conv_b"].reshape(CONV_DIM // 128, 128).T)
    dtb_l = np.ascontiguousarray(ii["L1_attn_dt_bias"].reshape(32, 1))
    alog_l = np.ascontiguousarray(ii["L1_attn_A_log"].reshape(32, 1))
    dvec_l = np.ascontiguousarray(ii["L1_attn_D"].reshape(1, 32))
    wout_l = np.ascontiguousarray(ii["L1_attn_W_out"].T.reshape(DI // 128, 128, D))
    nw_l = np.ascontiguousarray(ii["L1_attn_norm_w"].reshape(DI // 128, 128).T)
    w1t = np.ascontiguousarray(ii["L1_ffn_w1"].T)
    w2t = np.ascontiguousarray(ii["L1_ffn_w2"].T)
    w3t = np.ascontiguousarray(ii["L1_ffn_w3"].T)

    nc = build_fused(dbg=dbg, stop_after=stop_after)
    in_maps = []
    for c in range(NCORES):
        b, role = c // 2, c % 2
        hs = slice(role * HPC, (role + 1) * HPC)
        rm = np.full((128, 1), float(role), dtype=np.float32)
        in_maps.append({
            "xb": np.ascontiguousarray(x[b]),
            "xrows": np.ascontiguousarray(x[b, role * TPC:(role + 1) * TPC]),
            "rolemask": rm,
            "wq": lay_qkv(wq_t, hs), "wk": lay_qkv(wk_t, hs), "wv": lay_qkv(wv_t, hs),
            "wo": wo_l,
            "gw": gw_t, "sge": sge_t, "eg": eg_t, "eu": eu_t, "ed": ed_t,
            "sg": sg_t, "su": su_t, "sd": sd_t,
            "win": win_l, "cw": cw_l, "cb": cb_l, "dtb": dtb_l,
            "alog": alog_l, "dvec": dvec_l, "wout": wout_l, "normw": nw_l,
            "w1t": w1t, "w2t": w2t, "w3t": w3t,
        })
    res = _run(nc, in_maps)
    out = np.zeros((B, S, D), dtype=np.float32)
    for c in range(NCORES):
        b, role = c // 2, c % 2
        out[b, role * TPC:(role + 1) * TPC] = res[c]["out"]
    return out, res


def kernel(**inputs):
    ii = {k: np.asarray(v) for k, v in inputs.items()}
    out, _ = run_fused(ii)
    return np.ascontiguousarray(out).astype(np.float32)


def kernel_unfused(**inputs):

    ii = {k: np.asarray(v) for k, v in inputs.items()}
    x = ii["L0_attn_x"]
    xflat = np.ascontiguousarray(x.reshape(T, D))

    wq_t, wk_t, wv_t, wo_t = prep_attn_weights(
        ii["L0_attn_W_Q"], ii["L0_attn_W_K"], ii["L0_attn_W_V"], ii["L0_attn_W_O"])
    xln = run_ln(xflat)
    q, k, v = run_qkv(xln.reshape(B, S, D), wq_t, wk_t, wv_t)
    z = run_attnz(q, k, v)
    h1 = run_attnout(z, wo_t, xflat)

    moe_w = prep_moe_weights(
        ii["L0_ffn_gate_w"], ii["L0_ffn_expert_gate_w"], ii["L0_ffn_expert_up_w"],
        ii["L0_ffn_expert_down_w"], ii["L0_ffn_shared_gate_w"], ii["L0_ffn_shared_up_w"],
        ii["L0_ffn_shared_down_w"], ii["L0_ffn_shared_expert_gate_w"])
    x1ln = run_ln(h1)
    h2 = run_moe(x1ln, h1, *moe_w)

    x2ln = run_ln(h2)
    sz, xbcc, dt = run_mambain(x2ln, ii["L1_attn_conv_w"], ii["L1_attn_conv_b"],
                               ii["L1_attn_dt_bias"], ii["L1_attn_W_in"])
    y = run_ssd(xbcc, dt, ii["L1_attn_A_log"], ii["L1_attn_D"])
    h3 = run_mambaout(y, sz, ii["L1_attn_W_out"], ii["L1_attn_norm_w"], h2)

    q1t, q2t, q3t = run_quantize(ii["L1_ffn_w1"], ii["L1_ffn_w2"], ii["L1_ffn_w3"])
    x3ln = run_ln(h3)
    out = run_bitffn(x3ln, h3, q1t, q2t, q3t)
    return np.ascontiguousarray(out.reshape(B, S, D)).astype(np.float32)


# version 08
# speedup vs baseline: 1.7142x; 1.7142x over previous
import math
from contextlib import ExitStack

import numpy as np
import ml_dtypes
import concourse.bass as bass
import concourse.mybir as mybir
from concourse.bass_utils import run_bass_kernel_spmd

F32 = mybir.dt.float32
BF16 = mybir.dt.bfloat16
AF = mybir.ActivationFunctionType
ALU = mybir.AluOpType
AX = mybir.AxisListType
BF = ml_dtypes.bfloat16

NCORES = 8
B, S, D = 4, 2048, 2048
T = B * S
TPC = T // NCORES
NH, EH = 16, 128
HPC = NH // 2
SCALE = 1.0 / math.sqrt(EH)
NE, IE, ISH = 8, 1024, 4096
DI, MNH, DS, DCONV = 4096, 32, 64, 4
CONV_DIM = DI + 2 * DS
DIP = 2 * DI + 2 * DS + MNH
MCH = 128
HF = 5632

_eye_bf = np.eye(128, dtype=BF)
_eye_f32 = np.eye(128, dtype=np.float32)
_tril_bf = np.tril(np.ones((128, 128), dtype=np.float32)).astype(BF)
_triu_bf = np.triu(np.ones((128, 128), dtype=np.float32)).astype(BF)
_ones_col_bf = np.ones((128, 1), dtype=BF)
_ones_row_bf = np.ones((1, 128), dtype=BF)
_ones_row_f32 = np.ones((1, 128), dtype=np.float32)


class Prog:
    ENGS = ("tensor", "vector", "scalar", "gpsimd", "sync")
    DMA_POOL = 40

    def __init__(self):
        self.nc = bass.Bass("TRN2", target_bir_lowering=False, debug=False)
        self.steps = []
        self.counts = {}
        self.ndma = {"s": 0, "g": 0}
        self.ncoll = 0
        self.ctx = ExitStack()
        self._names = {}
        self._track = {}

    def _nm(self, base):
        n = self._names.get(base, 0)
        self._names[base] = n + 1
        return f"{base}{n}" if n else base

    def sbuf(self, name, shape, dtype):
        return self.ctx.enter_context(self.nc.sbuf_tensor(self._nm("sb_" + name), shape, dtype))

    def psum(self, name, shape, dtype=F32):
        return self.ctx.enter_context(self.nc.psum_tensor(self._nm("ps_" + name), shape, dtype))

    def inp(self, name, shape, dtype=F32):
        return self.nc.dram_tensor(name, list(shape), dtype, kind="ExternalInput").ap()

    def outp(self, name, shape, dtype=F32):
        return self.nc.dram_tensor(name, list(shape), dtype, kind="ExternalOutput").ap()

    def const(self, name, arr):
        return self.nc.inline_tensor(np.ascontiguousarray(arr), self._nm(name)).ap()

    @staticmethod
    def _iv(ap):
        t = ap.tensor
        name = t.name
        dims = list(ap.ap)
        off = int(ap.offset)
        space = str(getattr(ap, "space", "")).upper()
        is_dram = "DRAM" in space or "HBM" in space
        if is_dram or len(dims) == 0:
            span = 1
            for st, ct in dims:
                span += abs(int(st)) * (int(ct) - 1)
            return (name, off, off + span, 0, 129)
        bp = ap.base_partition
        bp = int(bp() if callable(bp) else bp)
        pc = int(dims[0][1]) if dims else 128
        span = 1
        for st, ct in dims[1:]:
            span += abs(int(st)) * (int(ct) - 1)
        lo, hi = off, off + span
        if "PSUM" in space:
            esz = mybir.dt.size(ap.dtype)
            bank = 2048 // esz
            lo = (lo // bank) * bank
            hi = ((hi + bank - 1) // bank) * bank
            return (name, lo, hi, 0, 129)
        return (name, lo, hi, bp, bp + pc)

    def _dep_read(self, iv, deps):
        name, lo, hi, plo, phi = iv
        for ent in self._track.get(name, ()):
            if ent[0] == "w" and ent[1] < hi and lo < ent[2] and ent[3] < phi and plo < ent[4]:
                deps.add(ent[5])
        self._track.setdefault(name, []).append(("r", lo, hi, plo, phi, len(self.steps)))

    def _dep_write(self, iv, deps):
        name, lo, hi, plo, phi = iv
        lst = self._track.setdefault(name, [])
        keep = []
        for ent in lst:
            if ent[1] < hi and lo < ent[2] and ent[3] < phi and plo < ent[4]:
                deps.add(ent[5])
                if lo <= ent[1] and ent[2] <= hi and plo <= ent[3] and ent[4] <= phi:
                    continue
            keep.append(ent)
        keep.append(("w", lo, hi, plo, phi, len(self.steps)))
        self._track[name] = keep

    def _step(self, eng, fn, reads, writes, semkey, inc, extra_waits=None):
        deps = set()
        for ap in reads:
            if ap is not None and hasattr(ap, "tensor"):
                self._dep_read(self._iv(ap), deps)
        for ap in writes:
            if ap is not None and hasattr(ap, "tensor"):
                self._dep_write(self._iv(ap), deps)
        deps.discard(len(self.steps))
        waits = dict(extra_waits or {})
        for d in deps:
            deng, dkey, dval = self.steps[d][0], self.steps[d][2], self.steps[d][4]
            if deng == eng == "tensor" and dkey == ("e", "tensor"):
                continue
            waits[dkey] = max(waits.get(dkey, 0), dval)
        cur = self.counts.get(semkey, 0) + inc
        self.counts[semkey] = cur
        self.steps.append((eng, fn, semkey, inc, cur, waits))

    def step(self, eng, fn, reads=(), writes=()):
        self._step(eng, fn, reads, writes, ("e", eng), 1)

    def dma(self, out, in_, eng="sync", **kw):
        pool = "g" if eng == "gpsimd" else "s"
        idx = self.ndma[pool] % self.DMA_POOL
        self.ndma[pool] += 1
        key = ("d" + pool, idx)
        prevgen = self.counts.get(key, 0)
        extra = {key: prevgen} if prevgen else None
        self._step(eng, (lambda e, o=out, i=in_, k=kw: e.dma_start(out=o, in_=i, **k)),
                   [in_], [out], key, 16, extra)

    def mm(self, out, lhsT, rhs, start=True, stop=True):
        self.step("tensor", (lambda e, o=out, l=lhsT, r=rhs, st=start, sp=stop:
                             e.matmul(o, l, r, start=st, stop=sp)),
                  reads=[lhsT, rhs], writes=[out])

    def trans(self, out, in_, ident):
        self.step("tensor", (lambda e, o=out, i=in_, d=ident: e.transpose(o, i, d)),
                  reads=[in_, ident], writes=[out])

    def act(self, out, in_, func, bias=0.0, scale=1.0, accum_out=None):
        reads = [in_]
        if hasattr(bias, "tensor"):
            reads.append(bias)
        writes = [out]
        if accum_out is not None:
            writes.append(accum_out)
        self.step("scalar", (lambda e, o=out, i=in_, f=func, b=bias, s=scale, a=accum_out:
                             e.activation(out=o, in_=i, func=f, bias=b, scale=s, accum_out=a)),
                  reads=reads, writes=writes)

    _VOP = {
        "tensor_copy": (1, 1), "tensor_tensor": (1, 2), "tensor_add": (1, 2),
        "tensor_sub": (1, 2), "tensor_mul": (1, 2), "tensor_max": (1, 2),
        "tensor_scalar": (1, 3), "scalar_tensor_tensor": (1, 3),
        "tensor_tensor_scan": (1, 3), "reduce_sum": (1, 1), "reduce_max": (1, 1),
        "tensor_reduce": (1, 1), "reciprocal": (1, 1), "memset": (1, 0),
        "select": (1, 3), "mul": (1, 2), "add": (1, 2), "copy": (1, 1),
        "iota": (1, 0), "transpose": (1, 1),
    }

    def _vecop(self, eng, meth, *args, **kw):
        nw, nr = self._VOP[meth]
        writes = [a for a in args[0:nw] if hasattr(a, "tensor")]
        reads = [a for a in args[nw:nw + nr] if hasattr(a, "tensor")]
        if kw.get("accum_out") is not None:
            writes.append(kw["accum_out"])
        self.step(eng, (lambda e, m=meth, a=args, k=kw: getattr(e, m)(*a, **k)),
                  reads=reads, writes=writes)

    def v(self, meth, *args, **kw):
        self._vecop("vector", meth, *args, **kw)

    def g(self, meth, *args, **kw):
        self._vecop("gpsimd", meth, *args, **kw)

    def s(self, meth, *args, **kw):
        self._vecop("scalar", meth, *args, **kw)

    def coll(self, fn, reads=(), writes=()):
        key = ("c", self.ncoll)
        self.ncoll += 1
        self._step("gpsimd", fn, reads, writes, key, 1)

    def finish(self):
        refs = {}
        for (eng, fn, semkey, inc, val, waits) in self.steps:
            for k, v in waits.items():
                refs.setdefault(k, set()).add(v)
        for k, tot in self.counts.items():
            refs.setdefault(k, set()).add(tot)
        remap = {}
        newtot = {}
        for k in self.counts:
            if k[0] == "e":
                rv = sorted(refs.get(k, ()))
                m = {}
                cnt = 0
                it = iter(rv)
                nxt = next(it, None)
                ranks = {}
                r = 0
                for v in rv:
                    r += 1
                    ranks[v] = r
                remap[k] = ranks
                newtot[k] = len(rv)
            else:
                remap[k] = None
                newtot[k] = self.counts[k]
        semkeys = sorted(self.counts.keys(), key=str)
        sems = {k: self.ctx.enter_context(
            self.nc.semaphore(f"S_{k[0]}_{k[1]}")) for k in semkeys}
        block = self.ctx.enter_context(self.nc.Block())
        per = {}
        for (eng, fn, semkey, inc, val, waits) in self.steps:
            per.setdefault(eng, []).append((fn, semkey, inc, val, waits))
        for eng, lst in per.items():
            sec = getattr(block, eng)

            def make_body(lst=lst):
                def body(e):
                    prev = {}
                    for fn, semkey, inc, val, waits in lst:
                        for wkey in sorted(waits.keys(), key=str):
                            wv = waits[wkey]
                            if remap[wkey] is not None:
                                wv = remap[wkey][wv]
                            if wv > prev.get(wkey, 0):
                                e.wait_ge(sems[wkey], wv)
                                prev[wkey] = wv
                        if semkey[0] == "c":
                            fn(e).then_inc(sems[semkey])
                        elif remap[semkey] is not None:
                            if val in remap[semkey]:
                                fn(e).then_inc(sems[semkey], 1)
                            else:
                                fn(e)
                        else:
                            fn(e).then_inc(sems[semkey], inc)
                return body
            sec(make_body())

        def final_wait(e):
            for k in semkeys:
                e.wait_ge(sems[k], newtot[k])
        block.sync(final_wait)
        self.ctx.close()
        return self.nc


def _run(nc, in_maps):
    res = run_bass_kernel_spmd(nc, in_maps, core_ids=list(range(NCORES)))
    return res.results


def build_ln(rows=TPC, d=D):
    p = Prog()
    h = p.inp("h", (rows, d))
    xln = p.outp("xln", (rows, d), BF16)
    x_sb = p.sbuf("x", [128, d], F32)
    sq_sb = p.sbuf("sq", [128, d], F32)
    o_sb = p.sbuf("o", [128, d], BF16)
    rs = p.sbuf("rs", [128, 1], F32)
    mean = p.sbuf("mean", [128, 1], F32)
    ssq = p.sbuf("ssq", [128, 1], F32)
    ex2 = p.sbuf("ex2", [128, 1], F32)
    m2 = p.sbuf("m2", [128, 1], F32)
    var = p.sbuf("var", [128, 1], F32)
    std = p.sbuf("std", [128, 1], F32)
    rstd = p.sbuf("rstd", [128, 1], F32)
    for i in range(rows // 128):
        sl = slice(i * 128, (i + 1) * 128)
        p.dma(x_sb[:], h[sl, :])
        p.v("reduce_sum", rs[:], x_sb[:], AX.X)
        p.s("mul", mean[:], rs[:], 1.0 / d)
        p.act(sq_sb[:], x_sb[:], AF.Square, accum_out=ssq[:])
        p.s("mul", ex2[:], ssq[:], 1.0 / d)
        p.v("tensor_mul", m2[:], mean[:], mean[:])
        p.v("tensor_sub", var[:], ex2[:], m2[:])
        p.v("tensor_scalar", var[:], var[:], 1e-5, None, ALU.add)
        p.act(std[:], var[:], AF.Sqrt)
        p.v("reciprocal", rstd[:], std[:])
        p.v("tensor_scalar", o_sb[:], x_sb[:], mean[:], rstd[:], ALU.subtract, ALU.mult)
        p.dma(xln[sl, :], o_sb[:])
    return p.finish()


def run_ln(h_full):
    nc = build_ln()
    in_maps = [{"h": np.ascontiguousarray(h_full[c * TPC:(c + 1) * TPC])}
               for c in range(NCORES)]
    res = _run(nc, in_maps)
    return np.concatenate([r["xln"] for r in res], axis=0)


def build_qkv():
    p = Prog()
    xln = p.inp("xln", (S, D), BF16)
    wq = p.inp("wq", (HPC, 128, D // 128, EH))
    wk = p.inp("wk", (HPC, 128, D // 128, EH))
    wv = p.inp("wv", (HPC, 128, D // 128, EH))
    q_o = p.outp("q", (HPC, EH, S), BF16)
    k_o = p.outp("k", (HPC, EH, S), BF16)
    v_o = p.outp("v", (HPC, S, EH), BF16)

    nDC = D // 128
    xt = p.sbuf("xt", [128, nDC, S], BF16)
    w_sb = {n: p.sbuf(f"w_{n}", [128, nDC, EH], BF16) for n in "qkv"}
    qk_sb = p.sbuf("qk", [128, S], BF16)
    v_sb = p.sbuf("v", [128, S // 128, EH], BF16)
    ps = p.psum("ps", [128, 512], F32)
    ps_v = p.psum("psv", [128, EH], F32)

    for dc in range(nDC):
        p.dma(xt[:, dc, :], xln[:, dc * 128:(dc + 1) * 128], transpose=True)
    for j in range(HPC):
        for n, w in (("q", wq), ("k", wk), ("v", wv)):
            p.dma(w_sb[n][:], w[j], eng="gpsimd")
        for n, o in (("q", q_o), ("k", k_o)):
            for sg in range(S // 512):
                for dc in range(nDC):
                    p.mm(ps[:], w_sb[n][:, dc, :], xt[:, dc, sg * 512:(sg + 1) * 512],
                         start=(dc == 0), stop=(dc == nDC - 1))
                p.v("tensor_copy", qk_sb[:, sg * 512:(sg + 1) * 512], ps[:])
            p.dma(o[j], qk_sb[:])
        for st in range(S // 128):
            for dc in range(nDC):
                p.mm(ps_v[:], xt[:, dc, st * 128:(st + 1) * 128], w_sb["v"][:, dc, :],
                     start=(dc == 0), stop=(dc == nDC - 1))
            p.v("tensor_copy", v_sb[:, st, :], ps_v[:])
        p.dma(v_o[j].rearrange("(t p) e -> p t e", p=128), v_sb[:])
    return p.finish()


def run_qkv(xln, wq_t, wk_t, wv_t):
    nc = build_qkv()
    in_maps = []
    for c in range(NCORES):
        b, hh = c // 2, c % 2
        hs = slice(hh * HPC, (hh + 1) * HPC)
        def lay(w):
            return np.ascontiguousarray(
                w[hs].reshape(HPC, D // 128, 128, EH).transpose(0, 2, 1, 3))
        in_maps.append({
            "xln": np.ascontiguousarray(xln[b]),
            "wq": lay(wq_t), "wk": lay(wk_t), "wv": lay(wv_t),
        })
    res = _run(nc, in_maps)
    q = np.stack([np.concatenate([res[2 * b]["q"], res[2 * b + 1]["q"]]) for b in range(B)])
    k = np.stack([np.concatenate([res[2 * b]["k"], res[2 * b + 1]["k"]]) for b in range(B)])
    v = np.stack([np.concatenate([res[2 * b]["v"], res[2 * b + 1]["v"]]) for b in range(B)])
    return q, k, v


def build_attnz():
    p = Prog()
    q_i = p.inp("q", (HPC, EH, S), BF16)
    k_i = p.inp("k", (HPC, EH, S), BF16)
    v_i = p.inp("v", (HPC, 128, S // 128, EH), BF16)
    z_o = p.outp("z", (HPC, EH, S), BF16)

    ident_bf = p.const("ident_bf", _eye_bf)
    id11_f32 = p.const("id11_f32", np.ones((1, 1), dtype=np.float32))
    tril = p.const("tril", _tril_bf)
    ones_col = p.const("ones_col", _ones_col_bf)

    nST = S // 128
    q_sb = p.sbuf("q", [128, S], BF16)
    k_sb = p.sbuf("k", [128, S], BF16)
    v_sb = p.sbuf("v", [128, nST, EH], BF16)
    sq_sb = p.sbuf("sqs", [128, S], BF16)
    id_bf_sb = p.sbuf("idbf", [128, 128], BF16)
    id11_sb = p.sbuf("id11", [1, 1], F32)
    tril_sb = p.sbuf("tril", [128, 128], BF16)
    ones_sb = p.sbuf("ones", [128, 1], BF16)
    r2_sb = p.sbuf("r2", [1, S], F32)
    k2_sb = p.sbuf("k2", [1, S], F32)
    km = p.sbuf("km", [1, 1], F32)
    negb = p.sbuf("negb", [128, nST], F32)
    P_sb = p.sbuf("P", [128, S], BF16)
    Pd_sb = p.sbuf("Pd", [128, 128], BF16)
    pt_sb = p.sbuf("pt", [128, 4, 128], BF16)
    sums = p.sbuf("sums", [128, 4], F32)
    sumtot = p.sbuf("sumtot", [128, 1], F32)
    rsum = p.sbuf("rsum", [128, 1], F32)
    diag_sb = p.sbuf("diag", [128, 128], BF16)
    z_sb = p.sbuf("z", [128, S], BF16)

    ps_s = p.psum("ps_s", [128, S], F32)
    ps_p = p.psum("ps_p", [128, 512], F32)
    ps_z = p.psum("ps_z", [128, 128], F32)
    ps_row = p.psum("ps_row", [1, 512], F32)
    ps_t = p.psum("ps_t", [128, 1], F32)

    p.dma(id_bf_sb[:], ident_bf[:])
    p.dma(id11_sb[:], id11_f32[:])
    p.dma(tril_sb[:], tril[:])
    p.dma(ones_sb[:], ones_col[:])

    for j in range(HPC):
        p.dma(q_sb[:], q_i[j])
        p.dma(k_sb[:], k_i[j])
        p.dma(v_sb[:], v_i[j])
        for (src, dst) in ((q_sb, r2_sb), (k_sb, k2_sb)):
            p.act(sq_sb[:], src[:], AF.Square)
            for sg in range(S // 512):
                p.mm(ps_row[:], ones_sb[:], sq_sb[:, sg * 512:(sg + 1) * 512])
                p.v("tensor_copy", dst[:, sg * 512:(sg + 1) * 512], ps_row[:])
        p.v("reduce_max", km[:], k2_sb[:], AX.X)
        p.v("tensor_scalar", r2_sb[:], r2_sb[:], km[:], None, ALU.mult)
        p.act(r2_sb[:], r2_sb[:], AF.Sqrt, scale=1.0 / EH)
        p.s("mul", r2_sb[:], r2_sb[:], -1.0)
        for st in range(nST):
            p.trans(ps_t[:], r2_sb[:, st * 128:(st + 1) * 128], id11_sb[:])
            p.v("tensor_copy", negb[:, st:st + 1], ps_t[:])
        for i in range(nST):
            nm = (i + 1) * 128
            ng = (nm + 511) // 512
            lastw = nm - (ng - 1) * 512
            for g in range(ng):
                w = min(512, nm - g * 512)
                p.mm(ps_s[:, g * 512:g * 512 + w], q_sb[:, i * 128:(i + 1) * 128],
                     k_sb[:, g * 512:g * 512 + w])
            for g in range(ng - 1):
                p.act(P_sb[:, g * 512:(g + 1) * 512], ps_s[:, g * 512:(g + 1) * 512],
                      AF.Exp, bias=negb[:, i:i + 1], scale=SCALE,
                      accum_out=sums[:, g:g + 1])
            off = (ng - 1) * 512
            p.act(P_sb[:, off:off + lastw], ps_s[:, off:off + lastw],
                  AF.Exp, bias=negb[:, i:i + 1], scale=SCALE)
            p.g("tensor_tensor", Pd_sb[:], P_sb[:, nm - 128:nm], tril_sb[:], ALU.mult)
            p.v("tensor_copy", P_sb[:, nm - 128:nm], Pd_sb[:])
            p.v("reduce_sum", sums[:, ng - 1:ng], P_sb[:, off:off + lastw], AX.X)
            p.v("reduce_sum", sumtot[:], sums[:, 0:ng], AX.X)
            p.v("reciprocal", rsum[:], sumtot[:])
            p.v("tensor_scalar", diag_sb[:], id_bf_sb[:], rsum[:], None, ALU.mult)
            for j4 in range((i + 1 + 3) // 4):
                cnt = min(4, i + 1 - j4 * 4)
                for u in range(cnt):
                    m = j4 * 4 + u
                    p.mm(ps_p[:, u * 128:(u + 1) * 128],
                         P_sb[:, m * 128:(m + 1) * 128], diag_sb[:])
                p.v("tensor_copy", pt_sb[:, 0:cnt, :], ps_p[:, 0:cnt * 128])
                for u in range(cnt):
                    m = j4 * 4 + u
                    p.mm(ps_z[:], v_sb[:, m, :], pt_sb[:, u, :],
                         start=(m == 0), stop=(m == i))
            p.v("tensor_copy", z_sb[:, i * 128:(i + 1) * 128], ps_z[:])
        p.dma(z_o[j], z_sb[:])
    return p.finish()


def run_attnz(q, k, v):
    nc = build_attnz()
    in_maps = []
    for c in range(NCORES):
        b, hh = c // 2, c % 2
        hs = slice(hh * HPC, (hh + 1) * HPC)
        in_maps.append({
            "q": np.ascontiguousarray(q[b, hs]),
            "k": np.ascontiguousarray(k[b, hs]),
            "v": np.ascontiguousarray(
                v[b, hs].reshape(HPC, S // 128, 128, EH).transpose(0, 2, 1, 3)),
        })
    res = _run(nc, in_maps)
    return np.stack([np.concatenate([res[2 * b]["z"], res[2 * b + 1]["z"]]) for b in range(B)])


def build_attnout():
    p = Prog()
    z_i = p.inp("z", (EH, NH, TPC), BF16)
    wo = p.inp("wo", (EH, NH, D))
    x_i = p.inp("x", (TPC, D))
    h1_o = p.outp("h1", (TPC, D))

    z_sb = p.sbuf("z", [128, NH, TPC], BF16)
    wo_sb = p.sbuf("wo", [128, NH, D], BF16)
    x_sb = p.sbuf("x", [128, D], F32)
    o_sb = p.sbuf("o", [128, D], F32)
    ps = p.psum("ps", [128, 512], F32)

    p.dma(z_sb[:], z_i[:])
    p.dma(wo_sb[:], wo[:], eng="gpsimd")
    for st in range(TPC // 128):
        p.dma(x_sb[:], x_i[st * 128:(st + 1) * 128, :])
        for dg in range(D // 512):
            for h in range(NH):
                p.mm(ps[:], z_sb[:, h, st * 128:(st + 1) * 128],
                     wo_sb[:, h, dg * 512:(dg + 1) * 512],
                     start=(h == 0), stop=(h == NH - 1))
            p.v("tensor_add", o_sb[:, dg * 512:(dg + 1) * 512], ps[:],
                x_sb[:, dg * 512:(dg + 1) * 512])
        p.dma(h1_o[st * 128:(st + 1) * 128, :], o_sb[:])
    return p.finish()


def run_attnout(z, wo_t, x_full):
    nc = build_attnout()
    wo_l = np.ascontiguousarray(wo_t.transpose(1, 0, 2))
    in_maps = []
    for c in range(NCORES):
        r0 = c * TPC
        b, s0 = r0 // S, r0 % S
        in_maps.append({
            "z": np.ascontiguousarray(z[b, :, :, s0:s0 + TPC].transpose(1, 0, 2)),
            "wo": wo_l,
            "x": np.ascontiguousarray(x_full[r0:r0 + TPC]),
        })
    res = _run(nc, in_maps)
    return np.concatenate([r["h1"] for r in res], axis=0)


def prep_attn_weights(W_Q, W_K, W_V, W_O):
    wq_t = np.ascontiguousarray(W_Q.transpose(0, 2, 1))
    wk_t = np.ascontiguousarray(W_K.transpose(0, 2, 1))
    wv_t = np.ascontiguousarray(W_V.transpose(0, 2, 1))
    wo_t = np.ascontiguousarray(W_O.transpose(0, 2, 1))
    return wq_t, wk_t, wv_t, wo_t




def build_moe():
    p = Prog()
    xln = p.inp("xln", (TPC, D), BF16)
    h1_i = p.inp("h1", (TPC, D))
    gw = p.inp("gw", (128, D // 128, NE))
    sge = p.inp("sge", (128, D // 128))
    eg = p.inp("eg", (NE, 128, D // 128, IE))
    eu = p.inp("eu", (NE, 128, D // 128, IE))
    ed = p.inp("ed", (NE, 128, IE // 128, D))
    sg = p.inp("sg", (ISH // 128, 128, D // 128, 128))
    su = p.inp("su", (ISH // 128, 128, D // 128, 128))
    sd = p.inp("sd", (4, 128, 8, D))
    h2_o = p.outp("h2", (TPC, D))

    ident_bf = p.const("ident_bf", _eye_bf)
    ones_row = p.const("ones_row", _ones_row_bf)

    nDC = D // 128
    nST = TPC // 128
    xt = p.sbuf("xt", [128, nDC, TPC], BF16)
    id_bf_sb = p.sbuf("idbf", [128, 128], BF16)
    ones_sb = p.sbuf("ones", [1, 128], BF16)
    gw_sb = p.sbuf("gw", [128, nDC, NE], BF16)
    sge_sb = p.sbuf("sge", [128, nDC], BF16)
    probs = p.sbuf("probs", [128, NE], F32)
    m1 = p.sbuf("m1", [128, 1], F32)
    sumr = p.sbuf("sumr", [128, 1], F32)
    t8 = p.sbuf("t8", [128, NE], F32)
    mask1 = p.sbuf("mask1", [128, NE], F32)
    pr2 = p.sbuf("pr2", [128, NE], F32)
    m2 = p.sbuf("m2", [128, 1], F32)
    cmask = p.sbuf("cmask", [128, NE], F32)
    combbf = p.sbuf("combbf", [128, NE], BF16)
    sigbf = p.sbuf("sigbf", [128, 1], BF16)
    combT = p.sbuf("combT", [1, NE, TPC], BF16)
    sigrow = p.sbuf("sigrow", [1, TPC], BF16)
    comb_bc = p.sbuf("comb_bc", [128, NE, TPC], BF16)
    sig_bc = p.sbuf("sig_bc", [128, TPC], BF16)
    wbuf = p.sbuf("wbuf", [128, nDC, IE], BF16)
    sgw = p.sbuf("sgw", [128, nDC, 128], BF16)
    suw = p.sbuf("suw", [128, nDC, 128], BF16)
    hg = p.sbuf("hg", [128, IE // 128, 512], BF16)
    h_e = p.sbuf("h_e", [128, IE // 128, 512], BF16)
    hgs = p.sbuf("hgs", [128, 512], BF16)
    h_sh = p.sbuf("h_sh", [128, ISH // 128, 512], BF16)
    macc = p.sbuf("macc", [128, 4, D], F32)
    sdw = wbuf[:].rearrange("p a b -> p (a b)").rearrange("p (c d) -> p c d", c=8)
    edbuf = h_sh[:].rearrange("p a b -> p (a b)").rearrange("p (c d) -> p c d", c=8)

    ps_g = p.psum("g", [128, 512], F32)
    ps_u = p.psum("u", [128, 512], F32)
    ps_d = p.psum("d", [128, 512], F32)
    ps_r = p.psum("r", [128, NE + 1], F32)
    ps_misc = p.psum("misc", [128, 512], F32)

    p.dma(id_bf_sb[:], ident_bf[:])
    p.dma(ones_sb[:], ones_row[:])
    for dc in range(nDC):
        p.dma(xt[:, dc, :], xln[:, dc * 128:(dc + 1) * 128], transpose=True)
    p.dma(gw_sb[:], gw[:], eng="gpsimd")
    p.dma(sge_sb[:], sge[:], eng="gpsimd")

    for st in range(nST):
        ssl = slice(st * 128, (st + 1) * 128)
        for dc in range(nDC):
            p.mm(ps_r[:, 0:NE], xt[:, dc, ssl], gw_sb[:, dc, :],
                 start=(dc == 0), stop=(dc == nDC - 1))
        p.v("reduce_max", m1[:], ps_r[:, 0:NE], AX.X)
        p.s("mul", m1[:], m1[:], -1.0)
        p.act(probs[:], ps_r[:, 0:NE], AF.Exp, bias=m1[:], accum_out=sumr[:])
        p.v("reciprocal", sumr[:], sumr[:])
        p.v("tensor_scalar", probs[:], probs[:], sumr[:], None, ALU.mult)
        p.v("reduce_max", m1[:], probs[:], AX.X)
        p.v("tensor_scalar", mask1[:], probs[:], m1[:], None, ALU.is_ge)
        p.v("tensor_mul", t8[:], probs[:], mask1[:])
        p.v("tensor_sub", pr2[:], probs[:], t8[:])
        p.v("reduce_max", m2[:], pr2[:], AX.X)
        p.v("tensor_scalar", cmask[:], probs[:], m2[:], None, ALU.is_ge)
        p.v("tensor_mul", t8[:], probs[:], cmask[:])
        p.v("tensor_copy", combbf[:], t8[:])
        for e in range(NE):
            p.mm(ps_misc[0:1, 0:128], combbf[:, e:e + 1], id_bf_sb[:])
            p.v("tensor_copy", combT[0:1, e, ssl], ps_misc[0:1, 0:128])
        for dc in range(nDC):
            p.mm(ps_r[:, NE:NE + 1], xt[:, dc, ssl], sge_sb[:, dc:dc + 1],
                 start=(dc == 0), stop=(dc == nDC - 1))
        p.act(sigbf[:], ps_r[:, NE:NE + 1], AF.Sigmoid)
        p.mm(ps_misc[0:1, 0:128], sigbf[:], id_bf_sb[:])
        p.v("tensor_copy", sigrow[:, ssl], ps_misc[0:1, 0:128])

    for e in range(NE):
        for g2 in range(TPC // 512):
            gsl = slice(g2 * 512, (g2 + 1) * 512)
            p.mm(ps_misc[:, 0:512], ones_sb[:], combT[0:1, e, gsl])
            p.v("tensor_copy", comb_bc[:, e, gsl], ps_misc[:, 0:512])
    for g2 in range(TPC // 512):
        gsl = slice(g2 * 512, (g2 + 1) * 512)
        p.mm(ps_misc[:, 0:512], ones_sb[:], sigrow[:, gsl])
        p.v("tensor_copy", sig_bc[:, gsl], ps_misc[:, 0:512])

    for g2 in range(TPC // 512):
        gsl = slice(g2 * 512, (g2 + 1) * 512)
        for st in range(4):
            row0 = g2 * 512 + st * 128
            p.dma(macc[:, st, :], h1_i[row0:row0 + 128, :])
        for jt in range(ISH // 128):
            jsl = slice(jt * 128, (jt + 1) * 128)
            p.dma(sgw[:], sg[jt], eng="gpsimd")
            p.dma(suw[:], su[jt], eng="gpsimd")
            for dc in range(nDC):
                p.mm(ps_g[:], sgw[:, dc, :], xt[:, dc, gsl],
                     start=(dc == 0), stop=(dc == nDC - 1))
            p.act(hgs[:], ps_g[:], AF.Silu)
            for dc in range(nDC):
                p.mm(ps_u[:], suw[:, dc, :], xt[:, dc, gsl],
                     start=(dc == 0), stop=(dc == nDC - 1))
            p.v("tensor_tensor", h_sh[:, jt, :], ps_u[:], hgs[:], ALU.mult)
            p.g("tensor_tensor", h_sh[:, jt, :], h_sh[:, jt, :], sig_bc[:, gsl], ALU.mult)
        for jtg in range(4):
            p.dma(sdw[:], sd[jtg], eng="gpsimd")
            for st in range(4):
                for dg in range(D // 512):
                    dsl = slice(dg * 512, (dg + 1) * 512)
                    for jc in range(8):
                        p.mm(ps_d[:], h_sh[:, jtg * 8 + jc, st * 128:(st + 1) * 128],
                             sdw[:, jc, dsl], start=(jc == 0), stop=(jc == 7))
                    p.v("tensor_tensor", macc[:, st, dsl], ps_d[:], macc[:, st, dsl], ALU.add)
        for e in range(NE):
            p.dma(wbuf[:], eg[e], eng="gpsimd")
            for it in range(IE // 128):
                for dc in range(nDC):
                    p.mm(ps_g[:], wbuf[:, dc, it * 128:(it + 1) * 128], xt[:, dc, gsl],
                         start=(dc == 0), stop=(dc == nDC - 1))
                p.act(hg[:, it, :], ps_g[:], AF.Silu)
            p.dma(wbuf[:], eu[e], eng="gpsimd")
            for it in range(IE // 128):
                for dc in range(nDC):
                    p.mm(ps_u[:], wbuf[:, dc, it * 128:(it + 1) * 128], xt[:, dc, gsl],
                         start=(dc == 0), stop=(dc == nDC - 1))
                p.v("tensor_tensor", h_e[:, it, :], ps_u[:], hg[:, it, :], ALU.mult)
                p.g("tensor_tensor", h_e[:, it, :], h_e[:, it, :], comb_bc[:, e, gsl], ALU.mult)
            p.dma(edbuf[:], ed[e], eng="gpsimd")
            for st in range(4):
                for dg in range(D // 512):
                    dsl = slice(dg * 512, (dg + 1) * 512)
                    for ic in range(IE // 128):
                        p.mm(ps_d[:], h_e[:, ic, st * 128:(st + 1) * 128],
                             edbuf[:, ic, dsl], start=(ic == 0), stop=(ic == IE // 128 - 1))
                    p.v("tensor_tensor", macc[:, st, dsl], ps_d[:], macc[:, st, dsl], ALU.add)
        for st in range(4):
            row0 = g2 * 512 + st * 128
            p.dma(h2_o[row0:row0 + 128, :], macc[:, st, :])
    return p.finish()


def run_moe(xln, h1, gw_t, sge_t, eg_t, eu_t, ed_t, sg_t, su_t, sd_t):
    nc = build_moe()
    in_maps = []
    for c in range(NCORES):
        r0 = c * TPC
        in_maps.append({
            "xln": np.ascontiguousarray(xln[r0:r0 + TPC]),
            "h1": np.ascontiguousarray(h1[r0:r0 + TPC]),
            "gw": gw_t, "sge": sge_t, "eg": eg_t, "eu": eu_t, "ed": ed_t,
            "sg": sg_t, "su": su_t, "sd": sd_t,
        })
    res = _run(nc, in_maps)
    return np.concatenate([r["h2"] for r in res], axis=0)


def prep_moe_weights(gate_w, expert_gate_w, expert_up_w, expert_down_w,
                     shared_gate_w, shared_up_w, shared_down_w, shared_expert_gate_w):
    nDC = D // 128
    gw_t = np.ascontiguousarray(gate_w.T.reshape(nDC, 128, NE).transpose(1, 0, 2))
    sge_t = np.ascontiguousarray(shared_expert_gate_w.reshape(nDC, 128).T)
    def lay_du(w):
        return np.ascontiguousarray(
            w.transpose(0, 2, 1).reshape(NE, nDC, 128, w.shape[1]).transpose(0, 2, 1, 3))
    eg_t = lay_du(expert_gate_w)
    eu_t = lay_du(expert_up_w)
    ed_t = np.ascontiguousarray(
        expert_down_w.transpose(0, 2, 1).reshape(NE, IE // 128, 128, D).transpose(0, 2, 1, 3))
    def lay_sh(w):
        return np.ascontiguousarray(
            w.T.reshape(nDC, 128, ISH // 128, 128).transpose(2, 1, 0, 3))
    sg_t = lay_sh(shared_gate_w)
    su_t = lay_sh(shared_up_w)
    sd_t = np.ascontiguousarray(
        shared_down_w.T.reshape(4, 8, 128, D).transpose(0, 2, 1, 3))
    return gw_t, sge_t, eg_t, eu_t, ed_t, sg_t, su_t, sd_t




def build_mambain():
    p = Prog()
    SEXT = TPC + 128
    nFT = 66
    xln = p.inp("xln", (SEXT, D), BF16)
    win = p.inp("win", (nFT, 128, D // 128, 128))
    cw = p.inp("cw", (128, CONV_DIM // 128, DCONV))
    cb = p.inp("cb", (128, CONV_DIM // 128))
    dtb = p.inp("dtb", (32, 1))
    sz_o = p.outp("sz", (DI, TPC), BF16)
    xbcc_o = p.outp("xbcc", (CONV_DIM, TPC), BF16)
    dt_o = p.outp("dt", (32, TPC))

    nDC = D // 128
    xt = p.sbuf("xt", [128, nDC, SEXT], BF16)
    wbuf = p.sbuf("wbuf", [128, nDC, 128], BF16)
    cw_sb = p.sbuf("cw", [128, CONV_DIM // 128, DCONV], F32)
    cb_sb = p.sbuf("cb", [128, CONV_DIM // 128], F32)
    dtb_sb = p.sbuf("dtb", [32, 1], F32)
    ztile = p.sbuf("ztile", [128, TPC], BF16)
    xbc_t = p.sbuf("xbc_t", [128, SEXT], BF16)
    cv = p.sbuf("cv", [128, TPC], F32)
    cvb = p.sbuf("cvb", [128, TPC], BF16)
    dtraw = p.sbuf("dtraw", [32, TPC], F32)
    dtt = p.sbuf("dtt", [32, TPC], F32)
    dta = p.sbuf("dta", [32, TPC], F32)
    ps = p.psum("i", [128, 512], F32)

    p.dma(cw_sb[:], cw[:], eng="gpsimd")
    p.dma(cb_sb[:], cb[:], eng="gpsimd")
    p.dma(dtb_sb[:], dtb[:])
    for dc in range(nDC):
        p.dma(xt[:, dc, :], xln[:, dc * 128:(dc + 1) * 128], transpose=True)

    sgroups = [(0, 512), (512, 512), (1024, 128)]
    for ft in range(nFT):
        p.dma(wbuf[:], win[ft], eng="gpsimd")
        if ft < 32:
            for (o, w) in sgroups[:2]:
                for dc in range(nDC):
                    p.mm(ps[:, 0:w], wbuf[:, dc, :], xt[:, dc, 128 + o:128 + o + w],
                         start=(dc == 0), stop=(dc == nDC - 1))
                p.act(ztile[:, o:o + w], ps[:, 0:w], AF.Silu)
            p.dma(sz_o[ft * 128:(ft + 1) * 128, :], ztile[:])
        elif ft < 65:
            ct = ft - 32
            for (o, w) in sgroups:
                for dc in range(nDC):
                    p.mm(ps[:, 0:w], wbuf[:, dc, :], xt[:, dc, o:o + w],
                         start=(dc == 0), stop=(dc == nDC - 1))
                p.v("tensor_copy", xbc_t[:, o:o + w], ps[:, 0:w])
            p.v("tensor_scalar", cv[:], xbc_t[:, 125:125 + TPC],
                cw_sb[:, ct, 0:1], cb_sb[:, ct:ct + 1], ALU.mult, ALU.add)
            for k2 in range(1, DCONV):
                p.v("scalar_tensor_tensor", cv[:], xbc_t[:, 125 + k2:125 + k2 + TPC],
                    cw_sb[:, ct, k2:k2 + 1], cv[:], ALU.mult, ALU.add)
            p.act(cvb[:], cv[:], AF.Silu)
            p.dma(xbcc_o[ct * 128:(ct + 1) * 128, :], cvb[:])
        else:
            for (o, w) in sgroups[:2]:
                for dc in range(nDC):
                    p.mm(ps[:, 0:w], wbuf[:, dc, :], xt[:, dc, 128 + o:128 + o + w],
                         start=(dc == 0), stop=(dc == nDC - 1))
                p.v("tensor_copy", dtraw[:, o:o + w], ps[0:32, 0:w])
            p.v("tensor_scalar", dtt[:], dtraw[:], dtb_sb[:], None, ALU.add)
            p.act(dta[:], dtt[:], AF.Abs)
            p.act(dta[:], dta[:], AF.Exp, scale=-1.0)
            p.v("tensor_scalar", dta[:], dta[:], 1.0, None, ALU.add)
            p.act(dta[:], dta[:], AF.Ln)
            p.v("tensor_scalar", dtt[:], dtt[:], 0.0, None, ALU.max)
            p.v("tensor_add", dtt[:], dtt[:], dta[:])
            p.dma(dt_o[:], dtt[:])
    return p.finish()


def run_mambain(xln, conv_w, conv_b, dt_bias, W_in):
    nc = build_mambain()
    nDC = D // 128
    win_pad = np.zeros((66 * 128, D), dtype=np.float32)
    win_pad[:DIP] = W_in
    win_l = np.ascontiguousarray(
        win_pad.reshape(66, 128, nDC, 128).transpose(0, 3, 2, 1))
    cw_l = np.ascontiguousarray(conv_w.reshape(CONV_DIM // 128, 128, DCONV).transpose(1, 0, 2))
    cb_l = np.ascontiguousarray(conv_b.reshape(CONV_DIM // 128, 128).T)
    dtb_l = np.ascontiguousarray(dt_bias.reshape(32, 1))
    xln_b = xln.reshape(B, S, D)
    in_maps = []
    for c in range(NCORES):
        r0 = c * TPC
        b, s0 = r0 // S, r0 % S
        if s0 == 0:
            ext = np.zeros((128, D), dtype=BF)
        else:
            ext = xln_b[b, s0 - 128:s0]
        in_maps.append({
            "xln": np.ascontiguousarray(np.concatenate([ext, xln_b[b, s0:s0 + TPC]], axis=0)),
            "win": win_l, "cw": cw_l, "cb": cb_l, "dtb": dtb_l,
        })
    res = _run(nc, in_maps)
    sz = np.stack([np.concatenate([res[2 * b]["sz"], res[2 * b + 1]["sz"]], axis=1)
                   for b in range(B)])
    xbcc = np.stack([np.concatenate([res[2 * b]["xbcc"], res[2 * b + 1]["xbcc"]], axis=1)
                     for b in range(B)])
    dt = np.stack([np.concatenate([res[2 * b]["dt"], res[2 * b + 1]["dt"]], axis=1)
                   for b in range(B)])
    return sz, xbcc, dt


def build_ssd():
    p = Prog()
    nHD = 16
    nC = S // MCH
    xf_i = p.inp("xf", (nHD * 128, S), BF16)
    b_i = p.inp("bi", (DS, S), BF16)
    c_i = p.inp("ci", (DS, S), BF16)
    dt_i = p.inp("dt", (nHD, S))
    alog_i = p.inp("alog", (nHD, 1))
    dvec_i = p.inp("dvec", (1, nHD))
    y_o = p.outp("y", (nHD * 128, S), BF16)

    id16 = p.const("id16", np.eye(16, dtype=np.float32))
    id64_bf = p.const("id64bf", np.eye(64, dtype=BF))
    id128_bf = p.const("id128bf", _eye_bf)
    id128_f = p.const("id128f", _eye_f32)
    triu_c = p.const("triu", _triu_bf)
    ones1x64 = p.const("ones1x64", np.ones((1, 64), dtype=np.float32))
    ones1x128f = p.const("ones1x128f", _ones_row_f32)

    xf = p.sbuf("xf", [128, nHD, S], BF16)
    bsb = p.sbuf("bsb", [DS, S], BF16)
    csb = p.sbuf("csb", [DS, S], BF16)
    y_all = p.sbuf("yall", [128, nHD, S], BF16)
    dt_sb = p.sbuf("dt", [nHD, S], F32)
    alog = p.sbuf("alog", [nHD, 1], F32)
    acol = p.sbuf("acol", [nHD, 1], F32)
    dvec = p.sbuf("dvec", [1, nHD], F32)
    dcols = p.sbuf("dcols", [128, nHD], F32)
    ones_s = p.sbuf("ones_s", [nHD, S], F32)
    dA = p.sbuf("dA", [nHD, S], F32)
    cum = p.sbuf("cum", [nHD, S], F32)
    negcum = p.sbuf("negcum", [nHD, S], F32)
    ecr = p.sbuf("ecr", [nHD, S], F32)
    dec = p.sbuf("dec", [nHD, S], F32)
    cdec = p.sbuf("cdec", [nHD, nC], F32)
    id16_sb = p.sbuf("id16", [16, 16], F32)
    id64b_sb = p.sbuf("id64b", [64, 64], BF16)
    id128b_sb = p.sbuf("id128b", [128, 128], BF16)
    id128f_sb = p.sbuf("id128f", [128, 128], F32)
    triu_sb = p.sbuf("triu", [128, 128], BF16)
    on64_sb = p.sbuf("on64", [1, 64], F32)
    on128_sb = p.sbuf("on128", [1, 128], F32)
    ones128t = p.sbuf("ones128t", [128, 128], F32)
    cumT = p.sbuf("cumT", [128, nHD], F32)
    negcumT = p.sbuf("negcumT", [128, nHD], F32)
    dtT = p.sbuf("dtT", [128, nHD], F32)
    decT = p.sbuf("decT", [128, nHD], F32)
    ecrT = p.sbuf("ecrT", [128, nHD], F32)
    cdec_row = p.sbuf("cdecrow", [1, nHD], F32)
    cdec_bc = p.sbuf("cdecbc", [64, nHD], F32)
    btok = p.sbuf("btok", [128, DS], BF16)
    sm = p.sbuf("sm", [128, 128], BF16)
    colbc = p.sbuf("colbc", [128, 128], F32)
    colbc64 = p.sbuf("colbc64", [128, 64], F32)
    lrelu = p.sbuf("lrelu", [128, 128], F32)
    lt = p.sbuf("lt", [128, 128], BF16)
    phat = p.sbuf("phat", [128, 128], BF16)
    xdt = p.sbuf("xdt", [128, 128], BF16)
    bd = p.sbuf("bd", [128, DS], BF16)
    chat = p.sbuf("chat", [DS, 128], BF16)
    carry = p.sbuf("carry", [DS, nHD, 128], F32)
    prevbf = p.sbuf("prevbf", [DS, 128], BF16)
    ysb = p.sbuf("ysb", [128, 128], F32)

    ps_S = p.psum("S", [128, 128], F32)
    ps_tr = p.psum("tr", [128, 128], F32)
    ps_xt = p.psum("xt", [128, 128], BF16)
    ps_st = p.psum("st", [DS, 128], F32)
    ps_y = p.psum("y", [128, 128], F32)
    ps_sm = p.psum("smll", [128, 16], F32)

    for cst, dst in ((id16, id16_sb), (id64_bf, id64b_sb), (id128_bf, id128b_sb),
                     (id128_f, id128f_sb), (triu_c, triu_sb), (ones1x64, on64_sb),
                     (ones1x128f, on128_sb)):
        p.dma(dst[:], cst[:])
    p.g("memset", ones128t[:], 1.0)
    p.g("memset", ones_s[:], 1.0)
    p.g("memset", carry[:], 0.0)
    p.dma(xf[:], xf_i.rearrange("(h p) s -> p h s", p=128))
    p.dma(bsb[:], b_i[:])
    p.dma(csb[:], c_i[:])
    p.dma(dt_sb[:], dt_i[:])
    p.dma(alog[:], alog_i[:])
    p.dma(dvec[:], dvec_i[:])
    p.act(acol[:], alog[:], AF.Exp)
    p.s("mul", acol[:], acol[:], -1.0)
    p.v("tensor_scalar", dA[:], dt_sb[:], acol[:], None, ALU.mult)
    p.v("tensor_tensor_scan", cum[:], ones_s[:], dA[:], 0.0, ALU.mult, ALU.add)
    p.s("mul", negcum[:], cum[:], -1.0)
    p.mm(ps_sm[:, 0:nHD], on128_sb[:], dvec[:])
    p.v("tensor_copy", dcols[:], ps_sm[:, 0:nHD])
    for c in range(nC):
        csl = slice(c * MCH, (c + 1) * MCH)
        last = (c + 1) * MCH - 1
        if c == 0:
            p.act(ecr[:, csl], cum[:, csl], AF.Exp)
            p.act(cdec[:, 0:1], cum[:, last:last + 1], AF.Exp)
        else:
            prev = c * MCH - 1
            p.act(ecr[:, csl], cum[:, csl], AF.Exp, bias=negcum[:, prev:prev + 1])
            p.act(cdec[:, c:c + 1], cum[:, last:last + 1], AF.Exp,
                  bias=negcum[:, prev:prev + 1])
        p.act(dec[:, csl], negcum[:, csl], AF.Exp, bias=cum[:, last:last + 1])

    for c in range(nC):
        csl = slice(c * MCH, (c + 1) * MCH)
        for (src, dst) in ((cum, cumT), (dt_sb, dtT), (dec, decT), (ecr, ecrT)):
            p.trans(ps_sm[:], src[:, csl], id16_sb[:])
            p.v("tensor_copy", dst[:], ps_sm[:])
        p.s("mul", negcumT[:], cumT[:], -1.0)
        p.trans(ps_xt[:, 0:DS], bsb[:, csl], id64b_sb[:])
        p.v("tensor_copy", btok[:], ps_xt[:, 0:DS])
        p.mm(ps_S[:], bsb[:, csl], csb[:, csl])
        p.v("tensor_tensor", sm[:], ps_S[:], triu_sb[:], ALU.mult)
        p.trans(ps_sm[0:1, 0:nHD], cdec[:, c:c + 1], id16_sb[:])
        p.v("tensor_copy", cdec_row[:], ps_sm[0:1, 0:nHD])
        p.mm(ps_sm[0:64, 0:nHD], on64_sb[:], cdec_row[:])
        p.v("tensor_copy", cdec_bc[:], ps_sm[0:64, 0:nHD])
        for h in range(nHD):
            p.g("tensor_scalar", colbc[:], ones128t[:], negcumT[:, h:h + 1], None, ALU.mult)
            p.trans(ps_tr[:], colbc[:], id128f_sb[:])
            p.v("tensor_scalar", lrelu[:], ps_tr[:], cumT[:, h:h + 1], 0.0,
                ALU.add, ALU.max)
            p.act(lt[:], lrelu[:], AF.Exp, scale=-1.0)
            p.g("tensor_tensor", phat[:], sm[:], lt[:], ALU.mult)
            p.trans(ps_xt[:], xf[:, h, csl], id128b_sb[:])
            p.v("tensor_scalar", xdt[:], ps_xt[:], dtT[:, h:h + 1], None, ALU.mult)
            p.g("tensor_scalar", bd[:], btok[:], decT[:, h:h + 1], None, ALU.mult)
            p.mm(ps_st[:], bd[:], xdt[:])
            p.g("tensor_scalar", colbc64[:], ones128t[:, 0:64], ecrT[:, h:h + 1],
                None, ALU.mult)
            p.trans(ps_tr[0:64, :], colbc64[:], id128f_sb[:])
            p.v("tensor_tensor", chat[:], ps_tr[0:64, :], csb[:, csl], ALU.mult)
            p.v("tensor_copy", prevbf[:], carry[:, h, :])
            p.v("scalar_tensor_tensor", carry[:, h, :], carry[:, h, :],
                cdec_bc[:, h:h + 1], ps_st[:], ALU.mult, ALU.add)
            p.mm(ps_y[:], xdt[:], phat[:], start=True, stop=False)
            p.mm(ps_y[:], prevbf[:], chat[:], start=False, stop=True)
            p.v("scalar_tensor_tensor", y_all[:, h, csl], xf[:, h, csl],
                dcols[:, h:h + 1], ps_y[:], ALU.mult, ALU.add)
    for h in range(nHD):
        p.dma(y_o[h * 128:(h + 1) * 128, :], y_all[:, h, :])
    return p.finish()


def run_ssd(xbcc, dt, A_log, Dv):
    nc = build_ssd()
    in_maps = []
    for c in range(NCORES):
        b, hh = c // 2, c % 2
        in_maps.append({
            "xf": np.ascontiguousarray(xbcc[b, hh * 2048:(hh + 1) * 2048]),
            "bi": np.ascontiguousarray(xbcc[b, DI:DI + DS]),
            "ci": np.ascontiguousarray(xbcc[b, DI + DS:DI + 2 * DS]),
            "dt": np.ascontiguousarray(dt[b, hh * 16:(hh + 1) * 16]),
            "alog": np.ascontiguousarray(A_log[hh * 16:(hh + 1) * 16].reshape(16, 1)),
            "dvec": np.ascontiguousarray(Dv[hh * 16:(hh + 1) * 16].reshape(1, 16)),
        })
    res = _run(nc, in_maps)
    return np.stack([np.concatenate([res[2 * b]["y"], res[2 * b + 1]["y"]], axis=0)
                     for b in range(B)])


def build_mambaout():
    p = Prog()
    nCC = DI // 128
    y_i = p.inp("y", (nCC, 128, TPC), BF16)
    sz_i = p.inp("sz", (nCC, 128, TPC), BF16)
    wout = p.inp("wout", (nCC, 128, D))
    normw = p.inp("normw", (128, nCC))
    h2_i = p.inp("h2", (TPC, D))
    h3_o = p.outp("h3", (TPC, D))

    ones_col = p.const("ones_col", _ones_col_bf)
    ones_row = p.const("ones_row", _ones_row_bf)

    yg = p.sbuf("yg", [128, nCC, TPC], BF16)
    szt = p.sbuf("szt", [128, TPC], BF16)
    sq = p.sbuf("sq", [128, TPC], BF16)
    onec = p.sbuf("onec", [128, 1], BF16)
    oner = p.sbuf("oner", [1, 128], BF16)
    nw = p.sbuf("nw", [128, nCC], F32)
    ssq = p.sbuf("ssq", [1, TPC], F32)
    rstd = p.sbuf("rstd", [1, TPC], F32)
    rstdb = p.sbuf("rstdb", [1, TPC], BF16)
    rstd_bc = p.sbuf("rstdbc", [128, TPC], F32)
    wo8 = p.sbuf("wo8", [128, 8, D], BF16)
    macc = p.sbuf("macc", [128, 4, D], F32)

    ps_a = p.psum("a", [1, 512], F32)
    ps_b = p.psum("b", [1, 512], F32)
    ps_bc = p.psum("bc", [128, 512], F32)
    ps_d = p.psum("d", [128, 512], F32)

    p.dma(onec[:], ones_col[:])
    p.dma(oner[:], ones_row[:])
    p.dma(nw[:], normw[:])
    for cc in range(nCC):
        p.dma(yg[:, cc, :], y_i[cc])
        p.dma(szt[:], sz_i[cc])
        p.g("tensor_tensor", yg[:, cc, :], yg[:, cc, :], szt[:], ALU.mult)
        p.act(sq[:], yg[:, cc, :], AF.Square)
        for half, psx in ((0, ps_a), (1, ps_b)):
            p.mm(psx[:], onec[:], sq[:, half * 512:(half + 1) * 512],
                 start=(cc == 0), stop=(cc == nCC - 1))
    p.v("tensor_copy", ssq[:, 0:512], ps_a[:])
    p.v("tensor_copy", ssq[:, 512:1024], ps_b[:])
    p.s("mul", ssq[:], ssq[:], 1.0 / DI)
    p.v("tensor_scalar", ssq[:], ssq[:], 1e-5, None, ALU.add)
    p.act(ssq[:], ssq[:], AF.Sqrt)
    p.v("reciprocal", rstd[:], ssq[:])
    p.v("tensor_copy", rstdb[:], rstd[:])
    for half in range(2):
        p.mm(ps_bc[:], oner[:], rstdb[:, half * 512:(half + 1) * 512])
        p.v("tensor_copy", rstd_bc[:, half * 512:(half + 1) * 512], ps_bc[:])
    for cc in range(nCC):
        p.v("scalar_tensor_tensor", yg[:, cc, :], yg[:, cc, :], nw[:, cc:cc + 1],
            rstd_bc[:], ALU.mult, ALU.mult)
    for g2 in range(TPC // 512):
        for st in range(4):
            row0 = g2 * 512 + st * 128
            p.dma(macc[:, st, :], h2_i[row0:row0 + 128, :])
        for ccg in range(4):
            p.dma(wo8[:], wout[ccg * 8:(ccg + 1) * 8].rearrange("c p d -> p c d"),
                  eng="gpsimd")
            for st in range(4):
                for dg in range(D // 512):
                    dsl = slice(dg * 512, (dg + 1) * 512)
                    for jc in range(8):
                        cc = ccg * 8 + jc
                        p.mm(ps_d[:], yg[:, cc, g2 * 512 + st * 128: g2 * 512 + (st + 1) * 128],
                             wo8[:, jc, dsl], start=(jc == 0), stop=(jc == 7))
                    p.v("tensor_tensor", macc[:, st, dsl], ps_d[:], macc[:, st, dsl], ALU.add)
        for st in range(4):
            row0 = g2 * 512 + st * 128
            p.dma(h3_o[row0:row0 + 128, :], macc[:, st, :])
    return p.finish()


def run_mambaout(y, sz, W_out, norm_w, h2):
    nc = build_mambaout()
    nCC = DI // 128
    wout_l = np.ascontiguousarray(W_out.T.reshape(nCC, 128, D))
    nw_l = np.ascontiguousarray(norm_w.reshape(nCC, 128).T)
    in_maps = []
    for c in range(NCORES):
        r0 = c * TPC
        b, s0 = r0 // S, r0 % S
        in_maps.append({
            "y": np.ascontiguousarray(y[b, :, s0:s0 + TPC].reshape(nCC, 128, TPC)),
            "sz": np.ascontiguousarray(sz[b, :, s0:s0 + TPC].reshape(nCC, 128, TPC)),
            "wout": wout_l, "normw": nw_l,
            "h2": np.ascontiguousarray(h2[r0:r0 + TPC]),
        })
    res = _run(nc, in_maps)
    return np.concatenate([r["h3"] for r in res], axis=0)




def build_qgamma():
    p = Prog()
    w1s = p.inp("w1s", (HF, 256))
    w2s = p.inp("w2s", (HF, 256))
    w3s = p.inp("w3s", (D, 704))
    part_o = p.outp("part", (1, 3))
    onescol = p.const("onescol", np.ones((128, 1), dtype=np.float32))

    wt = p.sbuf("wt", [128, 704], F32)
    acc = p.sbuf("acc", [128, 1], F32)
    red = p.sbuf("red", [128, 1], F32)
    ones_sb = p.sbuf("ones", [128, 1], F32)
    res = p.sbuf("res", [1, 3], F32)
    ps = p.psum("ps", [1, 1], F32)

    p.dma(ones_sb[:], onescol[:])
    for j, (w, rows, cols) in enumerate(((w1s, HF, 256), (w2s, HF, 256), (w3s, D, 704))):
        p.g("memset", acc[:], 0.0)
        for r in range(rows // 128):
            p.dma(wt[:, 0:cols], w[r * 128:(r + 1) * 128, :])
            p.v("reduce_sum", red[:], wt[:, 0:cols], AX.X, apply_absolute_value=True)
            p.v("tensor_add", acc[:], acc[:], red[:])
        p.mm(ps[:], acc[:], ones_sb[:])
        p.v("tensor_copy", res[:, j:j + 1], ps[:])
    p.dma(part_o[:], res[:])
    return p.finish()


def build_quant():
    p = Prog()
    w1s = p.inp("w1s", (704, D))
    w2s = p.inp("w2s", (704, D))
    w3s = p.inp("w3s", (256, HF))
    parts = p.inp("parts", (8, 3))
    q1_o = p.outp("q1t", (D, 704), BF16)
    q2_o = p.outp("q2t", (D, 704), BF16)
    q3_o = p.outp("q3t", (HF, 256), BF16)

    id128b = p.const("id128b", _eye_bf)
    id64b = p.const("id64b", np.eye(64, dtype=BF))
    id3 = p.const("id3", np.eye(3, dtype=np.float32))
    ones8 = p.const("ones8", np.ones((8, 1), dtype=np.float32))
    ones1x128 = p.const("ones1x128", _ones_row_f32)

    COUNT = float(HF * D)
    parts_sb = p.sbuf("parts", [8, 3], F32)
    ones8_sb = p.sbuf("ones8", [8, 1], F32)
    id3_sb = p.sbuf("id3", [3, 3], F32)
    on128_sb = p.sbuf("on128", [1, 128], F32)
    id128b_sb = p.sbuf("id128b", [128, 128], BF16)
    id64b_sb = p.sbuf("id64b", [64, 64], BF16)
    tot = p.sbuf("tot", [3, 1], F32)
    rgrow = p.sbuf("rgrow", [1, 3], F32)
    rgcol = p.sbuf("rgcol", [128, 3], F32)
    nrgcol = p.sbuf("nrgcol", [128, 3], F32)
    wt = p.sbuf("wt", [128, HF], F32)
    gtp = p.sbuf("gtp", [128, HF], F32)
    gtm = p.sbuf("gtm", [128, HF], F32)
    qt = p.sbuf("qt", [128, HF], BF16)
    qtt = p.sbuf("qtt", [128, 128], BF16)
    ps = p.psum("ps", [3, 3], F32)
    ps_b = p.psum("b", [128, 128], F32)
    ps_t = p.psum("t", [128, 128], BF16)

    for cst, dst in ((id128b, id128b_sb), (id64b, id64b_sb), (id3, id3_sb),
                     (ones8, ones8_sb), (ones1x128, on128_sb)):
        p.dma(dst[:], cst[:])
    p.dma(parts_sb[:], parts[:])
    p.mm(ps[0:3, 0:1], parts_sb[:], ones8_sb[:])
    p.v("tensor_copy", tot[:], ps[0:3, 0:1])
    p.s("mul", tot[:], tot[:], 1.0 / COUNT)
    p.v("tensor_scalar", tot[:], tot[:], 1e-8, None, ALU.add)
    p.v("reciprocal", tot[:], tot[:])
    p.trans(ps[0:1, 0:3], tot[:], id3_sb[:])
    p.v("tensor_copy", rgrow[:], ps[0:1, 0:3])
    for j in range(3):
        p.mm(ps_b[:, 0:1], on128_sb[:], rgrow[:, j:j + 1])
        p.v("tensor_copy", rgcol[:, j:j + 1], ps_b[:, 0:1])
    p.s("mul", nrgcol[:], rgcol[:], -1.0)

    def quant(w, q_o, j, rows, cols):
        nrc = rows // 128 + (1 if rows % 128 else 0)
        for r in range(nrc):
            pr = min(128, rows - r * 128)
            ident = id128b_sb if pr == 128 else id64b_sb
            p.dma(wt[0:pr, 0:cols], w[r * 128:r * 128 + pr, :])
            p.v("tensor_scalar", gtp[0:pr, 0:cols], wt[0:pr, 0:cols],
                rgcol[0:pr, j:j + 1], 0.5, ALU.mult, ALU.is_gt)
            p.v("tensor_scalar", gtm[0:pr, 0:cols], wt[0:pr, 0:cols],
                nrgcol[0:pr, j:j + 1], 0.5, ALU.mult, ALU.is_gt)
            p.v("tensor_tensor", qt[0:pr, 0:cols], gtp[0:pr, 0:cols],
                gtm[0:pr, 0:cols], ALU.subtract)
            for dc in range(cols // 128):
                p.trans(ps_t[:, 0:pr], qt[0:pr, dc * 128:(dc + 1) * 128], ident[0:pr, 0:pr])
                p.v("tensor_copy", qtt[:, 0:pr], ps_t[:, 0:pr])
                p.dma(q_o[dc * 128:(dc + 1) * 128, r * 128:r * 128 + pr], qtt[:, 0:pr])

    quant(w1s, q1_o, 0, 704, D)
    quant(w2s, q2_o, 1, 704, D)
    quant(w3s, q3_o, 2, 256, HF)
    return p.finish()


def run_quantize(w1, w2, w3):
    nc = build_qgamma()
    in_maps = []
    for c in range(NCORES):
        in_maps.append({
            "w1s": np.ascontiguousarray(w1[:, c * 256:(c + 1) * 256]),
            "w2s": np.ascontiguousarray(w2[:, c * 256:(c + 1) * 256]),
            "w3s": np.ascontiguousarray(w3[:, c * 704:(c + 1) * 704]),
        })
    res = _run(nc, in_maps)
    parts = np.concatenate([r["part"] for r in res], axis=0)

    nc2 = build_quant()
    in_maps = []
    for c in range(NCORES):
        in_maps.append({
            "parts": parts,
            "w1s": np.ascontiguousarray(w1[c * 704:(c + 1) * 704]),
            "w2s": np.ascontiguousarray(w2[c * 704:(c + 1) * 704]),
            "w3s": np.ascontiguousarray(w3[c * 256:(c + 1) * 256]),
        })
    res = _run(nc2, in_maps)
    q1t = np.concatenate([r["q1t"] for r in res], axis=1)
    q2t = np.concatenate([r["q2t"] for r in res], axis=1)
    q3t = np.concatenate([r["q3t"] for r in res], axis=1)
    return q1t, q2t, q3t


def build_bitffn():
    p = Prog()
    nFT = HF // 128
    nDC = D // 128
    xln = p.inp("xln", (TPC, D), BF16)
    h3_i = p.inp("h3", (TPC, D))
    q1 = p.inp("q1", (nFT, 128, nDC, 128), BF16)
    q2 = p.inp("q2", (nFT, 128, nDC, 128), BF16)
    q3 = p.inp("q3", (nFT, 128, D), BF16)
    out_o = p.outp("out", (TPC, D))

    xt = p.sbuf("xt", [128, nDC, TPC], BF16)
    w1b = p.sbuf("w1b", [128, nDC, 128], BF16)
    w2b = p.sbuf("w2b", [128, nDC, 128], BF16)
    sg1 = p.sbuf("sg1", [128, 512], BF16)
    hid = p.sbuf("hid", [128, nFT, TPC], BF16)
    q3b = p.sbuf("q3b", [128, 8, D], BF16)
    macc = p.sbuf("macc", [128, 4, D], F32)
    ps1 = p.psum("x1", [128, 512], F32)
    ps2 = p.psum("x2", [128, 512], F32)
    ps_d = p.psum("d", [128, 512], F32)

    for dc in range(nDC):
        p.dma(xt[:, dc, :], xln[:, dc * 128:(dc + 1) * 128], transpose=True)
    for ft in range(nFT):
        p.dma(w1b[:], q1[ft])
        p.dma(w2b[:], q2[ft])
        for sg2 in range(TPC // 512):
            gsl = slice(sg2 * 512, (sg2 + 1) * 512)
            for dc in range(nDC):
                p.mm(ps1[:], w1b[:, dc, :], xt[:, dc, gsl],
                     start=(dc == 0), stop=(dc == nDC - 1))
            p.act(sg1[:], ps1[:], AF.Sigmoid)
            for dc in range(nDC):
                p.mm(ps2[:], w2b[:, dc, :], xt[:, dc, gsl],
                     start=(dc == 0), stop=(dc == nDC - 1))
            p.v("tensor_tensor", hid[:, ft, gsl], ps2[:], sg1[:], ALU.mult)
    blocks = [(b8 * 8, min(8, nFT - b8 * 8)) for b8 in range((nFT + 7) // 8)]
    for g2 in range(TPC // 512):
        for st in range(4):
            row0 = g2 * 512 + st * 128
            p.dma(macc[:, st, :], h3_i[row0:row0 + 128, :])
        for (fc0, bw) in blocks:
            p.dma(q3b[:, 0:bw, :], q3[fc0:fc0 + bw].rearrange("c p d -> p c d"),
                  eng="gpsimd")
            for st in range(4):
                ssl = slice(g2 * 512 + st * 128, g2 * 512 + (st + 1) * 128)
                for dg in range(D // 512):
                    dsl = slice(dg * 512, (dg + 1) * 512)
                    for jc in range(bw):
                        p.mm(ps_d[:], hid[:, fc0 + jc, ssl], q3b[:, jc, dsl],
                             start=(jc == 0), stop=(jc == bw - 1))
                    p.v("tensor_tensor", macc[:, st, dsl], ps_d[:], macc[:, st, dsl], ALU.add)
        for st in range(4):
            row0 = g2 * 512 + st * 128
            p.dma(out_o[row0:row0 + 128, :], macc[:, st, :])
    return p.finish()


def run_bitffn(xln, h3, q1t, q2t, q3t):
    nc = build_bitffn()
    nFT = HF // 128
    nDC = D // 128
    def lay12(qt):
        return np.ascontiguousarray(
            qt.reshape(nDC, 128, nFT, 128).transpose(2, 1, 0, 3))
    q1l, q2l = lay12(q1t), lay12(q2t)
    q3l = np.ascontiguousarray(q3t.reshape(nFT, 128, D))
    in_maps = []
    for c in range(NCORES):
        r0 = c * TPC
        in_maps.append({
            "xln": np.ascontiguousarray(xln[r0:r0 + TPC]),
            "h3": np.ascontiguousarray(h3[r0:r0 + TPC]),
            "q1": q1l, "q2": q2l, "q3": q3l,
        })
    res = _run(nc, in_maps)
    return np.concatenate([r["out"] for r in res], axis=0)




def _carve(pool, start, shape):
    n = 1
    for s in shape:
        n *= s
    ap = pool[:, start:start + n]
    if len(shape) == 1:
        return ap
    names = " ".join(f"d{i}" for i in range(len(shape)))
    kw = {f"d{i}": s for i, s in enumerate(shape)}
    return ap.rearrange(f"p ({names}) -> p {names}", **kw)


class Env:
    pass


def build_fused(dbg=(), stop_after=None):
    p = Prog()
    e = Env()
    nDC = D // 128

    e.xb = p.inp("xb", (S, D))
    e.xrows = p.inp("xrows", (TPC, D))
    e.rolemask = p.inp("rolemask", (128, 1))
    e.wq = p.inp("wq", (HPC, 128, nDC, EH))
    e.wk = p.inp("wk", (HPC, 128, nDC, EH))
    e.wv = p.inp("wv", (HPC, 128, nDC, EH))
    e.wo = p.inp("wo", (EH, NH, D))
    e.gw = p.inp("gw", (128, nDC, NE))
    e.sge = p.inp("sge", (128, nDC))
    e.eg = p.inp("eg", (NE, 128, nDC, IE))
    e.eu = p.inp("eu", (NE, 128, nDC, IE))
    e.ed = p.inp("ed", (NE, 128, IE // 128, D))
    e.sg = p.inp("sg", (ISH // 128, 128, nDC, 128))
    e.su = p.inp("su", (ISH // 128, 128, nDC, 128))
    e.sd = p.inp("sd", (4, 128, 8, D))
    e.win = p.inp("win", (66, 128, nDC, 128))
    e.cw = p.inp("cw", (128, CONV_DIM // 128, DCONV))
    e.cb = p.inp("cb", (128, CONV_DIM // 128))
    e.dtb = p.inp("dtb", (32, 1))
    e.alog = p.inp("alog", (32, 1))
    e.dvec = p.inp("dvec", (1, 32))
    e.wout = p.inp("wout", (DI // 128, 128, D))
    e.normw = p.inp("normw", (128, DI // 128))
    e.w1t = p.inp("w1t", (D, HF))
    e.w2t = p.inp("w2t", (D, HF))
    e.w3t = p.inp("w3t", (HF, D))
    e.out = p.outp("out", (TPC, D))

    dram = lambda n, shp, dt=F32: p.nc.dram_tensor(n, list(shp), dt).ap()
    e.s_xln = dram("s_xln", (S, D), BF16)
    e.s_h = dram("s_h", (TPC, D))
    e.s_zloc = dram("s_zloc", (HPC * EH, S), BF16)
    e.s_zgat = dram("s_zgat", (NH * EH, S), BF16)
    e.s_xbraw = dram("s_xbraw", (CONV_DIM, TPC), BF16)
    e.s_xctr = dram("s_xctr", (CONV_DIM, 4), BF16)
    e.s_xgat = dram("s_xgat", (2 * CONV_DIM, 4), BF16)
    e.s_sz = dram("s_sz", (DI, TPC), BF16)
    e.s_xbcc = dram("s_xbcc", (CONV_DIM, TPC), BF16)
    e.s_dt = dram("s_dt", (32, TPC))
    e.s_cctr = dram("s_cctr", (DS * 32, 128))
    e.s_cgat = dram("s_cgat", (2 * DS * 32, 128))
    e.s_q1 = dram("s_q1", (D, HF), BF16)
    e.s_q2 = dram("s_q2", (D, HF), BF16)
    e.s_q3 = dram("s_q3", (HF, D), BF16)

    e.A = p.sbuf("poolA", [128, 32768], BF16)[:]
    e.B = p.sbuf("poolB", [128, 32768], BF16)[:]
    e.C = p.sbuf("poolC", [128, 8192], F32)[:]
    e.Dp = p.sbuf("poolD", [128, 8192], F32)[:]
    e.ps_big = p.psum("ps_big", [128, 2048], F32)[:]
    e.ps_a = p.psum("ps_a", [128, 512], F32)[:]
    e.ps_b = p.psum("ps_b", [128, 512], F32)[:]
    e.ps_c = p.psum("ps_c", [128, 512], F32)[:]
    e.ps_bf = p.psum("ps_bf", [128, 256], BF16)[:]

    e.id128b = p.sbuf("id128b", [128, 128], BF16)
    e.id128f = p.sbuf("id128f", [128, 128], F32)
    e.id64b = p.sbuf("id64b", [64, 64], BF16)
    e.id32f = p.sbuf("id32f", [32, 32], F32)
    e.trilb = p.sbuf("trilb", [128, 128], BF16)
    e.triub = p.sbuf("triub", [128, 128], BF16)
    e.onecol_b = p.sbuf("onecol_b", [128, 1], BF16)
    e.onerow_b = p.sbuf("onerow_b", [1, 128], BF16)
    e.onerow_f = p.sbuf("onerow_f", [1, 128], F32)
    e.on64_f = p.sbuf("on64_f", [1, 64], F32)
    e.ones128t = p.sbuf("ones128t", [128, 128], F32)
    for name, arr, dst in (
        ("c_id128b", _eye_bf, e.id128b), ("c_id128f", _eye_f32, e.id128f),
        ("c_id64b", np.eye(64, dtype=BF), e.id64b),
        ("c_id32f", np.eye(32, dtype=np.float32), e.id32f),
        ("c_trilb", _tril_bf, e.trilb), ("c_triub", _triu_bf, e.triub),
        ("c_onecol", _ones_col_bf, e.onecol_b), ("c_onerow", _ones_row_bf, e.onerow_b),
        ("c_onerowf", _ones_row_f32, e.onerow_f),
        ("c_on64f", np.ones((1, 64), dtype=np.float32), e.on64_f),
        ("c_id11f", np.ones((1, 1), dtype=np.float32), None),
    ):
        if dst is None:
            e.id11f = p.sbuf("id11f", [1, 1], F32)
            dst = e.id11f
        cst = p.const(name, arr)
        p.dma(dst[:], cst[:])
    p.g("memset", e.ones128t[:], 1.0)
    e.role = p.sbuf("role", [128, 1], F32)
    e.invrole = p.sbuf("invrole", [128, 1], F32)
    p.dma(e.role[:], e.rolemask[:])
    p.v("tensor_scalar", e.invrole[:], e.role[:], -1.0, 1.0, ALU.mult, ALU.add)

    qparts = make_quant_parts(p, e)
    qstate = {"i": 0}

    def pump(n=24):
        for _ in range(n):
            if qstate["i"] < len(qparts):
                qparts[qstate["i"]]()
                qstate["i"] += 1

    def drain_quant():
        pump(len(qparts))

    phases = [
        ("ln0", lambda: emit_ln_f(p, e, e.xb, e.s_xln, S)),
        ("attnz", lambda: emit_attnz_f(p, e, pump=pump)),
        ("collz", lambda: p.coll(lambda g, i=e.s_zloc, o=e.s_zgat: g.collective_compute(
            "AllGather", ALU.bypass, [[0, 1], [2, 3], [4, 5], [6, 7]], [i[:, :]], [o[:, :]]),
            reads=[e.s_zloc[:, :]], writes=[e.s_zgat[:, :]])),
        ("attnout", lambda: emit_attnout_f(p, e)),
        ("qdrain", drain_quant),
        ("ln1", lambda: emit_ln_f(p, e, e.s_h, e.s_xln, TPC)),
        ("moe", lambda: emit_moe_f(p, e)),
        ("ln2", lambda: emit_ln_f(p, e, e.s_h, e.s_xln, TPC)),
        ("mambain", lambda: emit_mambain_f(p, e)),
        ("collx", lambda: p.coll(lambda g, i=e.s_xctr, o=e.s_xgat: g.collective_compute(
            "AllGather", ALU.bypass, [[0, 1], [2, 3], [4, 5], [6, 7]], [i[:, :]], [o[:, :]]),
            reads=[e.s_xctr[:, :]], writes=[e.s_xgat[:, :]])),
        ("conv", lambda: emit_conv_f(p, e)),
        ("ssdA", lambda: emit_ssdA_f(p, e)),
        ("collc", lambda: p.coll(lambda g, i=e.s_cctr, o=e.s_cgat: g.collective_compute(
            "AllGather", ALU.bypass, [[0, 1], [2, 3], [4, 5], [6, 7]], [i[:, :]], [o[:, :]]),
            reads=[e.s_cctr[:, :]], writes=[e.s_cgat[:, :]])),
        ("ssdB", lambda: emit_ssdB_f(p, e)),
        ("mambaout", lambda: emit_mambaout_f(p, e)),
        ("ln3", lambda: emit_ln_f(p, e, e.s_h, e.s_xln, TPC)),
        ("bitffn", lambda: emit_bitffn_f(p, e)),
    ]
    for name, fn in phases:
        fn()
        if stop_after == name:
            break

    for name in dbg:
        src_ap, shp, dt = {
            "xln": (e.s_xln, (S, D), BF16),
            "h": (e.s_h, (TPC, D), F32),
            "zgat": (e.s_zgat, (NH * EH, S), BF16),
            "sz": (e.s_sz, (DI, TPC), BF16),
            "xbcc": (e.s_xbcc, (CONV_DIM, TPC), BF16),
            "xbraw": (e.s_xbraw, (CONV_DIM, TPC), BF16),
            "xgat": (e.s_xgat, (2 * CONV_DIM, 4), BF16),
            "dt": (e.s_dt, (32, TPC), F32),
            "q1": (e.s_q1, (D, HF), BF16),
            "q3": (e.s_q3, (HF, D), BF16),
            "cgat": (e.s_cgat, (2 * DS * 32, 128), F32),
        }[name]
        o = p.outp("dbg_" + name, shp, dt)
        rows, cols = shp
        nt = (rows + 127) // 128
        for i in range(nt):
            pr = min(128, rows - i * 128)
            stage = _carve(e.B, 0, (cols,)) if dt != F32 else _carve(e.C, 0, (cols,))
            p.dma(stage[0:pr, :], src_ap[i * 128:i * 128 + pr, :])
            p.dma(o[i * 128:i * 128 + pr, :], stage[0:pr, :])
    return p.finish()


def make_quant_parts(p, e):
    COUNT = float(HF * D)
    CH = 2816
    acc = p.sbuf("q_acc", [128, 1], F32)
    red = p.sbuf("q_red", [128, 1], F32)
    onescol_f = p.sbuf("q_ones", [128, 1], F32)
    tot3 = p.sbuf("q_tot", [1, 3], F32)
    rgcol = p.sbuf("q_rg", [128, 3], F32)
    nrgcol = p.sbuf("q_nrg", [128, 3], F32)
    wt2 = [_carve(e.C, 0, (CH,)), _carve(e.C, CH, (CH,))]
    gt2 = [_carve(e.Dp, 0, (CH,)), _carve(e.Dp, CH, (CH,))]
    mats = ((e.w1t, e.s_q1, D, HF, 0), (e.w2t, e.s_q2, D, HF, 1),
            (e.w3t, e.s_q3, HF, D, 2))
    parts = []
    kctr = [0]

    def init_part():
        p.g("memset", onescol_f[:], 1.0)
        p.g("memset", acc[:], 0.0)
    parts.append(init_part)

    for mi, (w, _, rows, cols, j) in enumerate(mats):
        for r in range(rows // 128):
            def gamma_part(w=w, r=r, cols=cols, mi=mi, j=j, last=(r == rows // 128 - 1)):
                for c0 in range(0, cols, CH):
                    wt = wt2[kctr[0] % 2]
                    kctr[0] += 1
                    cw = min(CH, cols - c0)
                    p.dma(wt[:, 0:cw], w[r * 128:(r + 1) * 128, c0:c0 + cw])
                    p.v("reduce_sum", red[:], wt[:, 0:cw], AX.X,
                        apply_absolute_value=True)
                    p.v("tensor_add", acc[:], acc[:], red[:])
                if last:
                    p.mm(e.ps_a[0:1, 0:1], acc[:], onescol_f[:])
                    p.v("tensor_copy", tot3[:, j:j + 1], e.ps_a[0:1, 0:1])
                    if j < 2:
                        p.g("memset", acc[:], 0.0)
            parts.append(gamma_part)

    def rg_part():
        p.s("mul", tot3[:], tot3[:], 1.0 / COUNT)
        p.v("tensor_scalar", tot3[:], tot3[:], 1e-8, None, ALU.add)
        p.v("reciprocal", tot3[:], tot3[:])
        for j in range(3):
            p.mm(e.ps_a[0:128, 0:1], e.onerow_f[:], tot3[:, j:j + 1])
            p.v("tensor_copy", rgcol[:, j:j + 1], e.ps_a[0:128, 0:1])
        p.s("mul", nrgcol[:], rgcol[:], -1.0)
    parts.append(rg_part)

    for (w, q_o, rows, cols, j) in mats:
        for r in range(rows // 128):
            def quant_part(w=w, q_o=q_o, r=r, cols=cols, j=j):
                for c0 in range(0, cols, CH):
                    wt, gtm = wt2[kctr[0] % 2], gt2[kctr[0] % 2]
                    kctr[0] += 1
                    cw = min(CH, cols - c0)
                    p.dma(wt[:, 0:cw], w[r * 128:(r + 1) * 128, c0:c0 + cw])
                    p.v("tensor_scalar", gtm[:, 0:cw], wt[:, 0:cw],
                        nrgcol[:, j:j + 1], 0.5, ALU.mult, ALU.is_gt)
                    p.v("tensor_scalar", wt[:, 0:cw], wt[:, 0:cw],
                        rgcol[:, j:j + 1], 0.5, ALU.mult, ALU.is_gt)
                    p.v("tensor_tensor", gtm[:, 0:cw], wt[:, 0:cw], gtm[:, 0:cw],
                        ALU.subtract)
                    p.dma(q_o[r * 128:(r + 1) * 128, c0:c0 + cw], gtm[:, 0:cw],
                          eng="gpsimd")
            parts.append(quant_part)
    return parts


def emit_ln_f(p, e, src, dst, rows):
    x_sb = _carve(e.C, 0, (D,))
    sq_sb = _carve(e.C, 2048, (D,))
    o_sb = _carve(e.A, 0, (D,))
    rs = p.sbuf("ln_rs", [128, 1], F32)
    mean = p.sbuf("ln_mean", [128, 1], F32)
    ssq = p.sbuf("ln_ssq", [128, 1], F32)
    m2 = p.sbuf("ln_m2", [128, 1], F32)
    var = p.sbuf("ln_var", [128, 1], F32)
    rstd = p.sbuf("ln_rstd", [128, 1], F32)
    for i in range(rows // 128):
        sl = slice(i * 128, (i + 1) * 128)
        p.dma(x_sb[:, :], src[sl, :])
        p.v("reduce_sum", rs[:], x_sb[:, :], AX.X)
        p.s("mul", mean[:], rs[:], 1.0 / D)
        p.act(sq_sb[:, :], x_sb[:, :], AF.Square, accum_out=ssq[:])
        p.s("mul", ssq[:], ssq[:], 1.0 / D)
        p.v("tensor_mul", m2[:], mean[:], mean[:])
        p.v("tensor_sub", var[:], ssq[:], m2[:])
        p.v("tensor_scalar", var[:], var[:], 1e-5, None, ALU.add)
        p.act(var[:], var[:], AF.Sqrt)
        p.v("reciprocal", rstd[:], var[:])
        p.v("tensor_scalar", o_sb[:, :], x_sb[:, :], mean[:], rstd[:],
            ALU.subtract, ALU.mult)
        p.dma(dst[sl, :], o_sb[:, :])


def emit_attnz_f(p, e, pump=None):
    nDC = D // 128
    nST = S // 128
    xt = _carve(e.A, 0, (nDC, S))
    q_sb2 = [_carve(e.B, 0, (S,)), _carve(e.B, 2048, (S,))]
    k_sb2 = [_carve(e.B, 4096, (S,)), _carve(e.B, 6144, (S,))]
    v_sb2 = [_carve(e.B, 8192, (nST, EH)), _carve(e.B, 10240, (nST, EH))]
    P_sb = _carve(e.B, 12288, (S,))
    sq_sb = _carve(e.B, 12288, (S,))
    z_sb2 = [_carve(e.B, 16384, (S,)), _carve(e.B, 18432, (S,))]
    pt_sb = _carve(e.B, 20480, (4, 128))
    diag_sb = _carve(e.B, 20992, (128,))
    Pd_sb = _carve(e.B, 21120, (128,))
    w_q2 = [_carve(e.B, 21248, (nDC, EH)), _carve(e.B, 23296, (nDC, EH))]
    w_k2 = [_carve(e.B, 25344, (nDC, EH)), _carve(e.B, 27392, (nDC, EH))]
    w_v2 = [_carve(e.B, 29440, (nDC, EH)), _carve(e.B, 29440, (nDC, EH))]
    r2 = _carve(e.C, 5632, (S,))[0:1, :]
    k2 = _carve(e.Dp, 5632, (S,))[0:1, :]
    km = p.sbuf("az_km", [1, 1], F32)
    negb = p.sbuf("az_negb", [128, nST], F32)
    sums = p.sbuf("az_sums", [128, 4], F32)
    sumtot = p.sbuf("az_st", [128, 1], F32)
    rsum = p.sbuf("az_rs", [128, 1], F32)
    ps_s = e.ps_big
    ps_p = e.ps_a
    ps_z = e.ps_b
    ps_r = e.ps_c[0:1, :]
    ps_t = e.ps_c
    ps_v = e.ps_c

    for dc in range(nDC):
        p.dma(xt[:, dc, :], e.s_xln[:, dc * 128:(dc + 1) * 128], transpose=True)
    for j in range(HPC):
        par = j % 2
        q_sb, k_sb, v_sb, z_sb = q_sb2[par], k_sb2[par], v_sb2[par], z_sb2[par]
        w_q, w_k, w_v = w_q2[par], w_k2[par], w_v2[par]
        for (w_in, w_dst) in ((e.wq, w_q), (e.wk, w_k), (e.wv, w_v)):
            p.dma(w_dst[:, :, :], w_in[j], eng="gpsimd")
        for (w_dst, dst) in ((w_q, q_sb), (w_k, k_sb)):
            for sg in range(S // 512):
                for dc in range(nDC):
                    p.mm(ps_p[:], w_dst[:, dc, :], xt[:, dc, sg * 512:(sg + 1) * 512],
                         start=(dc == 0), stop=(dc == nDC - 1))
                p.v("tensor_copy", dst[:, sg * 512:(sg + 1) * 512], ps_p[:])
        for st in range(nST):
            for dc in range(nDC):
                p.mm(ps_v[:, 0:EH], xt[:, dc, st * 128:(st + 1) * 128], w_v[:, dc, :],
                     start=(dc == 0), stop=(dc == nDC - 1))
            p.v("tensor_copy", v_sb[:, st, :], ps_v[:, 0:EH])
        for (srcb, dst) in ((q_sb, r2), (k_sb, k2)):
            p.act(sq_sb[:, :], srcb[:, :], AF.Square)
            for sg in range(S // 512):
                p.mm(ps_r[:], e.onecol_b[:], sq_sb[:, sg * 512:(sg + 1) * 512])
                p.v("tensor_copy", dst[:, sg * 512:(sg + 1) * 512], ps_r[:])
        p.v("reduce_max", km[:], k2[:], AX.X)
        p.v("tensor_scalar", r2[:], r2[:], km[:], None, ALU.mult)
        p.act(r2[:], r2[:], AF.Sqrt, scale=1.0 / EH)
        p.s("mul", r2[:], r2[:], -1.0)
        for st in range(nST):
            p.trans(ps_t[0:128, 0:1], r2[:, st * 128:(st + 1) * 128], e.id11f[:])
            p.v("tensor_copy", negb[:, st:st + 1], ps_t[0:128, 0:1])
        for i in range(nST):
            nm = (i + 1) * 128
            ng = (nm + 511) // 512
            lastw = nm - (ng - 1) * 512
            for g in range(ng):
                w = min(512, nm - g * 512)
                p.mm(ps_s[:, g * 512:g * 512 + w], q_sb[:, i * 128:(i + 1) * 128],
                     k_sb[:, g * 512:g * 512 + w])
            for g in range(ng - 1):
                p.act(P_sb[:, g * 512:(g + 1) * 512], ps_s[:, g * 512:(g + 1) * 512],
                      AF.Exp, bias=negb[:, i:i + 1], scale=SCALE,
                      accum_out=sums[:, g:g + 1])
            off = (ng - 1) * 512
            p.act(P_sb[:, off:off + lastw], ps_s[:, off:off + lastw],
                  AF.Exp, bias=negb[:, i:i + 1], scale=SCALE)
            p.g("tensor_tensor", Pd_sb[:, :], P_sb[:, nm - 128:nm], e.trilb[:], ALU.mult)
            p.v("tensor_copy", P_sb[:, nm - 128:nm], Pd_sb[:, :])
            p.v("reduce_sum", sums[:, ng - 1:ng], P_sb[:, off:off + lastw], AX.X)
            p.v("reduce_sum", sumtot[:], sums[:, 0:ng], AX.X)
            p.v("reciprocal", rsum[:], sumtot[:])
            p.v("tensor_scalar", diag_sb[:, :], e.id128b[:], rsum[:], None, ALU.mult)
            for j4 in range((i + 4) // 4):
                cnt = min(4, i + 1 - j4 * 4)
                for u in range(cnt):
                    m = j4 * 4 + u
                    p.mm(ps_p[:, u * 128:(u + 1) * 128],
                         P_sb[:, m * 128:(m + 1) * 128], diag_sb[:, :])
                p.v("tensor_copy", pt_sb[:, 0:cnt, :], ps_p[:, 0:cnt * 128])
                for u in range(cnt):
                    m = j4 * 4 + u
                    p.mm(ps_z[:, 0:128], v_sb[:, m, :], pt_sb[:, u, :],
                         start=(m == 0), stop=(m == i))
            p.v("tensor_copy", z_sb[:, i * 128:(i + 1) * 128], ps_z[:, 0:128])
        p.dma(e.s_zloc[j * 128:(j + 1) * 128, :], z_sb[:, :])
        if pump is not None:
            pump()


def emit_attnout_f(p, e):
    nDC = D // 128
    z16 = _carve(e.B, 0, (NH, TPC))
    wo_sb = _carve(e.A, 0, (NH, D))
    zh0 = _carve(e.B, 16384, (TPC,))
    zh1 = _carve(e.B, 17408, (TPC,))
    x_sb = _carve(e.C, 0, (D,))
    o_sb = _carve(e.C, 2048, (D,))
    ps = e.ps_a
    p.dma(wo_sb[:, :, :], e.wo[:], eng="gpsimd")
    zg = e.s_zgat.rearrange("(h p) s -> h p s", p=128)
    for h in range(NH):
        p.dma(zh0[:, :], zg[h, :, 0:TPC])
        p.dma(zh1[:, :], zg[h, :, TPC:S])
        p.v("tensor_scalar", zh0[:, :], zh0[:, :], e.invrole[:], None, ALU.mult)
        p.v("scalar_tensor_tensor", z16[:, h, :], zh1[:, :], e.role[:], zh0[:, :],
            ALU.mult, ALU.add)
    for st in range(TPC // 128):
        p.dma(x_sb[:, :], e.xrows[st * 128:(st + 1) * 128, :])
        for dg in range(D // 512):
            for h in range(NH):
                p.mm(ps[:], z16[:, h, st * 128:(st + 1) * 128],
                     wo_sb[:, h, dg * 512:(dg + 1) * 512],
                     start=(h == 0), stop=(h == NH - 1))
            p.v("tensor_add", o_sb[:, dg * 512:(dg + 1) * 512], ps[:],
                x_sb[:, dg * 512:(dg + 1) * 512])
        p.dma(e.s_h[st * 128:(st + 1) * 128, :], o_sb[:, :])




def emit_moe_f(p, e):
    nDC = D // 128
    nST = TPC // 128
    xt = _carve(e.A, 0, (nDC, TPC))
    h_sh = _carve(e.A, 16384, (ISH // 128, 512))
    gw_sb = _carve(e.B, 22528, (nDC, NE))
    combT = _carve(e.B, 0, (NE, TPC))[0:1, :, :]
    sigrow = _carve(e.B, 8192, (TPC,))[0:1, :]
    comb_bc = _carve(e.B, 24576, (NE, TPC))
    sgw2 = [_carve(e.B, 0, (nDC, 128)), _carve(e.B, 2048, (nDC, 128))]
    suw2 = [_carve(e.B, 4096, (nDC, 128)), _carve(e.B, 6144, (nDC, 128))]
    sdw2 = [_carve(e.B, 0, (4, D)), _carve(e.B, 8192, (4, D))]
    wege = _carve(e.B, 0, (nDC, IE))
    wede = _carve(e.B, 0, (IE // 128, D))
    weue = _carve(e.A, 16384, (nDC, IE))
    h_e2 = _carve(e.B, 16384, (IE // 128, TPC))
    macc2 = [_carve(e.C, 0, (4, D)), _carve(e.Dp, 0, (4, D))]
    probs = p.sbuf("mo_probs", [128, NE], F32)
    m1 = p.sbuf("mo_m1", [128, 1], F32)
    sumr = p.sbuf("mo_sumr", [128, 1], F32)
    t8 = p.sbuf("mo_t8", [128, NE], F32)
    mask1 = p.sbuf("mo_mask1", [128, NE], F32)
    pr2 = p.sbuf("mo_pr2", [128, NE], F32)
    m2 = p.sbuf("mo_m2", [128, 1], F32)
    cmask = p.sbuf("mo_cmask", [128, NE], F32)
    combbf = p.sbuf("mo_combbf", [128, NE], BF16)
    sigbf = p.sbuf("mo_sigbf", [128, 1], BF16)
    sge_sb = p.sbuf("mo_sge", [128, nDC], BF16)
    sig_bc = p.sbuf("mo_sigbc", [128, TPC], BF16)
    ps_r = e.ps_big[:, 0:NE + 1]
    ps_m = e.ps_big[:, 512:1024]
    ps_g2 = [e.ps_a, e.ps_big[:, 1024:1536]]
    ps_u2 = [e.ps_b, e.ps_big[:, 1536:2048]]
    ps_d2 = [e.ps_c, e.ps_big[:, 0:512]]

    for dc in range(nDC):
        p.dma(xt[:, dc, :], e.s_xln[0:TPC, dc * 128:(dc + 1) * 128], transpose=True)
    p.dma(gw_sb[:, :, :], e.gw[:], eng="gpsimd")
    p.dma(sge_sb[:], e.sge[:], eng="gpsimd")
    for st in range(nST):
        ssl = slice(st * 128, (st + 1) * 128)
        for dc in range(nDC):
            p.mm(ps_r[:, 0:NE], xt[:, dc, ssl], gw_sb[:, dc, :],
                 start=(dc == 0), stop=(dc == nDC - 1))
        p.v("reduce_max", m1[:], ps_r[:, 0:NE], AX.X)
        p.s("mul", m1[:], m1[:], -1.0)
        p.act(probs[:], ps_r[:, 0:NE], AF.Exp, bias=m1[:], accum_out=sumr[:])
        p.v("reciprocal", sumr[:], sumr[:])
        p.v("tensor_scalar", probs[:], probs[:], sumr[:], None, ALU.mult)
        p.v("reduce_max", m1[:], probs[:], AX.X)
        p.v("tensor_scalar", mask1[:], probs[:], m1[:], None, ALU.is_ge)
        p.v("tensor_mul", t8[:], probs[:], mask1[:])
        p.v("tensor_sub", pr2[:], probs[:], t8[:])
        p.v("reduce_max", m2[:], pr2[:], AX.X)
        p.v("tensor_scalar", cmask[:], probs[:], m2[:], None, ALU.is_ge)
        p.v("tensor_mul", t8[:], probs[:], cmask[:])
        p.v("tensor_copy", combbf[:], t8[:])
        for ex in range(NE):
            p.mm(ps_m[0:1, 0:128], combbf[:, ex:ex + 1], e.id128b[:])
            p.v("tensor_copy", combT[0:1, ex, ssl], ps_m[0:1, 0:128])
        for dc in range(nDC):
            p.mm(ps_r[:, NE:NE + 1], xt[:, dc, ssl], sge_sb[:, dc:dc + 1],
                 start=(dc == 0), stop=(dc == nDC - 1))
        p.act(sigbf[:], ps_r[:, NE:NE + 1], AF.Sigmoid)
        p.mm(ps_m[0:1, 0:128], sigbf[:], e.id128b[:])
        p.v("tensor_copy", sigrow[:, ssl], ps_m[0:1, 0:128])
    for ex in range(NE):
        for g2 in range(TPC // 512):
            gsl = slice(g2 * 512, (g2 + 1) * 512)
            p.mm(ps_m[:, 0:512], e.onerow_b[:], combT[0:1, ex, gsl])
            p.v("tensor_copy", comb_bc[:, ex, gsl], ps_m[:, 0:512])
    for g2 in range(TPC // 512):
        gsl = slice(g2 * 512, (g2 + 1) * 512)
        p.mm(ps_m[:, 0:512], e.onerow_b[:], sigrow[:, gsl])
        p.v("tensor_copy", sig_bc[:, gsl], ps_m[:, 0:512])
    for g2 in range(TPC // 512):
        for st in range(4):
            row0 = g2 * 512 + st * 128
            p.dma(macc2[g2][:, st, :], e.s_h[row0:row0 + 128, :])
    for g2 in range(TPC // 512):
        gsl = slice(g2 * 512, (g2 + 1) * 512)
        macc = macc2[g2]
        for jt in range(ISH // 128):
            par = jt % 2
            p.dma(sgw2[par][:, :, :], e.sg[jt], eng="gpsimd")
            p.dma(suw2[par][:, :, :], e.su[jt], eng="gpsimd")
            pg, pu = ps_g2[par], ps_u2[par]
            for dc in range(nDC):
                p.mm(pg[:], sgw2[par][:, dc, :], xt[:, dc, gsl],
                     start=(dc == 0), stop=(dc == nDC - 1))
            p.act(h_sh[:, jt, :], pg[:], AF.Silu)
            for dc in range(nDC):
                p.mm(pu[:], suw2[par][:, dc, :], xt[:, dc, gsl],
                     start=(dc == 0), stop=(dc == nDC - 1))
            p.v("tensor_tensor", h_sh[:, jt, :], pu[:], h_sh[:, jt, :], ALU.mult)
            p.g("tensor_tensor", h_sh[:, jt, :], h_sh[:, jt, :], sig_bc[:, gsl], ALU.mult)
        for jtg in range(8):
            par = jtg % 2
            p.dma(sdw2[par][:, :, :],
                  e.sd[jtg // 2, :, (jtg % 2) * 4:(jtg % 2) * 4 + 4, :], eng="gpsimd")
            for st in range(4):
                for dg in range(D // 512):
                    dsl = slice(dg * 512, (dg + 1) * 512)
                    pd = ps_d2[(st * 4 + dg) % 2]
                    for jc in range(4):
                        p.mm(pd[:], h_sh[:, jtg * 4 + jc, st * 128:(st + 1) * 128],
                             sdw2[par][:, jc, dsl], start=(jc == 0), stop=(jc == 3))
                    p.v("tensor_tensor", macc[:, st, dsl], pd[:], macc[:, st, dsl], ALU.add)
    for ex in range(NE):
        p.dma(wege[:, :, :], e.eg[ex], eng="gpsimd")
        p.dma(weue[:, :, :], e.eu[ex], eng="gpsimd")
        for g2 in range(TPC // 512):
            gsl = slice(g2 * 512, (g2 + 1) * 512)
            for it in range(IE // 128):
                pg = ps_g2[it % 2]
                for dc in range(nDC):
                    p.mm(pg[:], wege[:, dc, it * 128:(it + 1) * 128], xt[:, dc, gsl],
                         start=(dc == 0), stop=(dc == nDC - 1))
                p.act(h_e2[:, it, gsl], pg[:], AF.Silu)
                pu = ps_u2[it % 2]
                for dc in range(nDC):
                    p.mm(pu[:], weue[:, dc, it * 128:(it + 1) * 128],
                         xt[:, dc, gsl], start=(dc == 0), stop=(dc == nDC - 1))
                p.v("tensor_tensor", h_e2[:, it, gsl], pu[:], h_e2[:, it, gsl], ALU.mult)
                p.g("tensor_tensor", h_e2[:, it, gsl], h_e2[:, it, gsl],
                    comb_bc[:, ex, gsl], ALU.mult)
        p.dma(wede[:, :, :], e.ed[ex], eng="gpsimd")
        for g2 in range(TPC // 512):
            macc = macc2[g2]
            for st in range(4):
                sl2 = slice(g2 * 512 + st * 128, g2 * 512 + (st + 1) * 128)
                for dg in range(D // 512):
                    dsl = slice(dg * 512, (dg + 1) * 512)
                    pd = ps_d2[(st * 4 + dg) % 2]
                    for ic in range(IE // 128):
                        p.mm(pd[:], h_e2[:, ic, sl2], wede[:, ic, dsl],
                             start=(ic == 0), stop=(ic == IE // 128 - 1))
                    p.v("tensor_tensor", macc[:, st, dsl], pd[:], macc[:, st, dsl], ALU.add)
    for g2 in range(TPC // 512):
        for st in range(4):
            row0 = g2 * 512 + st * 128
            p.dma(e.s_h[row0:row0 + 128, :], macc2[g2][:, st, :])


def emit_mambain_f(p, e):
    nDC = D // 128
    xt = _carve(e.A, 0, (nDC, TPC))
    wbuf2 = [_carve(e.B, 0, (nDC, 128)), _carve(e.B, 6144, (nDC, 128))]
    ztile = _carve(e.B, 2048, (TPC,))
    xbc_sb = _carve(e.B, 3072, (TPC,))
    dtraw = _carve(e.C, 0, (TPC,))
    dtt = _carve(e.C, 1024, (TPC,))
    dta = _carve(e.C, 2048, (TPC,))
    dtb_sb = p.sbuf("mi_dtb", [32, 1], F32)
    ps2 = [e.ps_a, e.ps_b]
    p.dma(dtb_sb[:], e.dtb[:])
    for dc in range(nDC):
        p.dma(xt[:, dc, :], e.s_xln[0:TPC, dc * 128:(dc + 1) * 128], transpose=True)
    for ft in range(66):
        wbuf = wbuf2[ft % 2]
        p.dma(wbuf[:, :, :], e.win[ft], eng="gpsimd")
        if ft < 32:
            for g2 in range(TPC // 512):
                gsl = slice(g2 * 512, (g2 + 1) * 512)
                ps = ps2[g2]
                for dc in range(nDC):
                    p.mm(ps[:], wbuf[:, dc, :], xt[:, dc, gsl],
                         start=(dc == 0), stop=(dc == nDC - 1))
                p.act(ztile[:, gsl], ps[:], AF.Silu)
            p.dma(e.s_sz[ft * 128:(ft + 1) * 128, :], ztile[:, :])
        elif ft < 65:
            ct = ft - 32
            for g2 in range(TPC // 512):
                gsl = slice(g2 * 512, (g2 + 1) * 512)
                ps = ps2[g2]
                for dc in range(nDC):
                    p.mm(ps[:], wbuf[:, dc, :], xt[:, dc, gsl],
                         start=(dc == 0), stop=(dc == nDC - 1))
                p.v("tensor_copy", xbc_sb[:, gsl], ps[:])
            p.dma(e.s_xbraw[ct * 128:(ct + 1) * 128, :], xbc_sb[:, :])
            p.dma(e.s_xctr[ct * 128:(ct + 1) * 128, :], xbc_sb[:, TPC - 4:TPC])
        else:
            for g2 in range(TPC // 512):
                gsl = slice(g2 * 512, (g2 + 1) * 512)
                ps = ps2[g2]
                for dc in range(nDC):
                    p.mm(ps[:], wbuf[:, dc, :], xt[:, dc, gsl],
                         start=(dc == 0), stop=(dc == nDC - 1))
                p.v("tensor_copy", dtraw[0:32, gsl], ps[0:32, :])
            p.v("tensor_scalar", dtt[0:32, :], dtraw[0:32, :], dtb_sb[:], None, ALU.add)
            p.act(dta[0:32, :], dtt[0:32, :], AF.Abs)
            p.act(dta[0:32, :], dta[0:32, :], AF.Exp, scale=-1.0)
            p.v("tensor_scalar", dta[0:32, :], dta[0:32, :], 1.0, None, ALU.add)
            p.act(dta[0:32, :], dta[0:32, :], AF.Ln)
            p.v("tensor_scalar", dtt[0:32, :], dtt[0:32, :], 0.0, None, ALU.max)
            p.v("tensor_add", dtt[0:32, :], dtt[0:32, :], dta[0:32, :])
            p.dma(e.s_dt[:, :], dtt[0:32, :])


def emit_conv_f(p, e):
    nCT = CONV_DIM // 128
    ext = _carve(e.B, 0, (1028,))
    cvb = _carve(e.B, 2048, (TPC,))
    bnd = _carve(e.B, 3072, (4,))
    cv = _carve(e.C, 0, (TPC,))
    cw_sb = p.sbuf("cv_cw", [128, nCT, DCONV], F32)
    cb_sb = p.sbuf("cv_cb", [128, nCT], F32)
    p.dma(cw_sb[:], e.cw[:], eng="gpsimd")
    p.dma(cb_sb[:], e.cb[:], eng="gpsimd")
    for ct in range(nCT):
        sl = slice(ct * 128, (ct + 1) * 128)
        p.dma(bnd[:, :], e.s_xgat[sl, :])
        p.v("tensor_scalar", ext[:, 0:4], bnd[:, :], e.role[:], None, ALU.mult)
        p.dma(ext[:, 4:1028], e.s_xbraw[sl, :])
        p.v("tensor_scalar", cv[:, :], ext[:, 1:1 + TPC],
            cw_sb[:, ct, 0:1], cb_sb[:, ct:ct + 1], ALU.mult, ALU.add)
        for k2 in range(1, DCONV):
            p.v("scalar_tensor_tensor", cv[:, :], ext[:, 1 + k2:1 + k2 + TPC],
                cw_sb[:, ct, k2:k2 + 1], cv[:, :], ALU.mult, ALU.add)
        p.act(cvb[:, :], cv[:, :], AF.Silu)
        p.dma(e.s_xbcc[sl, :], cvb[:, :])




def emit_ssdA_f(p, e):
    nHD = 32
    nC2 = TPC // MCH
    xf = _carve(e.A, 0, (nHD, TPC))
    y_all = _carve(e.B, 0, (nHD, TPC))
    cum = _carve(e.C, 0, (TPC,))[0:nHD, :]
    negcum = _carve(e.C, 1024, (TPC,))[0:nHD, :]
    dec = _carve(e.C, 2048, (TPC,))[0:nHD, :]
    ecr = _carve(e.C, 3072, (TPC,))[0:nHD, :]
    dA = _carve(e.C, 4096, (TPC,))[0:nHD, :]
    ones_s = _carve(e.C, 5120, (TPC,))[0:nHD, :]
    dt_sb = _carve(e.C, 6144, (TPC,))[0:nHD, :]
    colbc = _carve(e.C, 7168, (128,))
    lrelu = _carve(e.C, 7296, (128,))
    colbc64 = _carve(e.C, 7424, (64,))
    carry = _carve(e.Dp, 0, (nHD, 128))[0:DS, :, :]
    cumT = _carve(e.Dp, 4096, (32,))
    negcumT = _carve(e.Dp, 4128, (32,))
    dtT = _carve(e.Dp, 4160, (32,))
    decT = _carve(e.Dp, 4192, (32,))
    ecrT_all = _carve(e.Dp, 4224, (nC2, 32))
    cdec = _carve(e.Dp, 4480, (nC2,))[0:nHD, :]
    zeros8 = _carve(e.Dp, 4488, (nC2,))[0:nHD, :]
    Gcp = _carve(e.Dp, 4496, (nC2,))[0:nHD, :]
    Gprev = _carve(e.Dp, 4504, (nC2,))[0:nHD, :]
    cdec_row = _carve(e.Dp, 4512, (32,))[0:1, :]
    cdec_bc = _carve(e.Dp, 4544, (32,))[0:DS, :]
    e.ssd_views = (xf, y_all, ecr, ecrT_all, Gprev, cum)
    lt = p.sbuf("sd_lt", [128, 128], BF16)
    phat = p.sbuf("sd_phat", [128, 128], BF16)
    xdt = p.sbuf("sd_xdt", [128, 128], BF16)
    sm = p.sbuf("sd_sm", [128, 128], BF16)
    btok = p.sbuf("sd_btok", [128, DS], BF16)
    bd = p.sbuf("sd_bd", [128, DS], BF16)
    chat = p.sbuf("sd_chat", [DS, 128], BF16)
    prevbf = p.sbuf("sd_prev", [DS, 128], BF16)
    B0 = p.sbuf("sd_B0", [DS, TPC], BF16)
    C0 = p.sbuf("sd_C0", [DS, TPC], BF16)
    e.ssd_named = (lt, phat, xdt, sm, btok, bd, chat, prevbf, B0, C0)
    alog_sb = p.sbuf("sd_alog", [32, 1], F32)
    acol = p.sbuf("sd_acol", [32, 1], F32)
    dvec_sb = p.sbuf("sd_dvec", [1, 32], F32)
    dcols = p.sbuf("sd_dcols", [128, 32], F32)
    ps_S = e.ps_big[:, 0:128]
    ps_tr = e.ps_big[:, 512:640]
    ps_st = e.ps_a[0:DS, 0:128]
    ps_y = e.ps_b[:, 0:128]
    ps_sm = e.ps_c[:, 0:32]
    ps_xt = e.ps_bf[:, 0:128]

    for h in range(nHD):
        p.dma(xf[:, h, :], e.s_xbcc[h * 128:(h + 1) * 128, :])
    p.dma(B0[:, :], e.s_xbcc[DI:DI + DS, :])
    p.dma(C0[:, :], e.s_xbcc[DI + DS:DI + 2 * DS, :])
    p.dma(dt_sb[:, :], e.s_dt[:, :])
    p.dma(alog_sb[:], e.alog[:])
    p.dma(dvec_sb[:], e.dvec[:])
    p.g("memset", ones_s[:, :], 1.0)
    p.g("memset", zeros8[:, :], 0.0)
    p.g("memset", carry[:, :, :], 0.0)
    p.act(acol[:], alog_sb[:], AF.Exp)
    p.s("mul", acol[:], acol[:], -1.0)
    p.v("tensor_scalar", dA[:, :], dt_sb[:, :], acol[:], None, ALU.mult)
    p.v("tensor_tensor_scan", cum[:, :], ones_s[:, :], dA[:, :], 0.0, ALU.mult, ALU.add)
    p.s("mul", negcum[:, :], cum[:, :], -1.0)
    p.mm(ps_sm[:, 0:32], e.onerow_f[:], dvec_sb[:])
    p.v("tensor_copy", dcols[:], ps_sm[:, 0:32])
    for c in range(nC2):
        csl = slice(c * MCH, (c + 1) * MCH)
        last = (c + 1) * MCH - 1
        if c == 0:
            p.act(ecr[:, csl], cum[:, csl], AF.Exp)
            p.act(cdec[:, 0:1], cum[:, last:last + 1], AF.Exp)
        else:
            prev = c * MCH - 1
            p.act(ecr[:, csl], cum[:, csl], AF.Exp, bias=negcum[:, prev:prev + 1])
            p.act(cdec[:, c:c + 1], cum[:, last:last + 1], AF.Exp,
                  bias=negcum[:, prev:prev + 1])
        p.act(dec[:, csl], negcum[:, csl], AF.Exp, bias=cum[:, last:last + 1])
    p.v("tensor_tensor_scan", Gcp[:, :], cdec[:, :], zeros8[:, :], 1.0, ALU.mult, ALU.add)
    p.g("memset", Gprev[:, 0:1], 1.0)
    p.v("tensor_copy", Gprev[:, 1:nC2], Gcp[:, 0:nC2 - 1])

    for c in range(nC2):
        csl = slice(c * MCH, (c + 1) * MCH)
        for (srcv, dstv) in ((cum, cumT), (dt_sb, dtT), (dec, decT)):
            p.trans(ps_sm[0:128, 0:32], srcv[:, csl], e.id32f[:])
            p.v("tensor_copy", dstv[:, :], ps_sm[0:128, 0:32])
        p.s("mul", negcumT[:, :], cumT[:, :], -1.0)
        p.trans(ps_sm[0:128, 0:32], ecr[:, csl], e.id32f[:])
        p.v("tensor_copy", ecrT_all[:, c, :], ps_sm[0:128, 0:32])
        p.trans(ps_xt[:, 0:DS], B0[:, csl], e.id64b[:])
        p.v("tensor_copy", btok[:], ps_xt[:, 0:DS])
        p.mm(ps_S[:, :], B0[:, csl], C0[:, csl])
        p.v("tensor_tensor", sm[:], ps_S[:, :], e.triub[:], ALU.mult)
        p.trans(ps_sm[0:1, 0:32], cdec[:, c:c + 1], e.id32f[:])
        p.v("tensor_copy", cdec_row[:, :], ps_sm[0:1, 0:32])
        p.mm(ps_sm[0:DS, 0:32], e.on64_f[:], cdec_row[:, :])
        p.v("tensor_copy", cdec_bc[:, :], ps_sm[0:DS, 0:32])
        for h in range(nHD):
            p.act(colbc[:, :], e.ones128t[:], AF.Identity, scale=negcumT[:, h:h + 1])
            p.trans(ps_tr[:, :], colbc[:, :], e.id128f[:])
            p.act(lrelu[:, :], ps_tr[:, :], AF.Relu, bias=cumT[:, h:h + 1])
            p.act(lt[:], lrelu[:, :], AF.Exp, scale=-1.0)
            p.g("tensor_tensor", phat[:], sm[:], lt[:], ALU.mult)
            p.trans(ps_xt[:, 0:128], xf[:, h, csl], e.id128b[:])
            p.v("tensor_scalar", xdt[:], ps_xt[:, 0:128], dtT[:, h:h + 1], None, ALU.mult)
            p.g("tensor_scalar", bd[:], btok[:], decT[:, h:h + 1], None, ALU.mult)
            p.mm(ps_st[:, :], bd[:], xdt[:])
            p.act(colbc64[:, :], e.ones128t[:, 0:64], AF.Identity,
                  scale=ecrT_all[:, c, h:h + 1])
            p.trans(ps_tr[0:DS, :], colbc64[:, :], e.id128f[:])
            p.v("tensor_tensor", chat[:], ps_tr[0:DS, 0:128], C0[:, csl], ALU.mult)
            p.v("tensor_copy", prevbf[:], carry[:, h, :])
            p.v("scalar_tensor_tensor", carry[:, h, :], carry[:, h, :],
                cdec_bc[:, h:h + 1], ps_st[:, :], ALU.mult, ALU.add)
            p.mm(ps_y[:, :], xdt[:], phat[:], start=True, stop=False)
            p.mm(ps_y[:, :], prevbf[:], chat[:], start=False, stop=True)
            p.v("scalar_tensor_tensor", y_all[:, h, csl], xf[:, h, csl],
                dcols[:, h:h + 1], ps_y[:, :], ALU.mult, ALU.add)
    cflat = _carve(e.Dp, 0, (nHD * 128,))[0:DS, :]
    p.dma(e.s_cctr.rearrange("(n h) pp -> n (h pp)", n=DS), cflat[:, :])


def emit_ssdB_f(p, e):
    nHD = 32
    nC2 = TPC // MCH
    (xf, y_all, ecr, ecrT_all, Gprev, cum) = e.ssd_views
    (lt, phat, xdt, sm, btok, bd, chat, prevbf, B0, C0) = e.ssd_named
    init = _carve(e.Dp, 0, (nHD, 128))[0:DS, :, :]
    initflat = _carve(e.Dp, 0, (nHD * 128,))[0:DS, :]
    g_row = _carve(e.Dp, 4512, (32,))[0:1, :]
    g_bc = _carve(e.Dp, 4544, (32,))[0:DS, :]
    colbc64 = _carve(e.C, 7424, (64,))
    initg = p.sbuf("sb_initg", [DS, 128], BF16)
    ps_tr = e.ps_big[:, 512:640]
    ps_y = e.ps_b[:, 0:128]
    ps_sm = e.ps_c[:, 0:32]
    p.dma(initflat[:, :], e.s_cgat[0:DS * 32, :].rearrange("(n h) pp -> n (h pp)", n=DS))
    p.v("tensor_scalar", initflat[:, :], initflat[:, :], e.role[0:DS, :], None, ALU.mult)
    for c in range(nC2):
        csl = slice(c * MCH, (c + 1) * MCH)
        p.trans(ps_sm[0:1, 0:32], Gprev[:, c:c + 1], e.id32f[:])
        p.v("tensor_copy", g_row[:, :], ps_sm[0:1, 0:32])
        p.mm(ps_sm[0:DS, 0:32], e.on64_f[:], g_row[:, :])
        p.v("tensor_copy", g_bc[:, :], ps_sm[0:DS, 0:32])
        for h in range(nHD):
            p.v("tensor_scalar", initg[:], init[:, h, :], g_bc[:, h:h + 1], None, ALU.mult)
            p.act(colbc64[:, :], e.ones128t[:, 0:64], AF.Identity,
                  scale=ecrT_all[:, c, h:h + 1])
            p.trans(ps_tr[0:DS, :], colbc64[:, :], e.id128f[:])
            p.v("tensor_tensor", chat[:], ps_tr[0:DS, 0:128], C0[:, csl], ALU.mult)
            p.mm(ps_y[:, :], initg[:], chat[:])
            p.v("tensor_tensor", y_all[:, h, csl], ps_y[:, :], y_all[:, h, csl], ALU.add)


def emit_mambaout_f(p, e):
    nCC = DI // 128
    y_all = _carve(e.B, 0, (nCC, TPC))
    wo8 = _carve(e.A, 0, (8, D))
    sz_t = _carve(e.A, 16384, (TPC,))
    sq = _carve(e.A, 17408, (TPC,))
    macc = _carve(e.C, 0, (4, D))
    rstd_bc = _carve(e.Dp, 0, (TPC,))
    nw = p.sbuf("mo2_nw", [128, nCC], F32)
    ssq = _carve(e.Dp, 1024, (TPC,))[0:1, :]
    rstdb = p.sbuf("mo2_rstdb", [1, TPC], BF16)
    ps_sa = e.ps_a[0:1, :]
    ps_sb2 = e.ps_b[0:1, :]
    ps_bc = e.ps_c
    ps_d = e.ps_big[:, 0:512]
    p.dma(nw[:], e.normw[:])
    for cc in range(nCC):
        p.dma(sz_t[:, :], e.s_sz[cc * 128:(cc + 1) * 128, :])
        p.g("tensor_tensor", y_all[:, cc, :], y_all[:, cc, :], sz_t[:, :], ALU.mult)
        p.act(sq[:, :], y_all[:, cc, :], AF.Square)
        for half, psx in ((0, ps_sa), (1, ps_sb2)):
            p.mm(psx[:, :], e.onecol_b[:], sq[:, half * 512:(half + 1) * 512],
                 start=(cc == 0), stop=(cc == nCC - 1))
    p.v("tensor_copy", ssq[:, 0:512], ps_sa[:, :])
    p.v("tensor_copy", ssq[:, 512:1024], ps_sb2[:, :])
    p.s("mul", ssq[:], ssq[:], 1.0 / DI)
    p.v("tensor_scalar", ssq[:], ssq[:], 1e-5, None, ALU.add)
    p.act(ssq[:], ssq[:], AF.Sqrt)
    p.v("reciprocal", ssq[:], ssq[:])
    p.v("tensor_copy", rstdb[:], ssq[:])
    for half in range(2):
        p.mm(ps_bc[:, :], e.onerow_b[:], rstdb[:, half * 512:(half + 1) * 512])
        p.v("tensor_copy", rstd_bc[:, half * 512:(half + 1) * 512], ps_bc[:, :])
    for cc in range(nCC):
        p.v("scalar_tensor_tensor", y_all[:, cc, :], y_all[:, cc, :], nw[:, cc:cc + 1],
            rstd_bc[:, :], ALU.mult, ALU.mult)
    for g2 in range(TPC // 512):
        for st in range(4):
            row0 = g2 * 512 + st * 128
            p.dma(macc[:, st, :], e.s_h[row0:row0 + 128, :])
        for ccg in range(4):
            p.dma(wo8[:, :, :], e.wout[ccg * 8:(ccg + 1) * 8].rearrange("c p d -> p c d"),
                  eng="gpsimd")
            for st in range(4):
                ssl = slice(g2 * 512 + st * 128, g2 * 512 + (st + 1) * 128)
                for dg in range(D // 512):
                    dsl = slice(dg * 512, (dg + 1) * 512)
                    for jc in range(8):
                        p.mm(ps_d[:, :], y_all[:, ccg * 8 + jc, ssl], wo8[:, jc, dsl],
                             start=(jc == 0), stop=(jc == 7))
                    p.v("tensor_tensor", macc[:, st, dsl], ps_d[:, :], macc[:, st, dsl], ALU.add)
        for st in range(4):
            row0 = g2 * 512 + st * 128
            p.dma(e.s_h[row0:row0 + 128, :], macc[:, st, :])


def emit_bitffn_f(p, e):
    nFT = HF // 128
    nDC = D // 128
    xt = _carve(e.A, 0, (nDC, TPC))
    qreg = [_carve(e.A, 16384, (nDC, 512)), _carve(e.A, 24576, (nDC, 512)),
            _carve(e.B, 23040, (nDC, 512))]
    q3reg = [_carve(e.A, 16384, (4, D)), _carve(e.A, 24576, (4, D))]
    hid = _carve(e.B, 0, (nFT, 512))
    sg1 = _carve(e.B, 22528, (512,))
    macc = _carve(e.C, 0, (4, D))
    ps12 = [e.ps_a, e.ps_big[:, 1024:1536], e.ps_b, e.ps_big[:, 1536:2048]]
    ps_d2 = [e.ps_c, e.ps_big[:, 0:512]]
    for dc in range(nDC):
        p.dma(xt[:, dc, :], e.s_xln[0:TPC, dc * 128:(dc + 1) * 128], transpose=True)
    q3blocks = [(b4 * 4, min(4, nFT - b4 * 4)) for b4 in range((nFT + 3) // 4)]
    for g2 in range(TPC // 512):
        gsl = slice(g2 * 512, (g2 + 1) * 512)
        for ftg in range(HF // 512):
            r1 = qreg[(2 * ftg) % 3]
            r2 = qreg[(2 * ftg + 1) % 3]
            for dc in range(nDC):
                p.dma(r1[:, dc, :], e.s_q1[dc * 128:(dc + 1) * 128,
                                           ftg * 512:(ftg + 1) * 512])
                p.dma(r2[:, dc, :], e.s_q2[dc * 128:(dc + 1) * 128,
                                           ftg * 512:(ftg + 1) * 512])
            for ftl in range(4):
                ft = ftg * 4 + ftl
                p1 = ps12[ftl % 4]
                p2 = ps12[(ftl + 2) % 4]
                for dc in range(nDC):
                    p.mm(p1[:], r1[:, dc, ftl * 128:(ftl + 1) * 128],
                         xt[:, dc, gsl], start=(dc == 0), stop=(dc == nDC - 1))
                p.act(sg1[:, :], p1[:], AF.Sigmoid)
                for dc in range(nDC):
                    p.mm(p2[:], r2[:, dc, ftl * 128:(ftl + 1) * 128],
                         xt[:, dc, gsl], start=(dc == 0), stop=(dc == nDC - 1))
                p.v("tensor_tensor", hid[:, ft, :], p2[:], sg1[:, :], ALU.mult)
        for st in range(4):
            row0 = g2 * 512 + st * 128
            p.dma(macc[:, st, :], e.s_h[row0:row0 + 128, :])
        for bi, (fc0, bw) in enumerate(q3blocks):
            q3b = q3reg[bi % 2]
            for j in range(bw):
                p.dma(q3b[:, j, :], e.s_q3[(fc0 + j) * 128:(fc0 + j + 1) * 128, :])
            for st in range(4):
                for dg in range(D // 512):
                    dsl = slice(dg * 512, (dg + 1) * 512)
                    pd = ps_d2[(st * 4 + dg) % 2]
                    for jc in range(bw):
                        p.mm(pd[:], hid[:, fc0 + jc, st * 128:(st + 1) * 128],
                             q3b[:, jc, dsl], start=(jc == 0), stop=(jc == bw - 1))
                    p.v("tensor_tensor", macc[:, st, dsl], pd[:], macc[:, st, dsl], ALU.add)
        for st in range(4):
            row0 = g2 * 512 + st * 128
            p.dma(e.out[row0:row0 + 128, :], macc[:, st, :])


def run_fused(ii, dbg=(), stop_after=None):
    nDC = D // 128
    x = ii["L0_attn_x"]
    wq_t, wk_t, wv_t, wo_t = prep_attn_weights(
        ii["L0_attn_W_Q"], ii["L0_attn_W_K"], ii["L0_attn_W_V"], ii["L0_attn_W_O"])
    wo_l = np.ascontiguousarray(wo_t.transpose(1, 0, 2))
    def lay_qkv(w, hs):
        return np.ascontiguousarray(
            w[hs].reshape(HPC, nDC, 128, EH).transpose(0, 2, 1, 3))
    moe_w = prep_moe_weights(
        ii["L0_ffn_gate_w"], ii["L0_ffn_expert_gate_w"], ii["L0_ffn_expert_up_w"],
        ii["L0_ffn_expert_down_w"], ii["L0_ffn_shared_gate_w"], ii["L0_ffn_shared_up_w"],
        ii["L0_ffn_shared_down_w"], ii["L0_ffn_shared_expert_gate_w"])
    gw_t, sge_t, eg_t, eu_t, ed_t, sg_t, su_t, sd_t = moe_w
    W_in = ii["L1_attn_W_in"]
    win_pad = np.zeros((66 * 128, D), dtype=np.float32)
    win_pad[:DIP] = W_in
    win_l = np.ascontiguousarray(win_pad.reshape(66, 128, nDC, 128).transpose(0, 3, 2, 1))
    cw_l = np.ascontiguousarray(ii["L1_attn_conv_w"].reshape(CONV_DIM // 128, 128, DCONV)
                                .transpose(1, 0, 2))
    cb_l = np.ascontiguousarray(ii["L1_attn_conv_b"].reshape(CONV_DIM // 128, 128).T)
    dtb_l = np.ascontiguousarray(ii["L1_attn_dt_bias"].reshape(32, 1))
    alog_l = np.ascontiguousarray(ii["L1_attn_A_log"].reshape(32, 1))
    dvec_l = np.ascontiguousarray(ii["L1_attn_D"].reshape(1, 32))
    wout_l = np.ascontiguousarray(ii["L1_attn_W_out"].T.reshape(DI // 128, 128, D))
    nw_l = np.ascontiguousarray(ii["L1_attn_norm_w"].reshape(DI // 128, 128).T)
    w1t = np.ascontiguousarray(ii["L1_ffn_w1"].T)
    w2t = np.ascontiguousarray(ii["L1_ffn_w2"].T)
    w3t = np.ascontiguousarray(ii["L1_ffn_w3"].T)

    nc = build_fused(dbg=dbg, stop_after=stop_after)
    in_maps = []
    for c in range(NCORES):
        b, role = c // 2, c % 2
        hs = slice(role * HPC, (role + 1) * HPC)
        rm = np.full((128, 1), float(role), dtype=np.float32)
        in_maps.append({
            "xb": np.ascontiguousarray(x[b]),
            "xrows": np.ascontiguousarray(x[b, role * TPC:(role + 1) * TPC]),
            "rolemask": rm,
            "wq": lay_qkv(wq_t, hs), "wk": lay_qkv(wk_t, hs), "wv": lay_qkv(wv_t, hs),
            "wo": wo_l,
            "gw": gw_t, "sge": sge_t, "eg": eg_t, "eu": eu_t, "ed": ed_t,
            "sg": sg_t, "su": su_t, "sd": sd_t,
            "win": win_l, "cw": cw_l, "cb": cb_l, "dtb": dtb_l,
            "alog": alog_l, "dvec": dvec_l, "wout": wout_l, "normw": nw_l,
            "w1t": w1t, "w2t": w2t, "w3t": w3t,
        })
    res = _run(nc, in_maps)
    out = np.zeros((B, S, D), dtype=np.float32)
    for c in range(NCORES):
        b, role = c // 2, c % 2
        out[b, role * TPC:(role + 1) * TPC] = res[c]["out"]
    return out, res


def kernel(**inputs):
    ii = {k: np.asarray(v) for k, v in inputs.items()}
    out, _ = run_fused(ii)
    return np.ascontiguousarray(out).astype(np.float32)


def kernel_unfused(**inputs):

    ii = {k: np.asarray(v) for k, v in inputs.items()}
    x = ii["L0_attn_x"]
    xflat = np.ascontiguousarray(x.reshape(T, D))

    wq_t, wk_t, wv_t, wo_t = prep_attn_weights(
        ii["L0_attn_W_Q"], ii["L0_attn_W_K"], ii["L0_attn_W_V"], ii["L0_attn_W_O"])
    xln = run_ln(xflat)
    q, k, v = run_qkv(xln.reshape(B, S, D), wq_t, wk_t, wv_t)
    z = run_attnz(q, k, v)
    h1 = run_attnout(z, wo_t, xflat)

    moe_w = prep_moe_weights(
        ii["L0_ffn_gate_w"], ii["L0_ffn_expert_gate_w"], ii["L0_ffn_expert_up_w"],
        ii["L0_ffn_expert_down_w"], ii["L0_ffn_shared_gate_w"], ii["L0_ffn_shared_up_w"],
        ii["L0_ffn_shared_down_w"], ii["L0_ffn_shared_expert_gate_w"])
    x1ln = run_ln(h1)
    h2 = run_moe(x1ln, h1, *moe_w)

    x2ln = run_ln(h2)
    sz, xbcc, dt = run_mambain(x2ln, ii["L1_attn_conv_w"], ii["L1_attn_conv_b"],
                               ii["L1_attn_dt_bias"], ii["L1_attn_W_in"])
    y = run_ssd(xbcc, dt, ii["L1_attn_A_log"], ii["L1_attn_D"])
    h3 = run_mambaout(y, sz, ii["L1_attn_W_out"], ii["L1_attn_norm_w"], h2)

    q1t, q2t, q3t = run_quantize(ii["L1_ffn_w1"], ii["L1_ffn_w2"], ii["L1_ffn_w3"])
    x3ln = run_ln(h3)
    out = run_bitffn(x3ln, h3, q1t, q2t, q3t)
    return np.ascontiguousarray(out.reshape(B, S, D)).astype(np.float32)
